# version 17
# baseline (speedup 1.0000x reference)
import numpy as np
import concourse.bass as bass
import concourse.mybir as mybir
from concourse.bass_utils import run_bass_kernel_spmd
from contextlib import ExitStack

F32 = mybir.dt.float32
U8 = mybir.dt.uint8
AF = mybir.ActivationFunctionType
ALU = mybir.AluOpType

H = 2048
KC = 16
W = 512
NG = 4
SW = 515
DTR = 64
NST = 16
IFF = 1408
NIC = 11
NE = 8
EPS = 1e-5
INVH = 1.0 / 2048.0


class Buf:
    __slots__ = ("name", "w", "r")

    def __init__(self, name):
        self.name = name
        self.w = None
        self.r = []


class Plan:
    ENGINES = ("tensor", "vector", "scalar", "sync", "gpsimd")

    def __init__(self):
        self.ops = {e: [] for e in self.ENGINES}
        self.ctr = {}
        self.seen = {e: {} for e in self.ENGINES}
        self.sems = {}
        self._bufs = {}
        self._pending = {e: [] for e in self.ENGINES}

    def buf(self, name):
        b = self._bufs.get(name)
        if b is None:
            b = self._bufs[name] = Buf(name)
        return b

    def _waits(self, eng, deps):
        need = {}
        for d in deps:
            if d is None:
                continue
            s, c = d
            if eng == "tensor" and s == "s_tensor":
                continue
            if need.get(s, 0) < c:
                need[s] = c
        out = []
        seen = self.seen[eng]
        for s, c in need.items():
            if seen.get(s, 0) < c:
                seen[s] = c
                out.append((s, c))
        return out

    def op(self, eng, fn, ins=(), outs=(), sem=None, amt=1, inc=True):
        deps = []
        for b in ins:
            deps.append(b.w)
        for b in outs:
            deps.append(b.w)
            deps.extend(b.r)
        waits = self._waits(eng, deps)
        plan = self
        if not inc:
            def closure0(e, waits=waits, fn=fn):
                for (s, c) in waits:
                    e.wait_ge(plan.sems[s], c)
                fn(e)
            self.ops[eng].append(closure0)
            self._pending[eng].append((list(ins), list(outs)))
            return None
        sem = sem or ("s_" + eng)
        self.ctr[sem] = self.ctr.get(sem, 0) + amt
        cnt = self.ctr[sem]

        def closure(e, waits=waits, fn=fn, sem=sem, amt=amt):
            for (s, c) in waits:
                e.wait_ge(plan.sems[s], c)
            fn(e).then_inc(plan.sems[sem], amt)

        self.ops[eng].append(closure)
        tok = (sem, cnt)
        for pins, pouts in self._pending[eng]:
            for b in pouts:
                b.w, b.r = tok, []
            for b in pins:
                b.r.append(tok)
        self._pending[eng] = []
        for b in outs:
            b.w, b.r = tok, []
        for b in ins:
            b.r.append(tok)
        return tok


def build():
    nc = bass.Bass("TRN2", target_bir_lowering=False, debug=False)
    P = Plan()

    di = lambda n, s: nc.dram_tensor(n, s, F32, kind="ExternalInput").ap()
    x_fm = di("x_fm", [H, 3 + 2048])
    win_xc = di("win_xc", [H, H])
    win_z = di("win_z", [H, H])
    wconv = di("wconv", [4, H, H])
    wxp_s = di("wxp_s", [128, KC * 96])
    wdt_s = di("wdt_s", [DTR, H])
    wout = di("wout", [H, H])
    wr_s = di("wr_s", [128, KC * 8])
    wg_s = di("wg_s", [NE, NIC, 128, KC * 128])
    wu_s = di("wu_s", [NE, NIC, 128, KC * 128])
    wd_s = di("wd_s", [NE, KC, 128, NIC * 128])
    vec4 = di("vec4", [128, 4 * KC])
    alog_s = di("alog_s", [128, KC * NST])
    ones_col_d = di("ones_col", [128, 1])
    ones_row_d = di("ones_row", [1, 128])
    ident_d = di("ident", [128, 128])
    msend_d = di("msend", [128, 1])
    mrecv_d = di("mrecv", [128, 1])
    out_fm = nc.dram_tensor("out_fm", [H, 2048], F32, kind="ExternalOutput").ap()

    ds = lambda n, s: nc.dram_tensor(n, s, F32).ap()
    xn_sp = ds("xn_sp", [H, 2048])
    delta_sp = ds("delta_sp", [H, 2048])
    xcv_sp = ds("xcv_sp", [H, 2048])
    bc_sp = ds("bc_sp", [32, 2048])
    cc_in = nc.dram_tensor("cc_in", [128, KC * NST], F32)
    cc_out = nc.dram_tensor("cc_out", [128, KC * NST], F32)
    BF16 = mybir.dt.bfloat16
    wg_b = nc.dram_tensor("wg_b", [NE, NIC, 128, 2 * KC * 128], BF16).ap()
    wu_b = nc.dram_tensor("wu_b", [NE, NIC, 128, 2 * KC * 128], BF16).ap()
    wd_b = nc.dram_tensor("wd_b", [NE, KC, 128, 2 * NIC * 128], BF16).ap()
    y_sp = nc.dram_tensor("y_sp", [H, 2048], F32).ap()
    ha_sp = nc.dram_tensor("ha_sp", [NE, NIC, 128, 2 * W], BF16).ap()

    with ExitStack() as stack:
        e_ = stack.enter_context
        sb = lambda n, s, dt=F32: e_(nc.sbuf_tensor(n, s, dt))
        S1 = sb("S1", [128, KC * SW])
        S2 = sb("S2", [128, KC * SW])
        S3 = sb("S3", [128, KC * SW])
        WS = [sb(f"ws{i}", [128, 2048]) for i in range(4)]
        vec_sb = sb("vec_sb", [128, 4 * KC])
        alog_sb = sb("alog_sb", [128, KC * NST])
        a_sb = sb("a_sb", [128, KC * NST])
        wxp_sb = sb("wxp_sb", [128, KC * 96])
        wr_sb = sb("wr_sb", [128, KC * 8])
        ones_col = sb("ones_col_s", [128, 1])
        ones_row = sb("ones_row_s", [1, 128])
        ident = sb("ident_s", [128, 128])
        msend = sb("msend_s", [128, 1])
        mrecv = sb("mrecv_s", [128, 1])
        state1 = sb("state1", [128, KC * NST])
        state2 = sb("state2", [128, KC * NST])
        r2all = sb("r2all", [1, 2048])
        xc_carry = sb("xc_carry", [128, KC * 3])
        xh_sb = sb("xh_sb", [128, KC * 3])
        xnh_sb = sb("xnh_sb", [128, KC * 3])
        h3_sb = sb("h3_sb", [128, KC * 3])
        tail_sb = sb("tail_sb", [128, 12])
        t3a = sb("t3a", [128, 3])
        eps_sb = sb("eps_sb", [128, 1])
        RA = sb("RA", [1, SW]); RB = sb("RB", [1, SW]); RC = sb("RC", [1, SW])
        RD = sb("RD", [1, SW]); RE = sb("RE", [1, SW]); RF = sb("RF", [1, SW])
        G = [sb(f"g{i}", [128, W]) for i in range(14)]
        tb0, tb1 = G[0], G[1]
        hb0, hb1 = G[2], G[3]
        ub = G[4]
        dA0, dA1 = G[5], G[6]
        bx0, bx1 = G[7], G[8]
        hs0, hs1 = G[9], G[10]
        e1b, spb, rbb = G[11], G[12], G[13]
        dch0, dch1 = G[0], G[1]
        xvh0, xvh1 = G[2], G[3]
        ym0, ym1 = G[11], G[12]
        zsb = G[0]
        to0, to1 = G[5], G[6]
        xg0, xg1 = G[7], G[8]
        gsb = G[0]
        cbb = G[1]
        hu_t = G[2]
        mskb = sb("mskb", [128, W], U8)
        r2b = sb("r2b", [128, W])
        dl_sb = sb("dl_sb", [DTR, W])
        bcq_sb = sb("bcq_sb", [32, W])
        bcr_sb = sb("bcr_sb", [32, W])
        lgr_sb = sb("lgr_sb", [8, W])
        crow_sb = sb("crow_sb", [8, W])
        lgt_sb = sb("lgt_sb", [128, 32])
        mx_sb = sb("mx_sb", [128, 32])
        ex_sb = sb("ex_sb", [128, 32])
        exs_sb = sb("exs_sb", [128, 32])
        comb_sb = sb("comb_sb", [128, 32])
        msk8_sb = sb("msk8_sb", [128, 32], U8)
        zero8 = sb("zero8", [128, 8])
        ngm_sb = sb("ngm_sb", [128, 4])
        den_sb = sb("den_sb", [128, 4])
        den1_sb = sb("den1_sb", [128, 4])
        rdn_sb = sb("rdn_sb", [128, 4])
        ccb_sb = sb("ccb_sb", [128, KC * NST])

        ps = [e_(nc.psum_tensor(f"ps{i}", [128, W], F32)) for i in range(8)]

        sems = {}
        slot_names = (["setup", "bcr", "bcq", "ccb"]
                      + [f"s1_{k}" for k in range(KC)]
                      + [f"s2_{k}" for k in range(KC)]
                      + [f"w2_{k}" for k in range(KC)]
                      + [f"w3_{k}" for k in range(KC)]
                      + [f"ws_{i}" for i in range(4)]
                      + [f"s3_{k}" for k in range(NIC)]
                      + [f"g{i}" for i in (0, 1, 2, 3, 4, 7, 8, 11, 12, 13)])
        for s in (["s_tensor", "s_vector", "s_scalar", "s_gpsimd", "s_cc"]
                  + ["qd_" + n for n in slot_names]):
            sems[s] = e_(nc.semaphore(s))
        P.sems = sems
        block = e_(nc.Block())

        B = P.buf
        PS = [B(f"ps{i}") for i in range(8)]
        ST1 = [B(f"st1:{i}") for i in range(KC * NST)]
        ST2 = [B(f"st2:{i}") for i in range(KC * NST)]

        def sc(k, c0, c1):
            return slice(k * SW + c0, k * SW + c1)

        S1b = [B(f"S1:{k}") for k in range(KC)]
        S2b = [B(f"S2:{k}") for k in range(KC)]
        S3b = [B(f"S3:{k}") for k in range(KC)]
        WSb = [B(f"ws:{i}") for i in range(4)]

        def dma(q, dst_ap, src_ap, ins, outs, slot):
            P.op(q, lambda e, d=dst_ap, s=src_ap: e.dma_start(out=d, in_=s),
                 ins=ins, outs=outs, sem="qd_" + slot, amt=16)

        def vop(fn, ins, outs, eng="vector"):
            P.op(eng, fn, ins=ins, outs=outs)

        wrot = [0]

        def next_ws():
            wrot[0] = (wrot[0] + 1) % 4
            return wrot[0]

        split_jobs = []
        for ex in range(NE):
            for ic in range(NIC):
                split_jobs.append((wg_s[ex, ic], wg_b[ex, ic], KC * 128))
                split_jobs.append((wu_s[ex, ic], wu_b[ex, ic], KC * 128))
        for ex in range(NE):
            for m in range(KC):
                split_jobs.append((wd_s[ex, m], wd_b[ex, m], NIC * 128))
        split_iter = iter(split_jobs)

        def wbkey(ap):
            return "wb:" + str(ap.name) + ":" + str(ap.offset)

        def emit_splits(n):
            for _ in range(n):
                job = next(split_iter, None)
                if job is None:
                    return
                srcw, dstw, wdt = job
                a = next_ws()
                b = next_ws()
                dma("sync", WS[a][:, 0:wdt], srcw, [], [WSb[a]], f"ws_{a}")
                bv = WS[b].ap().bitcast(BF16)
                P.op("gpsimd", lambda e, a=a, bv=bv, wdt=wdt: e.tensor_copy(
                    bv[:, 0:wdt], WS[a][:, 0:wdt]),
                    ins=[WSb[a]], outs=[WSb[b]])
                P.op("gpsimd", lambda e, a=a, bv=bv, wdt=wdt: e.tensor_tensor(
                    bv[:, wdt:2 * wdt], WS[a][:, 0:wdt], bv[:, 0:wdt],
                    ALU.subtract),
                    ins=[WSb[a], WSb[b]], outs=[WSb[b]])
                dma("sync", dstw, bv[:, 0:2 * wdt], [WSb[b]], [B(wbkey(dstw))],
                    f"ws_{b}")

        def warm(dep):
            P.op("tensor", lambda e: e.matmul(
                ps[7][0:1, 0:64], ones_col[:], a_sb[:, 0:64], start=True, stop=True),
                ins=[dep, B("onec"), B("A")], outs=[PS[7]])

        for src, dsttile, name in [
            (vec4, vec_sb, "vec"), (alog_s, alog_sb, "alog"), (wxp_s, wxp_sb, "wxp"),
            (wr_s, wr_sb, "wr"), (ones_col_d, ones_col, "onec"),
            (ones_row_d, ones_row, "oner"), (ident_d, ident, "ident"),
            (msend_d, msend, "msend"), (mrecv_d, mrecv, "mrecv"),
        ]:
            dma("sync", dsttile[:], src, [B("setup_chain")], [B(name), B("setup_chain")],
                "setup")
        nw_c = lambda k: vec_sb[:, k:k + 1]
        cb_c = lambda k: vec_sb[:, KC + k:KC + k + 1]
        db_c = lambda k: vec_sb[:, 2 * KC + k:2 * KC + k + 1]
        dd_c = lambda k: vec_sb[:, 3 * KC + k:3 * KC + k + 1]
        P.op("scalar", lambda e: e.activation(out=a_sb[:], in_=alog_sb[:], func=AF.Exp),
             ins=[B("alog")], outs=[B("A")])
        vop(lambda e: e.tensor_scalar_mul(a_sb[:], a_sb[:], -1.0), [B("A")], [B("A")])
        vop(lambda e: e.memset(state1[:], 0.0), [], ST1)
        vop(lambda e: e.memset(eps_sb[:], EPS), [], [B("eps")])
        vop(lambda e: e.memset(zero8[:], 0.0), [], [B("zero8")])

        for g in range(NG):
            c0 = g * W
            for k in range(KC):
                dma("scalar", S1[:, sc(k, 0, W)],
                    x_fm[k * 128:(k + 1) * 128, 3 + c0:3 + c0 + W], [], [S1b[k]],
                    f"s1_{k}")
            if g == 0:
                for k in range(KC):
                    dma("scalar", xh_sb[:, k * 3:(k + 1) * 3],
                        x_fm[k * 128:(k + 1) * 128, 0:3],
                        [B("setup_chain")], [B(f"xh{k}"), B("setup_chain")], "setup")
            for k in range(KC):
                sq, sqb = (tb0, B("g0")) if k % 2 == 0 else (tb1, B("g1"))
                P.op("scalar", lambda e, k=k, sq=sq: e.activation(
                    out=sq[:], in_=S1[:, sc(k, 0, W)], func=AF.Square),
                    ins=[S1b[k]], outs=[sqb])
                P.op("tensor", lambda e, k=k: e.matmul(
                    ps[0][0:1, :], ones_col[:], S1[:, sc(k, 0, W)],
                    start=(k == 0), stop=(k == KC - 1)),
                    ins=[S1b[k], B("onec")], outs=[PS[0]])
                P.op("tensor", lambda e, k=k, sq=sq: e.matmul(
                    ps[1][0:1, :], ones_col[:], sq[:],
                    start=(k == 0), stop=(k == KC - 1)),
                    ins=[sqb, B("onec")], outs=[PS[1]])
            if g == 0:
                for k in range(KC):
                    P.op("scalar", lambda e, k=k: e.activation(
                        out=t3a[:], in_=xh_sb[:, k * 3:(k + 1) * 3], func=AF.Square),
                        ins=[B(f"xh{k}")], outs=[B("t3a")])
                    P.op("tensor", lambda e, k=k: e.matmul(
                        ps[2][0:1, 0:3], ones_col[:], xh_sb[:, k * 3:(k + 1) * 3],
                        start=(k == 0), stop=(k == KC - 1)),
                        ins=[B(f"xh{k}"), B("onec")], outs=[PS[2]])
                    P.op("tensor", lambda e, k=k: e.matmul(
                        ps[3][0:1, 0:3], ones_col[:], t3a[:],
                        start=(k == 0), stop=(k == KC - 1)),
                        ins=[B("t3a"), B("onec")], outs=[PS[3]])
            EW = SW if g == 0 else W
            vop(lambda e: e.tensor_copy(RA[0:1, 0:W], ps[0][0:1, :]), [PS[0]], [B("RA")])
            vop(lambda e: e.tensor_copy(RB[0:1, 0:W], ps[1][0:1, :]), [PS[1]], [B("RB")])
            if g == 0:
                vop(lambda e: e.tensor_copy(RA[0:1, W:SW], ps[2][0:1, 0:3]), [PS[2]], [B("RA")])
                vop(lambda e: e.tensor_copy(RB[0:1, W:SW], ps[3][0:1, 0:3]), [PS[3]], [B("RB")])
            vop(lambda e, EW=EW: e.tensor_scalar_mul(RA[0:1, 0:EW], RA[0:1, 0:EW], INVH),
                [B("RA")], [B("RA")])
            vop(lambda e, EW=EW: e.tensor_scalar_mul(RB[0:1, 0:EW], RB[0:1, 0:EW], INVH),
                [B("RB")], [B("RB")])
            P.op("scalar", lambda e, EW=EW: e.activation(
                out=RD[0:1, 0:EW], in_=RA[0:1, 0:EW], func=AF.Square),
                ins=[B("RA")], outs=[B("RD")])
            vop(lambda e, EW=EW: e.tensor_sub(RC[0:1, 0:EW], RB[0:1, 0:EW], RD[0:1, 0:EW]),
                [B("RB"), B("RD")], [B("RC")])
            P.op("scalar", lambda e, EW=EW: e.activation(
                out=RD[0:1, 0:EW], in_=RC[0:1, 0:EW], func=AF.Sqrt, bias=eps_sb[0:1, 0:1]),
                ins=[B("RC"), B("eps")], outs=[B("RD")])
            vop(lambda e, EW=EW: e.reciprocal(RE[0:1, 0:EW], RD[0:1, 0:EW]),
                [B("RD")], [B("RE")])
            vop(lambda e, EW=EW: e.tensor_mul(RF[0:1, 0:EW], RA[0:1, 0:EW], RE[0:1, 0:EW]),
                [B("RA"), B("RE")], [B("RF")])
            vop(lambda e, EW=EW: e.tensor_scalar_mul(RF[0:1, 0:EW], RF[0:1, 0:EW], -1.0),
                [B("RF")], [B("RF")])
            vop(lambda e, EW=EW: e.tensor_mul(RB[0:1, 0:EW], RC[0:1, 0:EW], RE[0:1, 0:EW]),
                [B("RC"), B("RE")], [B("RB")])
            vop(lambda e, EW=EW: e.tensor_mul(RB[0:1, 0:EW], RB[0:1, 0:EW], RE[0:1, 0:EW]),
                [B("RB")], [B("RB")])
            P.op("scalar", lambda e, EW=EW: e.activation(
                out=RD[0:1, 0:EW], in_=RB[0:1, 0:EW], func=AF.Sqrt, bias=eps_sb[0:1, 0:1]),
                ins=[B("RB"), B("eps")], outs=[B("RD")])
            vop(lambda e, EW=EW: e.reciprocal(RD[0:1, 0:EW], RD[0:1, 0:EW]),
                [B("RD")], [B("RD")])
            vop(lambda e, c0=c0: e.tensor_copy(r2all[0:1, c0:c0 + W], RD[0:1, 0:W]),
                [B("RD")], [B("r2all")])
            P.op("tensor", lambda e: e.matmul(ps[4][:], ones_row[:], RE[0:1, 0:W],
                                              start=True, stop=True),
                 ins=[B("RE"), B("oner")], outs=[PS[4]])
            P.op("tensor", lambda e: e.matmul(ps[5][:], ones_row[:], RF[0:1, 0:W],
                                              start=True, stop=True),
                 ins=[B("RF"), B("oner")], outs=[PS[5]])
            P.op("tensor", lambda e: e.matmul(ps[6][:], ones_row[:], RD[0:1, 0:W],
                                              start=True, stop=True),
                 ins=[B("RD"), B("oner")], outs=[PS[6]])
            vop(lambda e: e.tensor_copy(r2b[:], ps[6][:]), [PS[6]], [B("r2b")])
            if g == 0:
                for i, (src, srcb) in enumerate([(RE, "RE"), (RF, "RF"), (RD, "RD")]):
                    P.op("tensor", lambda e, src=src: e.matmul(
                        ps[7][:, 0:3], ones_row[:], src[0:1, W:SW], start=True, stop=True),
                        ins=[B(srcb), B("oner")], outs=[PS[7]])
                    vop(lambda e, i=i: e.tensor_copy(tail_sb[:, 3 * i:3 * i + 3],
                                                     ps[7][:, 0:3]),
                        [PS[7]], [B("tail")])
            for k in range(KC):
                tbt, tbb = (tb0, B("g0")) if k % 2 == 0 else (tb1, B("g1"))
                vop(lambda e, k=k, tbt=tbt: e.tensor_mul(tbt[:], S1[:, sc(k, 0, W)], ps[4][:]),
                    [S1b[k], PS[4]], [tbb])
                vop(lambda e, k=k, tbt=tbt: e.tensor_add(S2[:, sc(k, 0, W)], tbt[:], ps[5][:]),
                    [tbb, PS[5]], [S2b[k]])
                dma("gpsimd", xn_sp[k * 128:(k + 1) * 128, c0:c0 + W], S2[:, sc(k, 0, W)],
                    [S2b[k]], [B(f"xnsp:{g}:{k}")], f"w2_{k}")
            if g == 0:
                for k in range(KC):
                    vop(lambda e, k=k: e.tensor_mul(t3a[:], xh_sb[:, k * 3:(k + 1) * 3],
                                                    tail_sb[:, 0:3]),
                        [B(f"xh{k}"), B("tail")], [B("t3a")])
                    vop(lambda e, k=k: e.tensor_add(xnh_sb[:, k * 3:(k + 1) * 3], t3a[:],
                                                    tail_sb[:, 3:6]),
                        [B("t3a"), B("tail")], [B(f"xnh{k}")])
                    P.op("vector", lambda e, k=k: e.scalar_tensor_tensor(
                        h3_sb[:, k * 3:(k + 1) * 3], xnh_sb[:, k * 3:(k + 1) * 3],
                        nw_c(k), tail_sb[:, 6:9], ALU.mult, ALU.mult),
                        ins=[B(f"xnh{k}"), B("vec"), B("tail")], outs=[B(f"h3{k}")])
                for mr in range(2):
                    for k in range(KC):
                        wi = next_ws()
                        dma("sync", WS[wi][:, 0:1024],
                            win_xc[k * 128:(k + 1) * 128, mr * 1024:(mr + 1) * 1024],
                            [], [WSb[wi]], f"ws_{wi}")
                        for m8 in range(8):
                            m = mr * 8 + m8
                            first = (mr == 0 and k == 0 and m8 == 0)
                            last = (mr == 1 and k == KC - 1 and m8 == 7)
                            P.op("tensor", lambda e, k=k, m8=m8, m=m, wi=wi, first=first,
                                 last=last: e.matmul(
                                     ps[0][:, m * 3:(m + 1) * 3],
                                     WS[wi][:, m8 * 128:(m8 + 1) * 128],
                                     h3_sb[:, k * 3:(k + 1) * 3],
                                     start=first, stop=last),
                                 ins=[WSb[wi], B(f"h3{k}")], outs=[PS[0]])
                for m in range(KC):
                    vop(lambda e, m=m: e.tensor_copy(xc_carry[:, m * 3:(m + 1) * 3],
                                                     ps[0][:, m * 3:(m + 1) * 3]),
                        [PS[0]], [B("xcc")])
            for mr in range(2):
                for k in range(KC):
                    emit_splits(1)
                    wi = next_ws()
                    dma("sync", WS[wi][:, 0:1024],
                        win_xc[k * 128:(k + 1) * 128, mr * 1024:(mr + 1) * 1024],
                        [], [WSb[wi]], f"ws_{wi}")
                    hbt, hbb = (hb0, B("g2")) if k % 2 == 0 else (hb1, B("g3"))
                    P.op("vector", lambda e, k=k, hbt=hbt: e.scalar_tensor_tensor(
                        hbt[:], S2[:, sc(k, 0, W)], nw_c(k), r2b[:], ALU.mult, ALU.mult),
                        ins=[S2b[k], B("vec"), B("r2b")], outs=[hbb])
                    for m8 in range(8):
                        P.op("tensor", lambda e, m8=m8, wi=wi, hbt=hbt, k=k: e.matmul(
                            ps[m8][:], WS[wi][:, m8 * 128:(m8 + 1) * 128], hbt[:],
                            start=(k == 0), stop=(k == KC - 1)),
                            ins=[WSb[wi], hbb], outs=[PS[m8]])
                for m8 in range(8):
                    m = mr * 8 + m8
                    vop(lambda e, m=m: e.tensor_copy(S1[:, sc(m, 0, 3)],
                                                     xc_carry[:, m * 3:(m + 1) * 3]),
                        [B("xcc")], [S1b[m]])
                    P.op("scalar", lambda e, m=m, m8=m8: e.activation(
                        out=S1[:, sc(m, 3, SW)], in_=ps[m8][:], func=AF.Copy),
                        ins=[PS[m8]], outs=[S1b[m]])
            for m in range(KC):
                vop(lambda e, m=m: e.tensor_copy(xc_carry[:, m * 3:(m + 1) * 3],
                                                 S1[:, sc(m, W, SW)]),
                    [S1b[m]], [B("xcc")])
            for mr in range(2):
                for tap in range(4):
                    for k in range(KC):
                        if k % 2 == 0:
                            emit_splits(1)
                        wi = next_ws()
                        dma("sync", WS[wi][:, 0:1024],
                            wconv[tap, k * 128:(k + 1) * 128, mr * 1024:(mr + 1) * 1024],
                            [], [WSb[wi]], f"ws_{wi}")
                        for m8 in range(8):
                            first = (tap == 0 and k == 0)
                            lastk = (tap == 3 and k == KC - 1)
                            P.op("tensor", lambda e, k=k, m8=m8, tap=tap, wi=wi, first=first,
                                 lastk=lastk: e.matmul(
                                     ps[m8][:], WS[wi][:, m8 * 128:(m8 + 1) * 128],
                                     S1[:, sc(k, tap, tap + W)],
                                     start=first, stop=lastk),
                                 ins=[WSb[wi], S1b[k]], outs=[PS[m8]])
                for m8 in range(8):
                    m = mr * 8 + m8
                    P.op("scalar", lambda e, m=m, m8=m8: e.activation(
                        out=S2[:, sc(m, 0, W)], in_=ps[m8][:], func=AF.Silu, bias=cb_c(m)),
                        ins=[PS[m8], B("vec")], outs=[S2b[m]])
                    dma("gpsimd", xcv_sp[m * 128:(m + 1) * 128, c0:c0 + W],
                        S2[:, sc(m, 0, W)], [S2b[m]], [B(f"xcvsp:{g}:{m}")], f"w2_{m}")
            for k in range(KC):
                P.op("tensor", lambda e, k=k: e.matmul(
                    ps[0][0:DTR, :], wxp_sb[:, k * 96:k * 96 + DTR], S2[:, sc(k, 0, W)],
                    start=(k == 0), stop=(k == KC - 1)),
                    ins=[S2b[k], B("wxp")], outs=[PS[0]])
                P.op("tensor", lambda e, k=k: e.matmul(
                    ps[1][0:32, :], wxp_sb[:, k * 96 + DTR:(k + 1) * 96], S2[:, sc(k, 0, W)],
                    start=(k == 0), stop=(k == KC - 1)),
                    ins=[S2b[k], B("wxp")], outs=[PS[1]])
            vop(lambda e: e.tensor_copy(dl_sb[:], ps[0][0:DTR, :]), [PS[0]], [B("dbc")])
            vop(lambda e: e.tensor_copy(bcq_sb[:], ps[1][0:32, :]), [PS[1]], [B("bcq")])
            dma("gpsimd", bc_sp[:, c0:c0 + W], bcq_sb[:], [B("bcq")], [B(f"bcsp:{g}")], "bcq")
            wdti = next_ws()
            dma("sync", WS[wdti][0:DTR, :], wdt_s, [], [WSb[wdti]], f"ws_{wdti}")
            for m in range(KC):
                pi = 2 + (m % 2)
                P.op("tensor", lambda e, m=m, pi=pi, wdti=wdti: e.matmul(
                    ps[pi][:], WS[wdti][0:DTR, m * 128:(m + 1) * 128], dl_sb[0:DTR, :],
                    start=True, stop=True),
                    ins=[B("dbc"), WSb[wdti]], outs=[PS[pi]])
                P.op("scalar", lambda e, m=m, pi=pi: e.activation(
                    out=e1b[:], in_=ps[pi][:], func=AF.Exp, bias=db_c(m)),
                    ins=[PS[pi], B("vec")], outs=[B("g11")])
                P.op("vector", lambda e, m=m, pi=pi: e.tensor_scalar_add(
                    rbb[:], ps[pi][:], db_c(m)),
                    ins=[PS[pi], B("vec")], outs=[B("g13")])
                P.op("scalar", lambda e: e.activation(out=spb[:], in_=e1b[:], func=AF.Ln,
                                                      bias=1.0),
                     ins=[B("g11")], outs=[B("g12")])
                vop(lambda e: e.tensor_single_scalar(mskb[:], rbb[:], 20.0, ALU.is_le),
                    [B("g13")], [B("mskb")])
                vop(lambda e, m=m: e.tensor_copy(S3[:, sc(m, 0, W)], rbb[:]),
                    [B("g13")], [S3b[m]])
                vop(lambda e, m=m: e.copy_predicated(S3[:, sc(m, 0, W)], mskb[:], spb[:]),
                    [B("mskb"), B("g12")], [S3b[m]])
                dma("gpsimd", delta_sp[m * 128:(m + 1) * 128, c0:c0 + W], S3[:, sc(m, 0, W)],
                    [S3b[m]], [B(f"dlsp:{g}:{m}")], f"w3_{m}")
            for n in range(NST):
                P.op("tensor", lambda e, n=n: e.matmul(
                    ps[7][:], ident[0:32, n:n + 1].broadcast_to([32, 128]), bcq_sb[:],
                    start=True, stop=True),
                    ins=[B("bcq"), B("ident")], outs=[PS[7]])
                vop(lambda e, n=n: e.tensor_copy(S1[:, sc(n, 0, W)], ps[7][:]),
                    [PS[7]], [S1b[n]])
            for hc in range(KC):
                vop(lambda e, hc=hc: e.tensor_mul(ub[:], S3[:, sc(hc, 0, W)],
                                                  S2[:, sc(hc, 0, W)]),
                    [S3b[hc], S2b[hc]], [B("g4")])
                for n in range(NST):
                    idx = hc * NST + n
                    dAt, dAb = (dA0, B("g5")) if n % 2 == 0 else (dA1, B("g6"))
                    bxt, bxb = (bx0, B("g7")) if n % 2 == 0 else (bx1, B("g8"))
                    hst, hsb = (hs0, B("g9")) if n % 2 == 0 else (hs1, B("g10"))
                    P.op("scalar", lambda e, hc=hc, dAt=dAt, idx=idx: e.activation(
                        out=dAt[:], in_=S3[:, sc(hc, 0, W)], func=AF.Exp,
                        scale=a_sb[:, idx:idx + 1]),
                        ins=[S3b[hc], B("A")], outs=[dAb])
                    vop(lambda e, n=n, bxt=bxt: e.tensor_mul(
                        bxt[:], ub[:], S1[:, sc(n, 0, W)]),
                        [B("g4"), S1b[n]], [bxb])
                    P.op("vector", lambda e, dAt=dAt, bxt=bxt, hst=hst, idx=idx:
                         e.tensor_tensor_scan(hst[:], dAt[:], bxt[:],
                                              state1[:, idx:idx + 1], ALU.mult, ALU.add),
                         ins=[dAb, bxb, ST1[idx]], outs=[hsb])
                    vop(lambda e, hst=hst, idx=idx: e.tensor_copy(
                        state1[:, idx:idx + 1], hst[:, W - 1:W]),
                        [hsb], [ST1[idx]])

        emit_splits(10 ** 6)
        vop(lambda e: e.tensor_scalar_mul(ccb_sb[:], state1[:], msend[:]),
            ST1 + [B("msend")], [B("ccb")])
        dma("gpsimd", cc_in[:, :], ccb_sb[:], [B("ccb")], [B("ccin")], "ccb")
        P.op("gpsimd", lambda e: e.collective_compute(
            "AllReduce", ALU.add,
            replica_groups=[[0, 1], [2, 3], [4, 5], [6, 7]],
            ins=[cc_in.ap().opt()], outs=[cc_out.ap().opt()]),
            ins=[B("ccin")], outs=[B("ccout")], sem="s_cc", amt=1)
        dma("gpsimd", ccb_sb[:], cc_out[:, :], [B("ccout")], [B("ccb")], "ccb")
        vop(lambda e: e.tensor_scalar_mul(state2[:], ccb_sb[:], mrecv[:]),
            [B("ccb"), B("mrecv")], ST2)

        S2v = S2.ap().bitcast(BF16)
        S3v = S3.ap().bitcast(BF16)
        xnh = lambda k: S3v[:, 2 * k * SW:2 * k * SW + W]
        xnl = lambda k: S3v[:, 2 * k * SW + W:2 * k * SW + 2 * W]
        hah = lambda ic: S3v[:, 2 * ic * SW:2 * ic * SW + W]
        hal = lambda ic: S3v[:, 2 * ic * SW + W:2 * ic * SW + 2 * W]
        G12v = G[12].ap().bitcast(BF16)
        for g in range(NG):
            c0 = g * W
            for k in range(KC):
                xnf, xnfb, xsl = (G[0], B("g0"), "g0") if k % 2 == 0 else (G[1], B("g1"), "g1")
                dma("scalar", xnf[:], xn_sp[k * 128:(k + 1) * 128, c0:c0 + W],
                    [B(f"xnsp:{g}:{k}")], [xnfb], xsl)
                P.op("tensor", lambda e, k=k, xnf=xnf: e.matmul(
                    ps[0][0:8, :], wr_sb[:, k * 8:(k + 1) * 8], xnf[:],
                    start=(k == 0), stop=(k == KC - 1)),
                    ins=[xnfb, B("wr")], outs=[PS[0]])
                vop(lambda e, k=k, xnf=xnf: e.tensor_copy(xnh(k), xnf[:]),
                    [xnfb], [S3b[k]])
                vop(lambda e, k=k, xnf=xnf: e.tensor_tensor(xnl(k), xnf[:], xnh(k),
                                                            ALU.subtract),
                    [xnfb, S3b[k]], [S3b[k]])
            vop(lambda e: e.tensor_copy(lgr_sb[:], ps[0][0:8, :]), [PS[0]], [B("lgr")])
            for s in range(4):
                P.op("tensor", lambda e, s=s: e.transpose(
                    ps[1][0:128, 0:8], lgr_sb[0:8, s * 128:(s + 1) * 128], ident[0:8, 0:8]),
                    ins=[B("lgr"), B("ident")], outs=[PS[1]])
                vop(lambda e, s=s: e.tensor_copy(lgt_sb[:, s * 8:(s + 1) * 8],
                                                 ps[1][:, 0:8]),
                    [PS[1]], [B("lgt")])
                P.op("vector", lambda e, s=s: e.max(out=mx_sb[:, s * 8:(s + 1) * 8],
                                                    in_=lgt_sb[:, s * 8:(s + 1) * 8]),
                     ins=[B("lgt")], outs=[B("mx")])
                vop(lambda e, s=s: e.tensor_scalar_mul(ngm_sb[:, s:s + 1],
                                                       mx_sb[:, s * 8:s * 8 + 1], -1.0),
                    [B("mx")], [B("ngm")])
                P.op("scalar", lambda e, s=s: e.activation(
                    out=ex_sb[:, s * 8:(s + 1) * 8], in_=lgt_sb[:, s * 8:(s + 1) * 8],
                    func=AF.Exp, bias=ngm_sb[:, s:s + 1]),
                    ins=[B("lgt"), B("ngm")], outs=[B("ex")])
                P.op("scalar", lambda e, s=s: e.activation(
                    out=den_sb[:, s:s + 1], in_=mx_sb[:, s * 8 + 1:s * 8 + 2],
                    func=AF.Exp, bias=ngm_sb[:, s:s + 1]),
                    ins=[B("mx"), B("ngm")], outs=[B("den")])
                P.op("scalar", lambda e, s=s: e.add(den1_sb[:, s:s + 1],
                                                    den_sb[:, s:s + 1], 1.0),
                     ins=[B("den")], outs=[B("den1")])
                vop(lambda e, s=s: e.reciprocal(rdn_sb[:, s:s + 1], den1_sb[:, s:s + 1]),
                    [B("den1")], [B("rdn")])
                vop(lambda e, s=s: e.tensor_scalar_mul(exs_sb[:, s * 8:(s + 1) * 8],
                                                       ex_sb[:, s * 8:(s + 1) * 8],
                                                       rdn_sb[:, s:s + 1]),
                    [B("ex"), B("rdn")], [B("exs")])
                vop(lambda e, s=s: e.tensor_single_scalar(
                    msk8_sb[:, s * 8:(s + 1) * 8], lgt_sb[:, s * 8:(s + 1) * 8],
                    mx_sb[:, s * 8 + 2:s * 8 + 3], ALU.is_le),
                    [B("lgt"), B("mx")], [B("msk8")])
                vop(lambda e, s=s: e.tensor_copy(comb_sb[:, s * 8:(s + 1) * 8],
                                                 exs_sb[:, s * 8:(s + 1) * 8]),
                    [B("exs")], [B("comb")])
                vop(lambda e, s=s: e.copy_predicated(comb_sb[:, s * 8:(s + 1) * 8],
                                                     msk8_sb[:, s * 8:(s + 1) * 8],
                                                     zero8[:]),
                    [B("msk8"), B("zero8")], [B("comb")])
                P.op("tensor", lambda e, s=s: e.transpose(
                    ps[1][0:8, 0:128], comb_sb[:, s * 8:(s + 1) * 8], ident[:]),
                    ins=[B("comb"), B("ident")], outs=[PS[1]])
                vop(lambda e, s=s: e.tensor_copy(crow_sb[0:8, s * 128:(s + 1) * 128],
                                                 ps[1][0:8, 0:128]),
                    [PS[1]], [B("crow")])
            dma("scalar", bcr_sb[:], bc_sp[:, c0:c0 + W], [B(f"bcsp:{g}")], [B("bcr")], "bcr")
            for n in range(NST):
                P.op("tensor", lambda e, n=n: e.matmul(
                    ps[7][:], ident[0:32, n:n + 1].broadcast_to([32, 128]), bcr_sb[:],
                    start=True, stop=True),
                    ins=[B("bcr"), B("ident")], outs=[PS[7]])
                vop(lambda e, n=n: e.tensor_copy(S1[:, sc(n, 0, W)], ps[7][:]),
                    [PS[7]], [S1b[n]])
                P.op("tensor", lambda e, n=n: e.matmul(
                    ps[7][:], ident[0:32, 16 + n:17 + n].broadcast_to([32, 128]), bcr_sb[:],
                    start=True, stop=True),
                    ins=[B("bcr"), B("ident")], outs=[PS[7]])
                vop(lambda e, n=n: e.tensor_copy(S2[:, sc(n, 0, W)], ps[7][:]),
                    [PS[7]], [S2b[n]])
            for hc in range(KC):
                dcht, dchb = (dch0, B("g0")) if hc % 2 == 0 else (dch1, B("g1"))
                xvht, xvhb = (xvh0, B("g2")) if hc % 2 == 0 else (xvh1, B("g3"))
                ya, yab, ysl = (G[12], B("g12"), "g12") if hc % 2 == 0 else (G[13], B("g13"), "g13")
                dma("scalar", dcht[:], delta_sp[hc * 128:(hc + 1) * 128, c0:c0 + W],
                    [B(f"dlsp:{g}:{hc}")], [dchb], "g0" if hc % 2 == 0 else "g1")
                dma("scalar", xvht[:], xcv_sp[hc * 128:(hc + 1) * 128, c0:c0 + W],
                    [B(f"xcvsp:{g}:{hc}")], [xvhb], "g2" if hc % 2 == 0 else "g3")
                vop(lambda e, dcht=dcht, xvht=xvht: e.tensor_mul(ub[:], dcht[:], xvht[:]),
                    [dchb, xvhb], [B("g4")])
                vop(lambda e, ya=ya: e.memset(ya[:], 0.0), [], [yab])
                for n in range(NST):
                    idx = hc * NST + n
                    dAt, dAb = (dA0, B("g5")) if n % 2 == 0 else (dA1, B("g6"))
                    bxt, bxb = (bx0, B("g7")) if n % 2 == 0 else (bx1, B("g8"))
                    hst, hsb = (hs0, B("g9")) if n % 2 == 0 else (hs1, B("g10"))
                    P.op("scalar", lambda e, dAt=dAt, dcht=dcht, idx=idx: e.activation(
                        out=dAt[:], in_=dcht[:], func=AF.Exp, scale=a_sb[:, idx:idx + 1]),
                        ins=[dchb, B("A")], outs=[dAb])
                    vop(lambda e, n=n, bxt=bxt: e.tensor_mul(bxt[:], ub[:],
                                                             S1[:, sc(n, 0, W)]),
                        [B("g4"), S1b[n]], [bxb])
                    P.op("vector", lambda e, dAt=dAt, bxt=bxt, hst=hst, idx=idx:
                         e.tensor_tensor_scan(hst[:], dAt[:], bxt[:],
                                              state2[:, idx:idx + 1], ALU.mult, ALU.add),
                         ins=[dAb, bxb, ST2[idx]], outs=[hsb])
                    vop(lambda e, hst=hst, idx=idx: e.tensor_copy(
                        state2[:, idx:idx + 1], hst[:, W - 1:W]),
                        [hsb], [ST2[idx]])
                    vop(lambda e, hst=hst, n=n: e.tensor_mul(ym0[:], hst[:],
                                                             S2[:, sc(n, 0, W)]),
                        [hsb, S2b[n]], [B("g11")])
                    vop(lambda e, ya=ya: e.tensor_add(ya[:], ya[:], ym0[:]),
                        [B("g11"), yab], [yab])
                P.op("vector", lambda e, ya=ya, hc=hc, xvht=xvht: e.scalar_tensor_tensor(
                    ya[:], xvht[:], dd_c(hc), ya[:], ALU.mult, ALU.add),
                    ins=[xvhb, B("vec"), yab], outs=[yab])
                dma("scalar", y_sp[hc * 128:(hc + 1) * 128, c0:c0 + W], ya[:],
                    [yab], [B(f"ysp:{g}:{hc}")], ysl)
            for ex in range(NE):
                P.op("tensor", lambda e, ex=ex: e.matmul(
                    ps[2][:], ident[0:8, ex:ex + 1].broadcast_to([8, 128]), crow_sb[:],
                    start=True, stop=True),
                    ins=[B("crow"), B("ident")], outs=[PS[2]])
                vop(lambda e: e.tensor_copy(r2b[:], ps[2][:]), [PS[2]], [B("r2b")])
                for ic in range(NIC):
                    wgi = next_ws()
                    wgv = WS[wgi].ap().bitcast(BF16)
                    dma("sync", wgv[:, 0:2 * KC * 128], wg_b[ex, ic],
                        [B(wbkey(wg_b[ex, ic]))], [WSb[wgi]], f"ws_{wgi}")
                    wui = next_ws()
                    wuv = WS[wui].ap().bitcast(BF16)
                    dma("sync", wuv[:, 0:2 * KC * 128], wu_b[ex, ic],
                        [B(wbkey(wu_b[ex, ic]))], [WSb[wui]], f"ws_{wui}")
                    HL = KC * 128
                    passes = [(0, xnh), (0, xnl), (HL, xnh)]
                    for pidx, (woff, xf) in enumerate(passes):
                        for k in range(KC):
                            first = (pidx == 0 and k == 0)
                            last = (pidx == 2 and k == KC - 1)
                            P.op("tensor", lambda e, k=k, wgv=wgv, woff=woff, xf=xf,
                                 first=first, last=last: e.matmul(
                                     ps[0][:],
                                     wgv[:, woff + k * 128:woff + (k + 1) * 128],
                                     xf(k), start=first, stop=last),
                                 ins=[WSb[wgi], S3b[k]], outs=[PS[0]], inc=last)
                    for pidx, (woff, xf) in enumerate(passes):
                        for k in range(KC):
                            first = (pidx == 0 and k == 0)
                            last = (pidx == 2 and k == KC - 1)
                            P.op("tensor", lambda e, k=k, wuv=wuv, woff=woff, xf=xf,
                                 first=first, last=last: e.matmul(
                                     ps[1][:],
                                     wuv[:, woff + k * 128:woff + (k + 1) * 128],
                                     xf(k), start=first, stop=last),
                                 ins=[WSb[wui], S3b[k]], outs=[PS[1]], inc=last)
                    P.op("scalar", lambda e: e.activation(out=G[13][:], in_=ps[0][:],
                                                          func=AF.Silu),
                         ins=[PS[0]], outs=[B("g13")])
                    vop(lambda e: e.tensor_mul(G[12][:], G[13][:], ps[1][:]),
                        [B("g13"), PS[1]], [B("g12")])
                    vop(lambda e: e.tensor_mul(G[13][:], G[12][:], r2b[:]),
                        [B("g12"), B("r2b")], [B("g13")])
                    vop(lambda e: e.tensor_copy(G12v[:, 0:W], G[13][:]),
                        [B("g13")], [B("g12")])
                    vop(lambda e: e.tensor_tensor(G12v[:, W:2 * W], G[13][:],
                                                  G12v[:, 0:W], ALU.subtract),
                        [B("g13")], [B("g12")])
                    dma("scalar", ha_sp[ex, ic], G12v[:, 0:2 * W],
                        [B("g12")], [B(f"ha:{ex}:{ic}")], "g12")
            for m in range(KC):
                vop(lambda e, m=m: e.memset(S2[:, sc(m, 0, W)], 0.0), [], [S2b[m]])
            for ex in range(NE):
                for ic in range(NIC):
                    dma("scalar", S3v[:, 2 * ic * SW:2 * ic * SW + 2 * W],
                        ha_sp[ex, ic], [B(f"ha:{ex}:{ic}")], [S3b[ic]], f"s3_{ic}")
                for m in range(KC):
                    wdi = next_ws()
                    wdv = WS[wdi].ap().bitcast(BF16)
                    dma("sync", wdv[:, 0:2 * NIC * 128], wd_b[ex, m],
                        [B(wbkey(wd_b[ex, m]))], [WSb[wdi]], f"ws_{wdi}")
                    pi = 3 + (m % 2)
                    DL = NIC * 128
                    for pidx, (woff, hf) in enumerate([(0, hah), (0, hal), (DL, hah)]):
                        for ic in range(NIC):
                            first = (pidx == 0 and ic == 0)
                            last = (pidx == 2 and ic == NIC - 1)
                            P.op("tensor", lambda e, ic=ic, wdv=wdv, pi=pi, woff=woff,
                                 hf=hf, first=first, last=last: e.matmul(
                                     ps[pi][:],
                                     wdv[:, woff + ic * 128:woff + (ic + 1) * 128],
                                     hf(ic), start=first, stop=last),
                                 ins=[WSb[wdi], S3b[ic]], outs=[PS[pi]], inc=last)
                    vop(lambda e, m=m, pi=pi: e.tensor_add(
                        S2[:, sc(m, 0, W)], S2[:, sc(m, 0, W)], ps[pi][:]),
                        [PS[pi], S2b[m]], [S2b[m]])
            P.op("tensor", lambda e, c0=c0: e.matmul(
                ps[6][:], ones_row[:], r2all[0:1, c0:c0 + W], start=True, stop=True),
                ins=[B("r2all"), B("oner")], outs=[PS[6]])
            vop(lambda e: e.tensor_copy(r2b[:], ps[6][:]), [PS[6]], [B("r2b")])
            for mr in range(2):
                for k in range(KC):
                    xnf, xnfb, xsl = (G[0], B("g0"), "g0") if k % 2 == 0 else (G[1], B("g1"), "g1")
                    dma("scalar", xnf[:], xn_sp[k * 128:(k + 1) * 128, c0:c0 + W],
                        [B(f"xnsp:{g}:{k}")], [xnfb], xsl)
                    wi = next_ws()
                    dma("sync", WS[wi][:, 0:1024],
                        win_z[k * 128:(k + 1) * 128, mr * 1024:(mr + 1) * 1024],
                        [], [WSb[wi]], f"ws_{wi}")
                    hbt, hbb = (hb0, B("g2")) if k % 2 == 0 else (hb1, B("g3"))
                    P.op("vector", lambda e, k=k, hbt=hbt, xnf=xnf: e.scalar_tensor_tensor(
                        hbt[:], xnf[:], nw_c(k), r2b[:], ALU.mult, ALU.mult),
                        ins=[xnfb, B("vec"), B("r2b")], outs=[hbb])
                    for m8 in range(8):
                        P.op("tensor", lambda e, m8=m8, wi=wi, hbt=hbt, k=k: e.matmul(
                            ps[m8][:], WS[wi][:, m8 * 128:(m8 + 1) * 128], hbt[:],
                            start=(k == 0), stop=(k == KC - 1)),
                            ins=[WSb[wi], hbb], outs=[PS[m8]], inc=(m8 == 7))
                for m8 in range(8):
                    m = mr * 8 + m8
                    yld, yldb, ysl2 = (G[11], B("g11"), "g11") if m8 % 2 == 0 else (G[4], B("g4"), "g4")
                    dma("scalar", yld[:], y_sp[m * 128:(m + 1) * 128, c0:c0 + W],
                        [B(f"ysp:{g}:{m}")], [yldb], ysl2)
                    P.op("scalar", lambda e, m8=m8: e.activation(
                        out=zsb[:], in_=ps[m8][:], func=AF.Silu),
                        ins=[PS[m8]], outs=[B("g0")])
                    vop(lambda e, m=m, yld=yld: e.tensor_mul(S3[:, sc(m, 0, W)],
                                                             yld[:], zsb[:]),
                        [yldb, B("g0")], [S3b[m]])
            for mr in range(2):
                for k in range(KC):
                    wi = next_ws()
                    dma("sync", WS[wi][:, 0:1024],
                        wout[k * 128:(k + 1) * 128, mr * 1024:(mr + 1) * 1024],
                        [], [WSb[wi]], f"ws_{wi}")
                    for m8 in range(8):
                        P.op("tensor", lambda e, m8=m8, wi=wi, k=k: e.matmul(
                            ps[m8][:], WS[wi][:, m8 * 128:(m8 + 1) * 128],
                            S3[:, sc(k, 0, W)],
                            start=(k == 0), stop=(k == KC - 1)),
                            ins=[WSb[wi], S3b[k]], outs=[PS[m8]], inc=(m8 == 7))
                for m8 in range(8):
                    m = mr * 8 + m8
                    xnf, xnfb, xsl = (G[0], B("g0"), "g0") if m8 % 2 == 0 else (G[1], B("g1"), "g1")
                    dma("scalar", xnf[:], xn_sp[m * 128:(m + 1) * 128, c0:c0 + W],
                        [B(f"xnsp:{g}:{m}")], [xnfb], xsl)
                    xgt, xgb = (xg0, B("g7")) if m8 % 2 == 0 else (xg1, B("g8"))
                    tot, tob = (to0, B("g5")) if m8 % 2 == 0 else (to1, B("g6"))
                    dma("scalar", xgt[:], x_fm[m * 128:(m + 1) * 128, 3 + c0:3 + c0 + W],
                        [], [xgb], "g7" if m8 % 2 == 0 else "g8")
                    vop(lambda e, m8=m8, m=m, tot=tot: e.tensor_add(
                        tot[:], ps[m8][:], S2[:, sc(m, 0, W)]),
                        [PS[m8], S2b[m]], [tob])
                    vop(lambda e, m=m, tot=tot, xnf=xnf: e.tensor_add(
                        tot[:], tot[:], xnf[:]),
                        [tob, xnfb], [tob])
                    vop(lambda e, m=m, tot=tot, xgt=xgt: e.tensor_add(
                        S1[:, sc(m, 0, W)], tot[:], xgt[:]),
                        [tob, xgb, S1b[m]], [S1b[m]])
            for m in range(KC):
                dma("scalar", out_fm[m * 128:(m + 1) * 128, c0:c0 + W], S1[:, sc(m, 0, W)],
                    [S1b[m]], [B(f"out:{g}:{m}")], f"s1_{m}")

        for m in range(KC):
            v = P.ctr.get(f"qd_s1_{m}", 0)
            P.ops["sync"].append(
                lambda e, v=v, m=m: e.wait_ge(P.sems[f"qd_s1_{m}"], v))

        @block.sync
        def _(eng):
            for cl in P.ops["sync"]:
                cl(eng)

        @block.tensor
        def _(eng):
            for cl in P.ops["tensor"]:
                cl(eng)

        @block.vector
        def _(eng):
            for cl in P.ops["vector"]:
                cl(eng)

        @block.scalar
        def _(eng):
            for cl in P.ops["scalar"]:
                cl(eng)

        @block.gpsimd
        def _(eng):
            for cl in P.ops["gpsimd"]:
                cl(eng)

    return nc


def _stage(inputs):
    f32 = np.float32
    x = np.asarray(inputs["L0_attn_x"], f32)
    inp = np.asarray(inputs["L0_attn_in_proj_w"], f32)
    conv = np.ascontiguousarray(np.asarray(inputs["L0_attn_conv_w"], f32))
    xp = np.asarray(inputs["L0_attn_x_proj_w"], f32)
    dt = np.asarray(inputs["L0_attn_dt_proj_w"], f32)
    outp = np.asarray(inputs["L0_attn_out_proj_w"], f32)
    wr = np.asarray(inputs["L0_ffn_Wr"], f32)
    wg = np.asarray(inputs["L0_ffn_Wg"], f32)
    wu = np.asarray(inputs["L0_ffn_Wu"], f32)
    wd = np.asarray(inputs["L0_ffn_Wd"], f32)

    win_xc = np.ascontiguousarray(inp[:H, :].T)
    win_z = np.ascontiguousarray(inp[H:, :].T)
    wxp_s = np.ascontiguousarray(
        xp.T.reshape(KC, 128, 96).transpose(1, 0, 2).reshape(128, KC * 96))
    wdt_s = np.ascontiguousarray(dt.T)
    wout = np.ascontiguousarray(outp.T)
    wr_s = np.ascontiguousarray(
        wr.reshape(KC, 128, 8).transpose(1, 0, 2).reshape(128, KC * 8))
    wg_s = np.ascontiguousarray(
        wg.reshape(NE, KC, 128, NIC, 128).transpose(0, 3, 2, 1, 4)
        .reshape(NE, NIC, 128, KC * 128))
    wu_s = np.ascontiguousarray(
        wu.reshape(NE, KC, 128, NIC, 128).transpose(0, 3, 2, 1, 4)
        .reshape(NE, NIC, 128, KC * 128))
    wd_s = np.ascontiguousarray(
        wd.reshape(NE, NIC, 128, KC, 128).transpose(0, 3, 2, 1, 4)
        .reshape(NE, KC, 128, NIC * 128))
    cm = lambda v: np.ascontiguousarray(np.asarray(v, f32).reshape(KC, 128).T)
    vec4 = np.ascontiguousarray(np.concatenate([
        cm(inputs["L0_attn_norm_w"]), cm(inputs["L0_attn_conv_b"]),
        cm(inputs["L0_attn_dt_proj_b"]), cm(inputs["L0_attn_D"])], axis=1))
    alog_s = np.ascontiguousarray(
        np.asarray(inputs["L0_attn_A_log"], f32).reshape(KC, 128, NST)
        .transpose(1, 0, 2).reshape(128, KC * NST))
    ones_col = np.ones((128, 1), f32)
    ones_row = np.ones((1, 128), f32)
    ident = np.eye(128, dtype=f32)

    shared = dict(win_xc=win_xc, win_z=win_z, wconv=conv, wxp_s=wxp_s, wdt_s=wdt_s,
                  wout=wout, wr_s=wr_s, wg_s=wg_s, wu_s=wu_s, wd_s=wd_s, vec4=vec4,
                  alog_s=alog_s, ones_col=ones_col, ones_row=ones_row, ident=ident)
    in_maps = []
    for c in range(8):
        b, j = c // 2, c % 2
        xb = x[b]
        halo = np.zeros((3, H), f32) if j == 0 else xb[2045:2048]
        x_fm = np.ascontiguousarray(
            np.concatenate([halo, xb[j * 2048:(j + 1) * 2048]], 0).T)
        im = dict(shared)
        im.update(x_fm=x_fm,
                  msend=np.full((128, 1), 1.0 if j == 0 else 0.0, f32),
                  mrecv=np.full((128, 1), float(j), f32))
        in_maps.append(im)
    return in_maps


def kernel(**inputs):
    in_maps = _stage(inputs)
    nc = build()
    res = run_bass_kernel_spmd(nc, in_maps, core_ids=list(range(8)))
    out = np.empty((4, 4096, 2048), np.float32)
    for c in range(8):
        b, j = c // 2, c % 2
        out[b, j * 2048:(j + 1) * 2048, :] = res.results[c]["out_fm"].T
    return out


# version 18
# speedup vs baseline: 1.0084x; 1.0084x over previous
import numpy as np
import concourse.bass as bass
import concourse.mybir as mybir
from concourse.bass_utils import run_bass_kernel_spmd
from contextlib import ExitStack

F32 = mybir.dt.float32
U8 = mybir.dt.uint8
AF = mybir.ActivationFunctionType
ALU = mybir.AluOpType

H = 2048
KC = 16
W = 512
NG = 4
SW = 515
DTR = 64
NST = 16
IFF = 1408
NIC = 11
NE = 8
EPS = 1e-5
INVH = 1.0 / 2048.0


class Buf:
    __slots__ = ("name", "w", "r")

    def __init__(self, name):
        self.name = name
        self.w = None
        self.r = []


class Plan:
    ENGINES = ("tensor", "vector", "scalar", "sync", "gpsimd")

    def __init__(self):
        self.ops = {e: [] for e in self.ENGINES}
        self.ctr = {}
        self.seen = {e: {} for e in self.ENGINES}
        self.sems = {}
        self._bufs = {}
        self._pending = {e: [] for e in self.ENGINES}

    def buf(self, name):
        b = self._bufs.get(name)
        if b is None:
            b = self._bufs[name] = Buf(name)
        return b

    def _waits(self, eng, deps):
        need = {}
        for d in deps:
            if d is None:
                continue
            s, c = d
            if eng == "tensor" and s == "s_tensor":
                continue
            if need.get(s, 0) < c:
                need[s] = c
        out = []
        seen = self.seen[eng]
        for s, c in need.items():
            if seen.get(s, 0) < c:
                seen[s] = c
                out.append((s, c))
        return out

    def op(self, eng, fn, ins=(), outs=(), sem=None, amt=1, inc=True):
        deps = []
        for b in ins:
            deps.append(b.w)
        for b in outs:
            deps.append(b.w)
            deps.extend(b.r)
        waits = self._waits(eng, deps)
        plan = self
        if not inc:
            def closure0(e, waits=waits, fn=fn):
                for (s, c) in waits:
                    e.wait_ge(plan.sems[s], c)
                fn(e)
            self.ops[eng].append(closure0)
            self._pending[eng].append((list(ins), list(outs)))
            return None
        sem = sem or ("s_" + eng)
        self.ctr[sem] = self.ctr.get(sem, 0) + amt
        cnt = self.ctr[sem]

        def closure(e, waits=waits, fn=fn, sem=sem, amt=amt):
            for (s, c) in waits:
                e.wait_ge(plan.sems[s], c)
            fn(e).then_inc(plan.sems[sem], amt)

        self.ops[eng].append(closure)
        tok = (sem, cnt)
        for pins, pouts in self._pending[eng]:
            for b in pouts:
                b.w, b.r = tok, []
            for b in pins:
                b.r.append(tok)
        self._pending[eng] = []
        for b in outs:
            b.w, b.r = tok, []
        for b in ins:
            b.r.append(tok)
        return tok


def build():
    nc = bass.Bass("TRN2", target_bir_lowering=False, debug=False)
    P = Plan()

    di = lambda n, s: nc.dram_tensor(n, s, F32, kind="ExternalInput").ap()
    x_fm = di("x_fm", [H, 3 + 2048])
    win_xc = di("win_xc", [H, H])
    win_z = di("win_z", [H, H])
    wconv = di("wconv", [4, H, H])
    wxp_s = di("wxp_s", [128, KC * 96])
    wdt_s = di("wdt_s", [DTR, H])
    wout = di("wout", [H, H])
    wr_s = di("wr_s", [128, KC * 8])
    wg_s = di("wg_s", [NE, NIC, 128, KC * 128])
    wu_s = di("wu_s", [NE, NIC, 128, KC * 128])
    wd_s = di("wd_s", [NE, KC, 128, NIC * 128])
    vec4 = di("vec4", [128, 4 * KC])
    alog_s = di("alog_s", [128, KC * NST])
    ones_col_d = di("ones_col", [128, 1])
    ones_row_d = di("ones_row", [1, 128])
    ident_d = di("ident", [128, 128])
    msend_d = di("msend", [128, 1])
    mrecv_d = di("mrecv", [128, 1])
    out_fm = nc.dram_tensor("out_fm", [H, 2048], F32, kind="ExternalOutput").ap()

    ds = lambda n, s: nc.dram_tensor(n, s, F32).ap()
    xn_sp = ds("xn_sp", [H, 2048])
    delta_sp = ds("delta_sp", [H, 2048])
    xcv_sp = ds("xcv_sp", [H, 2048])
    bc_sp = ds("bc_sp", [32, 2048])
    cc_in = nc.dram_tensor("cc_in", [128, KC * NST], F32)
    cc_out = nc.dram_tensor("cc_out", [128, KC * NST], F32)
    BF16 = mybir.dt.bfloat16
    wg_b = nc.dram_tensor("wg_b", [NE, NIC, 128, 2 * KC * 128], BF16).ap()
    wu_b = nc.dram_tensor("wu_b", [NE, NIC, 128, 2 * KC * 128], BF16).ap()
    wd_b = nc.dram_tensor("wd_b", [NE, KC, 128, 2 * NIC * 128], BF16).ap()
    y_sp = nc.dram_tensor("y_sp", [H, 2048], F32).ap()
    ha_sp = nc.dram_tensor("ha_sp", [NE, NIC, 128, 2 * W], BF16).ap()

    with ExitStack() as stack:
        e_ = stack.enter_context
        sb = lambda n, s, dt=F32: e_(nc.sbuf_tensor(n, s, dt))
        S1 = sb("S1", [128, KC * SW])
        S2 = sb("S2", [128, KC * SW])
        S3 = sb("S3", [128, KC * SW])
        WS = [sb(f"ws{i}", [128, 2048]) for i in range(4)]
        vec_sb = sb("vec_sb", [128, 4 * KC])
        alog_sb = sb("alog_sb", [128, KC * NST])
        a_sb = sb("a_sb", [128, KC * NST])
        wxp_sb = sb("wxp_sb", [128, KC * 96])
        wr_sb = sb("wr_sb", [128, KC * 8])
        ones_col = sb("ones_col_s", [128, 1])
        ones_row = sb("ones_row_s", [1, 128])
        ident = sb("ident_s", [128, 128])
        msend = sb("msend_s", [128, 1])
        mrecv = sb("mrecv_s", [128, 1])
        state1 = sb("state1", [128, KC * NST])
        state2 = sb("state2", [128, KC * NST])
        r2all = sb("r2all", [1, 2048])
        xc_carry = sb("xc_carry", [128, KC * 3])
        xh_sb = sb("xh_sb", [128, KC * 3])
        xnh_sb = sb("xnh_sb", [128, KC * 3])
        h3_sb = sb("h3_sb", [128, KC * 3])
        tail_sb = sb("tail_sb", [128, 12])
        t3a = sb("t3a", [128, 3])
        eps_sb = sb("eps_sb", [128, 1])
        RA = sb("RA", [1, SW]); RB = sb("RB", [1, SW]); RC = sb("RC", [1, SW])
        RD = sb("RD", [1, SW]); RE = sb("RE", [1, SW]); RF = sb("RF", [1, SW])
        G = [sb(f"g{i}", [128, W]) for i in range(14)]
        tb0, tb1 = G[0], G[1]
        hb0, hb1 = G[2], G[3]
        ub = G[4]
        dA0, dA1 = G[5], G[6]
        bx0, bx1 = G[7], G[8]
        hs0, hs1 = G[9], G[10]
        e1b, spb, rbb = G[11], G[12], G[13]
        dch0, dch1 = G[0], G[1]
        xvh0, xvh1 = G[2], G[3]
        ym0, ym1 = G[11], G[12]
        zsb = G[0]
        to0, to1 = G[5], G[6]
        xg0, xg1 = G[7], G[8]
        gsb = G[0]
        cbb = G[1]
        hu_t = G[2]
        mskb = sb("mskb", [128, W], U8)
        r2b = sb("r2b", [128, W])
        dl_sb = sb("dl_sb", [DTR, W])
        bcq_sb = sb("bcq_sb", [32, W])
        bcr_sb = sb("bcr_sb", [32, W])
        lgr_sb = sb("lgr_sb", [8, W])
        crow_sb = sb("crow_sb", [8, W])
        lgt_sb = sb("lgt_sb", [128, 32])
        mx_sb = sb("mx_sb", [128, 32])
        ex_sb = sb("ex_sb", [128, 32])
        exs_sb = sb("exs_sb", [128, 32])
        comb_sb = sb("comb_sb", [128, 32])
        msk8_sb = sb("msk8_sb", [128, 32], U8)
        zero8 = sb("zero8", [128, 8])
        ngm_sb = sb("ngm_sb", [128, 4])
        den_sb = sb("den_sb", [128, 4])
        den1_sb = sb("den1_sb", [128, 4])
        rdn_sb = sb("rdn_sb", [128, 4])
        ccb_sb = sb("ccb_sb", [128, KC * NST])

        ps = [e_(nc.psum_tensor(f"ps{i}", [128, W], F32)) for i in range(8)]

        sems = {}
        slot_names = (["setup", "bcr", "bcq", "ccb"]
                      + [f"s1_{k}" for k in range(KC)]
                      + [f"s2_{k}" for k in range(KC)]
                      + [f"w2_{k}" for k in range(KC)]
                      + [f"w3_{k}" for k in range(KC)]
                      + [f"ws_{i}" for i in range(4)]
                      + [f"s3_{k}" for k in range(NIC)]
                      + [f"g{i}" for i in (0, 1, 2, 3, 4, 7, 8, 11, 12, 13)])
        for s in (["s_tensor", "s_vector", "s_scalar", "s_gpsimd", "s_cc"]
                  + ["qd_" + n for n in slot_names]):
            sems[s] = e_(nc.semaphore(s))
        P.sems = sems
        block = e_(nc.Block())

        B = P.buf
        PS = [B(f"ps{i}") for i in range(8)]
        ST1 = [B(f"st1:{i}") for i in range(KC * NST)]
        ST2 = [B(f"st2:{i}") for i in range(KC * NST)]

        def sc(k, c0, c1):
            return slice(k * SW + c0, k * SW + c1)

        S1b = [B(f"S1:{k}") for k in range(KC)]
        S2b = [B(f"S2:{k}") for k in range(KC)]
        S3b = [B(f"S3:{k}") for k in range(KC)]
        WSb = [B(f"ws:{i}") for i in range(4)]

        def dma(q, dst_ap, src_ap, ins, outs, slot):
            P.op(q, lambda e, d=dst_ap, s=src_ap: e.dma_start(out=d, in_=s),
                 ins=ins, outs=outs, sem="qd_" + slot, amt=16)

        def vop(fn, ins, outs, eng="vector"):
            P.op(eng, fn, ins=ins, outs=outs)

        wrot = [0]

        def next_ws():
            wrot[0] = (wrot[0] + 1) % 4
            return wrot[0]

        split_jobs = []
        for ex in range(NE):
            for ic in range(NIC):
                split_jobs.append((wg_s[ex, ic], wg_b[ex, ic], KC * 128))
                split_jobs.append((wu_s[ex, ic], wu_b[ex, ic], KC * 128))
        for ex in range(NE):
            for m in range(KC):
                split_jobs.append((wd_s[ex, m], wd_b[ex, m], NIC * 128))
        split_iter = iter(split_jobs)

        def wbkey(ap):
            return "wb:" + str(ap.name) + ":" + str(ap.offset)

        def emit_splits(n):
            for _ in range(n):
                job = next(split_iter, None)
                if job is None:
                    return
                srcw, dstw, wdt = job
                a = next_ws()
                b = next_ws()
                dma("sync", WS[a][:, 0:wdt], srcw, [], [WSb[a]], f"ws_{a}")
                bv = WS[b].ap().bitcast(BF16)
                P.op("gpsimd", lambda e, a=a, bv=bv, wdt=wdt: e.tensor_copy(
                    bv[:, 0:wdt], WS[a][:, 0:wdt]),
                    ins=[WSb[a]], outs=[WSb[b]])
                P.op("gpsimd", lambda e, a=a, bv=bv, wdt=wdt: e.tensor_tensor(
                    bv[:, wdt:2 * wdt], WS[a][:, 0:wdt], bv[:, 0:wdt],
                    ALU.subtract),
                    ins=[WSb[a], WSb[b]], outs=[WSb[b]])
                dma("sync", dstw, bv[:, 0:2 * wdt], [WSb[b]], [B(wbkey(dstw))],
                    f"ws_{b}")

        def warm(dep):
            P.op("tensor", lambda e: e.matmul(
                ps[7][0:1, 0:64], ones_col[:], a_sb[:, 0:64], start=True, stop=True),
                ins=[dep, B("onec"), B("A")], outs=[PS[7]])

        for src, dsttile, name in [
            (vec4, vec_sb, "vec"), (alog_s, alog_sb, "alog"), (wxp_s, wxp_sb, "wxp"),
            (wr_s, wr_sb, "wr"), (ones_col_d, ones_col, "onec"),
            (ones_row_d, ones_row, "oner"), (ident_d, ident, "ident"),
            (msend_d, msend, "msend"), (mrecv_d, mrecv, "mrecv"),
        ]:
            dma("sync", dsttile[:], src, [B("setup_chain")], [B(name), B("setup_chain")],
                "setup")
        nw_c = lambda k: vec_sb[:, k:k + 1]
        cb_c = lambda k: vec_sb[:, KC + k:KC + k + 1]
        db_c = lambda k: vec_sb[:, 2 * KC + k:2 * KC + k + 1]
        dd_c = lambda k: vec_sb[:, 3 * KC + k:3 * KC + k + 1]
        P.op("scalar", lambda e: e.activation(out=a_sb[:], in_=alog_sb[:], func=AF.Exp),
             ins=[B("alog")], outs=[B("A")])
        vop(lambda e: e.tensor_scalar_mul(a_sb[:], a_sb[:], -1.0), [B("A")], [B("A")])
        vop(lambda e: e.memset(state1[:], 0.0), [], ST1)
        vop(lambda e: e.memset(eps_sb[:], EPS), [], [B("eps")])
        vop(lambda e: e.memset(zero8[:], 0.0), [], [B("zero8")])

        for g in range(NG):
            c0 = g * W
            for k in range(KC):
                dma("scalar", S1[:, sc(k, 0, W)],
                    x_fm[k * 128:(k + 1) * 128, 3 + c0:3 + c0 + W], [], [S1b[k]],
                    f"s1_{k}")
            if g == 0:
                for k in range(KC):
                    dma("scalar", xh_sb[:, k * 3:(k + 1) * 3],
                        x_fm[k * 128:(k + 1) * 128, 0:3],
                        [B("setup_chain")], [B(f"xh{k}"), B("setup_chain")], "setup")
            for k in range(KC):
                sq, sqb = (tb0, B("g0")) if k % 2 == 0 else (tb1, B("g1"))
                P.op("scalar", lambda e, k=k, sq=sq: e.activation(
                    out=sq[:], in_=S1[:, sc(k, 0, W)], func=AF.Square),
                    ins=[S1b[k]], outs=[sqb])
                P.op("tensor", lambda e, k=k: e.matmul(
                    ps[0][0:1, :], ones_col[:], S1[:, sc(k, 0, W)],
                    start=(k == 0), stop=(k == KC - 1)),
                    ins=[S1b[k], B("onec")], outs=[PS[0]])
                P.op("tensor", lambda e, k=k, sq=sq: e.matmul(
                    ps[1][0:1, :], ones_col[:], sq[:],
                    start=(k == 0), stop=(k == KC - 1)),
                    ins=[sqb, B("onec")], outs=[PS[1]])
            if g == 0:
                for k in range(KC):
                    P.op("scalar", lambda e, k=k: e.activation(
                        out=t3a[:], in_=xh_sb[:, k * 3:(k + 1) * 3], func=AF.Square),
                        ins=[B(f"xh{k}")], outs=[B("t3a")])
                    P.op("tensor", lambda e, k=k: e.matmul(
                        ps[2][0:1, 0:3], ones_col[:], xh_sb[:, k * 3:(k + 1) * 3],
                        start=(k == 0), stop=(k == KC - 1)),
                        ins=[B(f"xh{k}"), B("onec")], outs=[PS[2]])
                    P.op("tensor", lambda e, k=k: e.matmul(
                        ps[3][0:1, 0:3], ones_col[:], t3a[:],
                        start=(k == 0), stop=(k == KC - 1)),
                        ins=[B("t3a"), B("onec")], outs=[PS[3]])
            EW = SW if g == 0 else W
            vop(lambda e: e.tensor_copy(RA[0:1, 0:W], ps[0][0:1, :]), [PS[0]], [B("RA")])
            vop(lambda e: e.tensor_copy(RB[0:1, 0:W], ps[1][0:1, :]), [PS[1]], [B("RB")])
            if g == 0:
                vop(lambda e: e.tensor_copy(RA[0:1, W:SW], ps[2][0:1, 0:3]), [PS[2]], [B("RA")])
                vop(lambda e: e.tensor_copy(RB[0:1, W:SW], ps[3][0:1, 0:3]), [PS[3]], [B("RB")])
            vop(lambda e, EW=EW: e.tensor_scalar_mul(RA[0:1, 0:EW], RA[0:1, 0:EW], INVH),
                [B("RA")], [B("RA")])
            vop(lambda e, EW=EW: e.tensor_scalar_mul(RB[0:1, 0:EW], RB[0:1, 0:EW], INVH),
                [B("RB")], [B("RB")])
            P.op("scalar", lambda e, EW=EW: e.activation(
                out=RD[0:1, 0:EW], in_=RA[0:1, 0:EW], func=AF.Square),
                ins=[B("RA")], outs=[B("RD")])
            vop(lambda e, EW=EW: e.tensor_sub(RC[0:1, 0:EW], RB[0:1, 0:EW], RD[0:1, 0:EW]),
                [B("RB"), B("RD")], [B("RC")])
            P.op("scalar", lambda e, EW=EW: e.activation(
                out=RD[0:1, 0:EW], in_=RC[0:1, 0:EW], func=AF.Sqrt, bias=eps_sb[0:1, 0:1]),
                ins=[B("RC"), B("eps")], outs=[B("RD")])
            vop(lambda e, EW=EW: e.reciprocal(RE[0:1, 0:EW], RD[0:1, 0:EW]),
                [B("RD")], [B("RE")])
            vop(lambda e, EW=EW: e.tensor_mul(RF[0:1, 0:EW], RA[0:1, 0:EW], RE[0:1, 0:EW]),
                [B("RA"), B("RE")], [B("RF")])
            vop(lambda e, EW=EW: e.tensor_scalar_mul(RF[0:1, 0:EW], RF[0:1, 0:EW], -1.0),
                [B("RF")], [B("RF")])
            vop(lambda e, EW=EW: e.tensor_mul(RB[0:1, 0:EW], RC[0:1, 0:EW], RE[0:1, 0:EW]),
                [B("RC"), B("RE")], [B("RB")])
            vop(lambda e, EW=EW: e.tensor_mul(RB[0:1, 0:EW], RB[0:1, 0:EW], RE[0:1, 0:EW]),
                [B("RB")], [B("RB")])
            P.op("scalar", lambda e, EW=EW: e.activation(
                out=RD[0:1, 0:EW], in_=RB[0:1, 0:EW], func=AF.Sqrt, bias=eps_sb[0:1, 0:1]),
                ins=[B("RB"), B("eps")], outs=[B("RD")])
            vop(lambda e, EW=EW: e.reciprocal(RD[0:1, 0:EW], RD[0:1, 0:EW]),
                [B("RD")], [B("RD")])
            vop(lambda e, c0=c0: e.tensor_copy(r2all[0:1, c0:c0 + W], RD[0:1, 0:W]),
                [B("RD")], [B("r2all")])
            P.op("tensor", lambda e: e.matmul(ps[4][:], ones_row[:], RE[0:1, 0:W],
                                              start=True, stop=True),
                 ins=[B("RE"), B("oner")], outs=[PS[4]])
            P.op("tensor", lambda e: e.matmul(ps[5][:], ones_row[:], RF[0:1, 0:W],
                                              start=True, stop=True),
                 ins=[B("RF"), B("oner")], outs=[PS[5]])
            P.op("tensor", lambda e: e.matmul(ps[6][:], ones_row[:], RD[0:1, 0:W],
                                              start=True, stop=True),
                 ins=[B("RD"), B("oner")], outs=[PS[6]])
            vop(lambda e: e.tensor_copy(r2b[:], ps[6][:]), [PS[6]], [B("r2b")])
            if g == 0:
                for i, (src, srcb) in enumerate([(RE, "RE"), (RF, "RF"), (RD, "RD")]):
                    P.op("tensor", lambda e, src=src: e.matmul(
                        ps[7][:, 0:3], ones_row[:], src[0:1, W:SW], start=True, stop=True),
                        ins=[B(srcb), B("oner")], outs=[PS[7]])
                    vop(lambda e, i=i: e.tensor_copy(tail_sb[:, 3 * i:3 * i + 3],
                                                     ps[7][:, 0:3]),
                        [PS[7]], [B("tail")])
            for k in range(KC):
                tbt, tbb = (tb0, B("g0")) if k % 2 == 0 else (tb1, B("g1"))
                vop(lambda e, k=k, tbt=tbt: e.tensor_mul(tbt[:], S1[:, sc(k, 0, W)], ps[4][:]),
                    [S1b[k], PS[4]], [tbb])
                vop(lambda e, k=k, tbt=tbt: e.tensor_add(S2[:, sc(k, 0, W)], tbt[:], ps[5][:]),
                    [tbb, PS[5]], [S2b[k]])
                dma("gpsimd", xn_sp[k * 128:(k + 1) * 128, c0:c0 + W], S2[:, sc(k, 0, W)],
                    [S2b[k]], [B(f"xnsp:{g}:{k}")], f"w2_{k}")
            if g == 0:
                for k in range(KC):
                    vop(lambda e, k=k: e.tensor_mul(t3a[:], xh_sb[:, k * 3:(k + 1) * 3],
                                                    tail_sb[:, 0:3]),
                        [B(f"xh{k}"), B("tail")], [B("t3a")])
                    vop(lambda e, k=k: e.tensor_add(xnh_sb[:, k * 3:(k + 1) * 3], t3a[:],
                                                    tail_sb[:, 3:6]),
                        [B("t3a"), B("tail")], [B(f"xnh{k}")])
                    P.op("vector", lambda e, k=k: e.scalar_tensor_tensor(
                        h3_sb[:, k * 3:(k + 1) * 3], xnh_sb[:, k * 3:(k + 1) * 3],
                        nw_c(k), tail_sb[:, 6:9], ALU.mult, ALU.mult),
                        ins=[B(f"xnh{k}"), B("vec"), B("tail")], outs=[B(f"h3{k}")])
                for mr in range(2):
                    for k in range(KC):
                        wi = next_ws()
                        dma("sync", WS[wi][:, 0:1024],
                            win_xc[k * 128:(k + 1) * 128, mr * 1024:(mr + 1) * 1024],
                            [], [WSb[wi]], f"ws_{wi}")
                        for m8 in range(8):
                            m = mr * 8 + m8
                            first = (mr == 0 and k == 0 and m8 == 0)
                            last = (mr == 1 and k == KC - 1 and m8 == 7)
                            P.op("tensor", lambda e, k=k, m8=m8, m=m, wi=wi, first=first,
                                 last=last: e.matmul(
                                     ps[0][:, m * 3:(m + 1) * 3],
                                     WS[wi][:, m8 * 128:(m8 + 1) * 128],
                                     h3_sb[:, k * 3:(k + 1) * 3],
                                     start=first, stop=last),
                                 ins=[WSb[wi], B(f"h3{k}")], outs=[PS[0]])
                for m in range(KC):
                    vop(lambda e, m=m: e.tensor_copy(xc_carry[:, m * 3:(m + 1) * 3],
                                                     ps[0][:, m * 3:(m + 1) * 3]),
                        [PS[0]], [B("xcc")])
            for mr in range(2):
                for k in range(KC):
                    emit_splits(1)
                    wi = next_ws()
                    dma("sync", WS[wi][:, 0:1024],
                        win_xc[k * 128:(k + 1) * 128, mr * 1024:(mr + 1) * 1024],
                        [], [WSb[wi]], f"ws_{wi}")
                    hbt, hbb = (hb0, B("g2")) if k % 2 == 0 else (hb1, B("g3"))
                    P.op("vector", lambda e, k=k, hbt=hbt: e.scalar_tensor_tensor(
                        hbt[:], S2[:, sc(k, 0, W)], nw_c(k), r2b[:], ALU.mult, ALU.mult),
                        ins=[S2b[k], B("vec"), B("r2b")], outs=[hbb])
                    for m8 in range(8):
                        P.op("tensor", lambda e, m8=m8, wi=wi, hbt=hbt, k=k: e.matmul(
                            ps[m8][:], WS[wi][:, m8 * 128:(m8 + 1) * 128], hbt[:],
                            start=(k == 0), stop=(k == KC - 1)),
                            ins=[WSb[wi], hbb], outs=[PS[m8]])
                for m8 in range(8):
                    m = mr * 8 + m8
                    vop(lambda e, m=m: e.tensor_copy(S1[:, sc(m, 0, 3)],
                                                     xc_carry[:, m * 3:(m + 1) * 3]),
                        [B("xcc")], [S1b[m]])
                    P.op("scalar", lambda e, m=m, m8=m8: e.activation(
                        out=S1[:, sc(m, 3, SW)], in_=ps[m8][:], func=AF.Copy),
                        ins=[PS[m8]], outs=[S1b[m]])
            for m in range(KC):
                vop(lambda e, m=m: e.tensor_copy(xc_carry[:, m * 3:(m + 1) * 3],
                                                 S1[:, sc(m, W, SW)]),
                    [S1b[m]], [B("xcc")])
            for mr in range(2):
                for tap in range(4):
                    for k in range(KC):
                        if k % 2 == 0:
                            emit_splits(1)
                        wi = next_ws()
                        dma("sync", WS[wi][:, 0:1024],
                            wconv[tap, k * 128:(k + 1) * 128, mr * 1024:(mr + 1) * 1024],
                            [], [WSb[wi]], f"ws_{wi}")
                        for m8 in range(8):
                            first = (tap == 0 and k == 0)
                            lastk = (tap == 3 and k == KC - 1)
                            P.op("tensor", lambda e, k=k, m8=m8, tap=tap, wi=wi, first=first,
                                 lastk=lastk: e.matmul(
                                     ps[m8][:], WS[wi][:, m8 * 128:(m8 + 1) * 128],
                                     S1[:, sc(k, tap, tap + W)],
                                     start=first, stop=lastk),
                                 ins=[WSb[wi], S1b[k]], outs=[PS[m8]])
                for m8 in range(8):
                    m = mr * 8 + m8
                    P.op("scalar", lambda e, m=m, m8=m8: e.activation(
                        out=S2[:, sc(m, 0, W)], in_=ps[m8][:], func=AF.Silu, bias=cb_c(m)),
                        ins=[PS[m8], B("vec")], outs=[S2b[m]])
                    dma("gpsimd", xcv_sp[m * 128:(m + 1) * 128, c0:c0 + W],
                        S2[:, sc(m, 0, W)], [S2b[m]], [B(f"xcvsp:{g}:{m}")], f"w2_{m}")
            for k in range(KC):
                P.op("tensor", lambda e, k=k: e.matmul(
                    ps[0][0:DTR, :], wxp_sb[:, k * 96:k * 96 + DTR], S2[:, sc(k, 0, W)],
                    start=(k == 0), stop=(k == KC - 1)),
                    ins=[S2b[k], B("wxp")], outs=[PS[0]])
                P.op("tensor", lambda e, k=k: e.matmul(
                    ps[1][0:32, :], wxp_sb[:, k * 96 + DTR:(k + 1) * 96], S2[:, sc(k, 0, W)],
                    start=(k == 0), stop=(k == KC - 1)),
                    ins=[S2b[k], B("wxp")], outs=[PS[1]])
            vop(lambda e: e.tensor_copy(dl_sb[:], ps[0][0:DTR, :]), [PS[0]], [B("dbc")])
            vop(lambda e: e.tensor_copy(bcq_sb[:], ps[1][0:32, :]), [PS[1]], [B("bcq")])
            dma("gpsimd", bc_sp[:, c0:c0 + W], bcq_sb[:], [B("bcq")], [B(f"bcsp:{g}")], "bcq")
            wdti = next_ws()
            dma("sync", WS[wdti][0:DTR, :], wdt_s, [], [WSb[wdti]], f"ws_{wdti}")
            for m in range(KC):
                pi = 2 + (m % 2)
                P.op("tensor", lambda e, m=m, pi=pi, wdti=wdti: e.matmul(
                    ps[pi][:], WS[wdti][0:DTR, m * 128:(m + 1) * 128], dl_sb[0:DTR, :],
                    start=True, stop=True),
                    ins=[B("dbc"), WSb[wdti]], outs=[PS[pi]])
                P.op("scalar", lambda e, m=m, pi=pi: e.activation(
                    out=e1b[:], in_=ps[pi][:], func=AF.Exp, bias=db_c(m)),
                    ins=[PS[pi], B("vec")], outs=[B("g11")])
                P.op("vector", lambda e, m=m, pi=pi: e.tensor_scalar_add(
                    rbb[:], ps[pi][:], db_c(m)),
                    ins=[PS[pi], B("vec")], outs=[B("g13")])
                P.op("scalar", lambda e: e.activation(out=spb[:], in_=e1b[:], func=AF.Ln,
                                                      bias=1.0),
                     ins=[B("g11")], outs=[B("g12")])
                vop(lambda e: e.tensor_single_scalar(mskb[:], rbb[:], 20.0, ALU.is_le),
                    [B("g13")], [B("mskb")])
                vop(lambda e, m=m: e.tensor_copy(S3[:, sc(m, 0, W)], rbb[:]),
                    [B("g13")], [S3b[m]])
                vop(lambda e, m=m: e.copy_predicated(S3[:, sc(m, 0, W)], mskb[:], spb[:]),
                    [B("mskb"), B("g12")], [S3b[m]])
                dma("gpsimd", delta_sp[m * 128:(m + 1) * 128, c0:c0 + W], S3[:, sc(m, 0, W)],
                    [S3b[m]], [B(f"dlsp:{g}:{m}")], f"w3_{m}")
            for n in range(NST):
                P.op("tensor", lambda e, n=n: e.matmul(
                    ps[7][:], ident[0:32, n:n + 1].broadcast_to([32, 128]), bcq_sb[:],
                    start=True, stop=True),
                    ins=[B("bcq"), B("ident")], outs=[PS[7]])
                vop(lambda e, n=n: e.tensor_copy(S1[:, sc(n, 0, W)], ps[7][:]),
                    [PS[7]], [S1b[n]])
            for hc in range(KC):
                vop(lambda e, hc=hc: e.tensor_mul(ub[:], S3[:, sc(hc, 0, W)],
                                                  S2[:, sc(hc, 0, W)]),
                    [S3b[hc], S2b[hc]], [B("g4")])
                for n in range(NST):
                    idx = hc * NST + n
                    dAt, dAb = (dA0, B("g5")) if n % 2 == 0 else (dA1, B("g6"))
                    bxt, bxb = (bx0, B("g7")) if n % 2 == 0 else (bx1, B("g8"))
                    hst, hsb = (hs0, B("g9")) if n % 2 == 0 else (hs1, B("g10"))
                    P.op("scalar", lambda e, hc=hc, dAt=dAt, idx=idx: e.activation(
                        out=dAt[:], in_=S3[:, sc(hc, 0, W)], func=AF.Exp,
                        scale=a_sb[:, idx:idx + 1]),
                        ins=[S3b[hc], B("A")], outs=[dAb])
                    vop(lambda e, n=n, bxt=bxt: e.tensor_mul(
                        bxt[:], ub[:], S1[:, sc(n, 0, W)]),
                        [B("g4"), S1b[n]], [bxb])
                    P.op("vector", lambda e, dAt=dAt, bxt=bxt, hst=hst, idx=idx:
                         e.tensor_tensor_scan(hst[:], dAt[:], bxt[:],
                                              state1[:, idx:idx + 1], ALU.mult, ALU.add),
                         ins=[dAb, bxb, ST1[idx]], outs=[hsb])
                    vop(lambda e, hst=hst, idx=idx: e.tensor_copy(
                        state1[:, idx:idx + 1], hst[:, W - 1:W]),
                        [hsb], [ST1[idx]])

        emit_splits(10 ** 6)
        vop(lambda e: e.tensor_scalar_mul(ccb_sb[:], state1[:], msend[:]),
            ST1 + [B("msend")], [B("ccb")])
        dma("gpsimd", cc_in[:, :], ccb_sb[:], [B("ccb")], [B("ccin")], "ccb")
        P.op("gpsimd", lambda e: e.collective_compute(
            "AllReduce", ALU.add,
            replica_groups=[[0, 1], [2, 3], [4, 5], [6, 7]],
            ins=[cc_in.ap().opt()], outs=[cc_out.ap().opt()]),
            ins=[B("ccin")], outs=[B("ccout")], sem="s_cc", amt=1)
        dma("gpsimd", ccb_sb[:], cc_out[:, :], [B("ccout")], [B("ccb")], "ccb")
        vop(lambda e: e.tensor_scalar_mul(state2[:], ccb_sb[:], mrecv[:]),
            [B("ccb"), B("mrecv")], ST2)

        S2v = S2.ap().bitcast(BF16)
        S3v = S3.ap().bitcast(BF16)
        xnh = lambda k: S3v[:, 2 * k * SW:2 * k * SW + W]
        xnl = lambda k: S3v[:, 2 * k * SW + W:2 * k * SW + 2 * W]
        hah = lambda ic: S3v[:, 2 * ic * SW:2 * ic * SW + W]
        hal = lambda ic: S3v[:, 2 * ic * SW + W:2 * ic * SW + 2 * W]
        G12v = G[12].ap().bitcast(BF16)
        for g in range(NG):
            c0 = g * W
            for k in range(KC):
                xnf, xnfb, xsl = (G[0], B("g0"), "g0") if k % 2 == 0 else (G[1], B("g1"), "g1")
                dma("scalar", xnf[:], xn_sp[k * 128:(k + 1) * 128, c0:c0 + W],
                    [B(f"xnsp:{g}:{k}")], [xnfb], xsl)
                P.op("tensor", lambda e, k=k, xnf=xnf: e.matmul(
                    ps[0][0:8, :], wr_sb[:, k * 8:(k + 1) * 8], xnf[:],
                    start=(k == 0), stop=(k == KC - 1)),
                    ins=[xnfb, B("wr")], outs=[PS[0]])
                vop(lambda e, k=k, xnf=xnf: e.tensor_copy(xnh(k), xnf[:]),
                    [xnfb], [S3b[k]])
                vop(lambda e, k=k, xnf=xnf: e.tensor_tensor(xnl(k), xnf[:], xnh(k),
                                                            ALU.subtract),
                    [xnfb, S3b[k]], [S3b[k]])
            vop(lambda e: e.tensor_copy(lgr_sb[:], ps[0][0:8, :]), [PS[0]], [B("lgr")])
            for s in range(4):
                P.op("tensor", lambda e, s=s: e.transpose(
                    ps[1][0:128, 0:8], lgr_sb[0:8, s * 128:(s + 1) * 128], ident[0:8, 0:8]),
                    ins=[B("lgr"), B("ident")], outs=[PS[1]])
                vop(lambda e, s=s: e.tensor_copy(lgt_sb[:, s * 8:(s + 1) * 8],
                                                 ps[1][:, 0:8]),
                    [PS[1]], [B("lgt")])
                P.op("vector", lambda e, s=s: e.max(out=mx_sb[:, s * 8:(s + 1) * 8],
                                                    in_=lgt_sb[:, s * 8:(s + 1) * 8]),
                     ins=[B("lgt")], outs=[B("mx")])
                vop(lambda e, s=s: e.tensor_scalar_mul(ngm_sb[:, s:s + 1],
                                                       mx_sb[:, s * 8:s * 8 + 1], -1.0),
                    [B("mx")], [B("ngm")])
                P.op("scalar", lambda e, s=s: e.activation(
                    out=ex_sb[:, s * 8:(s + 1) * 8], in_=lgt_sb[:, s * 8:(s + 1) * 8],
                    func=AF.Exp, bias=ngm_sb[:, s:s + 1]),
                    ins=[B("lgt"), B("ngm")], outs=[B("ex")])
                P.op("scalar", lambda e, s=s: e.activation(
                    out=den_sb[:, s:s + 1], in_=mx_sb[:, s * 8 + 1:s * 8 + 2],
                    func=AF.Exp, bias=ngm_sb[:, s:s + 1]),
                    ins=[B("mx"), B("ngm")], outs=[B("den")])
                P.op("scalar", lambda e, s=s: e.add(den1_sb[:, s:s + 1],
                                                    den_sb[:, s:s + 1], 1.0),
                     ins=[B("den")], outs=[B("den1")])
                vop(lambda e, s=s: e.reciprocal(rdn_sb[:, s:s + 1], den1_sb[:, s:s + 1]),
                    [B("den1")], [B("rdn")])
                vop(lambda e, s=s: e.tensor_scalar_mul(exs_sb[:, s * 8:(s + 1) * 8],
                                                       ex_sb[:, s * 8:(s + 1) * 8],
                                                       rdn_sb[:, s:s + 1]),
                    [B("ex"), B("rdn")], [B("exs")])
                vop(lambda e, s=s: e.tensor_single_scalar(
                    msk8_sb[:, s * 8:(s + 1) * 8], lgt_sb[:, s * 8:(s + 1) * 8],
                    mx_sb[:, s * 8 + 2:s * 8 + 3], ALU.is_le),
                    [B("lgt"), B("mx")], [B("msk8")])
                vop(lambda e, s=s: e.tensor_copy(comb_sb[:, s * 8:(s + 1) * 8],
                                                 exs_sb[:, s * 8:(s + 1) * 8]),
                    [B("exs")], [B("comb")])
                vop(lambda e, s=s: e.copy_predicated(comb_sb[:, s * 8:(s + 1) * 8],
                                                     msk8_sb[:, s * 8:(s + 1) * 8],
                                                     zero8[:]),
                    [B("msk8"), B("zero8")], [B("comb")])
                P.op("tensor", lambda e, s=s: e.transpose(
                    ps[1][0:8, 0:128], comb_sb[:, s * 8:(s + 1) * 8], ident[:]),
                    ins=[B("comb"), B("ident")], outs=[PS[1]])
                vop(lambda e, s=s: e.tensor_copy(crow_sb[0:8, s * 128:(s + 1) * 128],
                                                 ps[1][0:8, 0:128]),
                    [PS[1]], [B("crow")])
            dma("scalar", bcr_sb[:], bc_sp[:, c0:c0 + W], [B(f"bcsp:{g}")], [B("bcr")], "bcr")
            for n in range(NST):
                P.op("tensor", lambda e, n=n: e.matmul(
                    ps[7][:], ident[0:32, n:n + 1].broadcast_to([32, 128]), bcr_sb[:],
                    start=True, stop=True),
                    ins=[B("bcr"), B("ident")], outs=[PS[7]])
                vop(lambda e, n=n: e.tensor_copy(S1[:, sc(n, 0, W)], ps[7][:]),
                    [PS[7]], [S1b[n]])
                P.op("tensor", lambda e, n=n: e.matmul(
                    ps[7][:], ident[0:32, 16 + n:17 + n].broadcast_to([32, 128]), bcr_sb[:],
                    start=True, stop=True),
                    ins=[B("bcr"), B("ident")], outs=[PS[7]])
                vop(lambda e, n=n: e.tensor_copy(S2[:, sc(n, 0, W)], ps[7][:]),
                    [PS[7]], [S2b[n]])
            def emit_2a_hc(hc):
                dcht, dchb = (dch0, B("g0")) if hc % 2 == 0 else (dch1, B("g1"))
                xvht, xvhb = (xvh0, B("g2")) if hc % 2 == 0 else (xvh1, B("g3"))
                ya, yab, ysl = (G[12], B("g12"), "g12") if hc % 2 == 0 else (G[13], B("g13"), "g13")
                dma("scalar", dcht[:], delta_sp[hc * 128:(hc + 1) * 128, c0:c0 + W],
                    [B(f"dlsp:{g}:{hc}")], [dchb], "g0" if hc % 2 == 0 else "g1")
                dma("scalar", xvht[:], xcv_sp[hc * 128:(hc + 1) * 128, c0:c0 + W],
                    [B(f"xcvsp:{g}:{hc}")], [xvhb], "g2" if hc % 2 == 0 else "g3")
                vop(lambda e, dcht=dcht, xvht=xvht: e.tensor_mul(ub[:], dcht[:], xvht[:]),
                    [dchb, xvhb], [B("g4")])
                vop(lambda e, ya=ya: e.memset(ya[:], 0.0), [], [yab])
                for n in range(NST):
                    idx = hc * NST + n
                    dAt, dAb = (dA0, B("g5")) if n % 2 == 0 else (dA1, B("g6"))
                    bxt, bxb = (bx0, B("g7")) if n % 2 == 0 else (bx1, B("g8"))
                    hst, hsb = (hs0, B("g9")) if n % 2 == 0 else (hs1, B("g10"))
                    P.op("scalar", lambda e, dAt=dAt, dcht=dcht, idx=idx: e.activation(
                        out=dAt[:], in_=dcht[:], func=AF.Exp, scale=a_sb[:, idx:idx + 1]),
                        ins=[dchb, B("A")], outs=[dAb])
                    vop(lambda e, n=n, bxt=bxt: e.tensor_mul(bxt[:], ub[:],
                                                             S1[:, sc(n, 0, W)]),
                        [B("g4"), S1b[n]], [bxb])
                    P.op("vector", lambda e, dAt=dAt, bxt=bxt, hst=hst, idx=idx:
                         e.tensor_tensor_scan(hst[:], dAt[:], bxt[:],
                                              state2[:, idx:idx + 1], ALU.mult, ALU.add),
                         ins=[dAb, bxb, ST2[idx]], outs=[hsb])
                    vop(lambda e, hst=hst, idx=idx: e.tensor_copy(
                        state2[:, idx:idx + 1], hst[:, W - 1:W]),
                        [hsb], [ST2[idx]])
                    vop(lambda e, hst=hst, n=n: e.tensor_mul(ym0[:], hst[:],
                                                             S2[:, sc(n, 0, W)]),
                        [hsb, S2b[n]], [B("g11")])
                    vop(lambda e, ya=ya: e.tensor_add(ya[:], ya[:], ym0[:]),
                        [B("g11"), yab], [yab])
                P.op("vector", lambda e, ya=ya, hc=hc, xvht=xvht: e.scalar_tensor_tensor(
                    ya[:], xvht[:], dd_c(hc), ya[:], ALU.mult, ALU.add),
                    ins=[xvhb, B("vec"), yab], outs=[yab])
                dma("scalar", y_sp[hc * 128:(hc + 1) * 128, c0:c0 + W], ya[:],
                    [yab], [B(f"ysp:{g}:{hc}")], ysl)

            def emit_ffn_unit(ex, ic):
                if ic == 0:
                    P.op("tensor", lambda e, ex=ex: e.matmul(
                        ps[2][:], ident[0:8, ex:ex + 1].broadcast_to([8, 128]), crow_sb[:],
                        start=True, stop=True),
                        ins=[B("crow"), B("ident")], outs=[PS[2]])
                    vop(lambda e: e.tensor_copy(r2b[:], ps[2][:]), [PS[2]], [B("r2b")])
                wgi = next_ws()
                wgv = WS[wgi].ap().bitcast(BF16)
                dma("sync", wgv[:, 0:2 * KC * 128], wg_b[ex, ic],
                    [B(wbkey(wg_b[ex, ic]))], [WSb[wgi]], f"ws_{wgi}")
                wui = next_ws()
                wuv = WS[wui].ap().bitcast(BF16)
                dma("sync", wuv[:, 0:2 * KC * 128], wu_b[ex, ic],
                    [B(wbkey(wu_b[ex, ic]))], [WSb[wui]], f"ws_{wui}")
                HL = KC * 128
                passes = [(0, xnh), (0, xnl), (HL, xnh)]
                for pidx, (woff, xf) in enumerate(passes):
                    for k in range(KC):
                        first = (pidx == 0 and k == 0)
                        last = (pidx == 2 and k == KC - 1)
                        P.op("tensor", lambda e, k=k, wgv=wgv, woff=woff, xf=xf,
                             first=first, last=last: e.matmul(
                                 ps[0][:],
                                 wgv[:, woff + k * 128:woff + (k + 1) * 128],
                                 xf(k), start=first, stop=last),
                             ins=[WSb[wgi], S3b[k]], outs=[PS[0]], inc=last)
                for pidx, (woff, xf) in enumerate(passes):
                    for k in range(KC):
                        first = (pidx == 0 and k == 0)
                        last = (pidx == 2 and k == KC - 1)
                        P.op("tensor", lambda e, k=k, wuv=wuv, woff=woff, xf=xf,
                             first=first, last=last: e.matmul(
                                 ps[1][:],
                                 wuv[:, woff + k * 128:woff + (k + 1) * 128],
                                 xf(k), start=first, stop=last),
                             ins=[WSb[wui], S3b[k]], outs=[PS[1]], inc=last)
                P.op("scalar", lambda e: e.activation(out=G[13][:], in_=ps[0][:],
                                                      func=AF.Silu),
                     ins=[PS[0]], outs=[B("g13")])
                vop(lambda e: e.tensor_mul(G[12][:], G[13][:], ps[1][:]),
                    [B("g13"), PS[1]], [B("g12")])
                vop(lambda e: e.tensor_mul(G[13][:], G[12][:], r2b[:]),
                    [B("g12"), B("r2b")], [B("g13")])
                vop(lambda e: e.tensor_copy(G12v[:, 0:W], G[13][:]),
                    [B("g13")], [B("g12")])
                vop(lambda e: e.tensor_tensor(G12v[:, W:2 * W], G[13][:],
                                              G12v[:, 0:W], ALU.subtract),
                    [B("g13")], [B("g12")])
                dma("scalar", ha_sp[ex, ic], G12v[:, 0:2 * W],
                    [B("g12")], [B(f"ha:{ex}:{ic}")], "g12")

            units = [(ex, ic) for ex in range(NE) for ic in range(NIC)]
            ui = 0
            for hc in range(KC):
                emit_2a_hc(hc)
                tgt = (hc + 1) * len(units) // KC
                while ui < tgt:
                    emit_ffn_unit(*units[ui])
                    ui += 1

            for m in range(KC):
                vop(lambda e, m=m: e.memset(S2[:, sc(m, 0, W)], 0.0), [], [S2b[m]])
            for ex in range(NE):
                for ic in range(NIC):
                    dma("scalar", S3v[:, 2 * ic * SW:2 * ic * SW + 2 * W],
                        ha_sp[ex, ic], [B(f"ha:{ex}:{ic}")], [S3b[ic]], f"s3_{ic}")
                for m in range(KC):
                    wdi = next_ws()
                    wdv = WS[wdi].ap().bitcast(BF16)
                    dma("sync", wdv[:, 0:2 * NIC * 128], wd_b[ex, m],
                        [B(wbkey(wd_b[ex, m]))], [WSb[wdi]], f"ws_{wdi}")
                    pi = 3 + (m % 2)
                    DL = NIC * 128
                    for pidx, (woff, hf) in enumerate([(0, hah), (0, hal), (DL, hah)]):
                        for ic in range(NIC):
                            first = (pidx == 0 and ic == 0)
                            last = (pidx == 2 and ic == NIC - 1)
                            P.op("tensor", lambda e, ic=ic, wdv=wdv, pi=pi, woff=woff,
                                 hf=hf, first=first, last=last: e.matmul(
                                     ps[pi][:],
                                     wdv[:, woff + ic * 128:woff + (ic + 1) * 128],
                                     hf(ic), start=first, stop=last),
                                 ins=[WSb[wdi], S3b[ic]], outs=[PS[pi]], inc=last)
                    vop(lambda e, m=m, pi=pi: e.tensor_add(
                        S2[:, sc(m, 0, W)], S2[:, sc(m, 0, W)], ps[pi][:]),
                        [PS[pi], S2b[m]], [S2b[m]])
            P.op("tensor", lambda e, c0=c0: e.matmul(
                ps[6][:], ones_row[:], r2all[0:1, c0:c0 + W], start=True, stop=True),
                ins=[B("r2all"), B("oner")], outs=[PS[6]])
            vop(lambda e: e.tensor_copy(r2b[:], ps[6][:]), [PS[6]], [B("r2b")])
            for mr in range(2):
                for k in range(KC):
                    xnf, xnfb, xsl = (G[0], B("g0"), "g0") if k % 2 == 0 else (G[1], B("g1"), "g1")
                    dma("scalar", xnf[:], xn_sp[k * 128:(k + 1) * 128, c0:c0 + W],
                        [B(f"xnsp:{g}:{k}")], [xnfb], xsl)
                    wi = next_ws()
                    dma("sync", WS[wi][:, 0:1024],
                        win_z[k * 128:(k + 1) * 128, mr * 1024:(mr + 1) * 1024],
                        [], [WSb[wi]], f"ws_{wi}")
                    hbt, hbb = (hb0, B("g2")) if k % 2 == 0 else (hb1, B("g3"))
                    P.op("vector", lambda e, k=k, hbt=hbt, xnf=xnf: e.scalar_tensor_tensor(
                        hbt[:], xnf[:], nw_c(k), r2b[:], ALU.mult, ALU.mult),
                        ins=[xnfb, B("vec"), B("r2b")], outs=[hbb])
                    for m8 in range(8):
                        P.op("tensor", lambda e, m8=m8, wi=wi, hbt=hbt, k=k: e.matmul(
                            ps[m8][:], WS[wi][:, m8 * 128:(m8 + 1) * 128], hbt[:],
                            start=(k == 0), stop=(k == KC - 1)),
                            ins=[WSb[wi], hbb], outs=[PS[m8]], inc=(m8 == 7))
                for m8 in range(8):
                    m = mr * 8 + m8
                    yld, yldb, ysl2 = (G[11], B("g11"), "g11") if m8 % 2 == 0 else (G[4], B("g4"), "g4")
                    dma("scalar", yld[:], y_sp[m * 128:(m + 1) * 128, c0:c0 + W],
                        [B(f"ysp:{g}:{m}")], [yldb], ysl2)
                    P.op("scalar", lambda e, m8=m8: e.activation(
                        out=zsb[:], in_=ps[m8][:], func=AF.Silu),
                        ins=[PS[m8]], outs=[B("g0")])
                    vop(lambda e, m=m, yld=yld: e.tensor_mul(S3[:, sc(m, 0, W)],
                                                             yld[:], zsb[:]),
                        [yldb, B("g0")], [S3b[m]])
            for mr in range(2):
                for k in range(KC):
                    wi = next_ws()
                    dma("sync", WS[wi][:, 0:1024],
                        wout[k * 128:(k + 1) * 128, mr * 1024:(mr + 1) * 1024],
                        [], [WSb[wi]], f"ws_{wi}")
                    for m8 in range(8):
                        P.op("tensor", lambda e, m8=m8, wi=wi, k=k: e.matmul(
                            ps[m8][:], WS[wi][:, m8 * 128:(m8 + 1) * 128],
                            S3[:, sc(k, 0, W)],
                            start=(k == 0), stop=(k == KC - 1)),
                            ins=[WSb[wi], S3b[k]], outs=[PS[m8]], inc=(m8 == 7))
                for m8 in range(8):
                    m = mr * 8 + m8
                    xnf, xnfb, xsl = (G[0], B("g0"), "g0") if m8 % 2 == 0 else (G[1], B("g1"), "g1")
                    dma("scalar", xnf[:], xn_sp[m * 128:(m + 1) * 128, c0:c0 + W],
                        [B(f"xnsp:{g}:{m}")], [xnfb], xsl)
                    xgt, xgb = (xg0, B("g7")) if m8 % 2 == 0 else (xg1, B("g8"))
                    tot, tob = (to0, B("g5")) if m8 % 2 == 0 else (to1, B("g6"))
                    dma("scalar", xgt[:], x_fm[m * 128:(m + 1) * 128, 3 + c0:3 + c0 + W],
                        [], [xgb], "g7" if m8 % 2 == 0 else "g8")
                    vop(lambda e, m8=m8, m=m, tot=tot: e.tensor_add(
                        tot[:], ps[m8][:], S2[:, sc(m, 0, W)]),
                        [PS[m8], S2b[m]], [tob])
                    vop(lambda e, m=m, tot=tot, xnf=xnf: e.tensor_add(
                        tot[:], tot[:], xnf[:]),
                        [tob, xnfb], [tob])
                    vop(lambda e, m=m, tot=tot, xgt=xgt: e.tensor_add(
                        S1[:, sc(m, 0, W)], tot[:], xgt[:]),
                        [tob, xgb, S1b[m]], [S1b[m]])
            for m in range(KC):
                dma("scalar", out_fm[m * 128:(m + 1) * 128, c0:c0 + W], S1[:, sc(m, 0, W)],
                    [S1b[m]], [B(f"out:{g}:{m}")], f"s1_{m}")

        for m in range(KC):
            v = P.ctr.get(f"qd_s1_{m}", 0)
            P.ops["sync"].append(
                lambda e, v=v, m=m: e.wait_ge(P.sems[f"qd_s1_{m}"], v))

        @block.sync
        def _(eng):
            for cl in P.ops["sync"]:
                cl(eng)

        @block.tensor
        def _(eng):
            for cl in P.ops["tensor"]:
                cl(eng)

        @block.vector
        def _(eng):
            for cl in P.ops["vector"]:
                cl(eng)

        @block.scalar
        def _(eng):
            for cl in P.ops["scalar"]:
                cl(eng)

        @block.gpsimd
        def _(eng):
            for cl in P.ops["gpsimd"]:
                cl(eng)

    return nc


def _stage(inputs):
    f32 = np.float32
    x = np.asarray(inputs["L0_attn_x"], f32)
    inp = np.asarray(inputs["L0_attn_in_proj_w"], f32)
    conv = np.ascontiguousarray(np.asarray(inputs["L0_attn_conv_w"], f32))
    xp = np.asarray(inputs["L0_attn_x_proj_w"], f32)
    dt = np.asarray(inputs["L0_attn_dt_proj_w"], f32)
    outp = np.asarray(inputs["L0_attn_out_proj_w"], f32)
    wr = np.asarray(inputs["L0_ffn_Wr"], f32)
    wg = np.asarray(inputs["L0_ffn_Wg"], f32)
    wu = np.asarray(inputs["L0_ffn_Wu"], f32)
    wd = np.asarray(inputs["L0_ffn_Wd"], f32)

    win_xc = np.ascontiguousarray(inp[:H, :].T)
    win_z = np.ascontiguousarray(inp[H:, :].T)
    wxp_s = np.ascontiguousarray(
        xp.T.reshape(KC, 128, 96).transpose(1, 0, 2).reshape(128, KC * 96))
    wdt_s = np.ascontiguousarray(dt.T)
    wout = np.ascontiguousarray(outp.T)
    wr_s = np.ascontiguousarray(
        wr.reshape(KC, 128, 8).transpose(1, 0, 2).reshape(128, KC * 8))
    wg_s = np.ascontiguousarray(
        wg.reshape(NE, KC, 128, NIC, 128).transpose(0, 3, 2, 1, 4)
        .reshape(NE, NIC, 128, KC * 128))
    wu_s = np.ascontiguousarray(
        wu.reshape(NE, KC, 128, NIC, 128).transpose(0, 3, 2, 1, 4)
        .reshape(NE, NIC, 128, KC * 128))
    wd_s = np.ascontiguousarray(
        wd.reshape(NE, NIC, 128, KC, 128).transpose(0, 3, 2, 1, 4)
        .reshape(NE, KC, 128, NIC * 128))
    cm = lambda v: np.ascontiguousarray(np.asarray(v, f32).reshape(KC, 128).T)
    vec4 = np.ascontiguousarray(np.concatenate([
        cm(inputs["L0_attn_norm_w"]), cm(inputs["L0_attn_conv_b"]),
        cm(inputs["L0_attn_dt_proj_b"]), cm(inputs["L0_attn_D"])], axis=1))
    alog_s = np.ascontiguousarray(
        np.asarray(inputs["L0_attn_A_log"], f32).reshape(KC, 128, NST)
        .transpose(1, 0, 2).reshape(128, KC * NST))
    ones_col = np.ones((128, 1), f32)
    ones_row = np.ones((1, 128), f32)
    ident = np.eye(128, dtype=f32)

    shared = dict(win_xc=win_xc, win_z=win_z, wconv=conv, wxp_s=wxp_s, wdt_s=wdt_s,
                  wout=wout, wr_s=wr_s, wg_s=wg_s, wu_s=wu_s, wd_s=wd_s, vec4=vec4,
                  alog_s=alog_s, ones_col=ones_col, ones_row=ones_row, ident=ident)
    in_maps = []
    for c in range(8):
        b, j = c // 2, c % 2
        xb = x[b]
        halo = np.zeros((3, H), f32) if j == 0 else xb[2045:2048]
        x_fm = np.ascontiguousarray(
            np.concatenate([halo, xb[j * 2048:(j + 1) * 2048]], 0).T)
        im = dict(shared)
        im.update(x_fm=x_fm,
                  msend=np.full((128, 1), 1.0 if j == 0 else 0.0, f32),
                  mrecv=np.full((128, 1), float(j), f32))
        in_maps.append(im)
    return in_maps


def kernel(**inputs):
    in_maps = _stage(inputs)
    nc = build()
    res = run_bass_kernel_spmd(nc, in_maps, core_ids=list(range(8)))
    out = np.empty((4, 4096, 2048), np.float32)
    for c in range(8):
        b, j = c // 2, c % 2
        out[b, j * 2048:(j + 1) * 2048, :] = res.results[c]["out_fm"].T
    return out


# version 20
# speedup vs baseline: 1.1314x; 1.1219x over previous
import numpy as np
import concourse.bass as bass
import concourse.mybir as mybir
from concourse.bass_utils import run_bass_kernel_spmd
from contextlib import ExitStack

F32 = mybir.dt.float32
U8 = mybir.dt.uint8
AF = mybir.ActivationFunctionType
ALU = mybir.AluOpType

H = 2048
KC = 16
W = 512
NG = 4
SW = 515
DTR = 64
NST = 16
IFF = 1408
NIC = 11
NE = 8
EPS = 1e-5
INVH = 1.0 / 2048.0


class Buf:
    __slots__ = ("name", "w", "r")

    def __init__(self, name):
        self.name = name
        self.w = None
        self.r = []


class Plan:
    ENGINES = ("tensor", "vector", "scalar", "sync", "gpsimd")

    def __init__(self):
        self.ops = {e: [] for e in self.ENGINES}
        self.ctr = {}
        self.seen = {e: {} for e in self.ENGINES}
        self.sems = {}
        self._bufs = {}
        self._pending = {e: [] for e in self.ENGINES}

    def buf(self, name):
        b = self._bufs.get(name)
        if b is None:
            b = self._bufs[name] = Buf(name)
        return b

    def _waits(self, eng, deps):
        need = {}
        for d in deps:
            if d is None:
                continue
            s, c = d
            if eng == "tensor" and s == "s_tensor":
                continue
            if need.get(s, 0) < c:
                need[s] = c
        out = []
        seen = self.seen[eng]
        for s, c in need.items():
            if seen.get(s, 0) < c:
                seen[s] = c
                out.append((s, c))
        return out

    def op(self, eng, fn, ins=(), outs=(), sem=None, amt=1, inc=True):
        deps = []
        for b in ins:
            deps.append(b.w)
        for b in outs:
            deps.append(b.w)
            deps.extend(b.r)
        waits = self._waits(eng, deps)
        plan = self
        if not inc:
            def closure0(e, waits=waits, fn=fn):
                for (s, c) in waits:
                    e.wait_ge(plan.sems[s], c)
                fn(e)
            self.ops[eng].append(closure0)
            self._pending[eng].append((list(ins), list(outs)))
            return None
        sem = sem or ("s_" + eng)
        self.ctr[sem] = self.ctr.get(sem, 0) + amt
        cnt = self.ctr[sem]

        def closure(e, waits=waits, fn=fn, sem=sem, amt=amt):
            for (s, c) in waits:
                e.wait_ge(plan.sems[s], c)
            fn(e).then_inc(plan.sems[sem], amt)

        self.ops[eng].append(closure)
        tok = (sem, cnt)
        for pins, pouts in self._pending[eng]:
            for b in pouts:
                b.w, b.r = tok, []
            for b in pins:
                b.r.append(tok)
        self._pending[eng] = []
        for b in outs:
            b.w, b.r = tok, []
        for b in ins:
            b.r.append(tok)
        return tok


def build():
    nc = bass.Bass("TRN2", target_bir_lowering=False, debug=False)
    P = Plan()

    di = lambda n, s: nc.dram_tensor(n, s, F32, kind="ExternalInput").ap()
    x_fm = di("x_fm", [H, 3 + 2048])
    win_xc = di("win_xc", [H, H])
    win_z = di("win_z", [H, H])
    wconv = di("wconv", [4, H, H])
    wxp_s = di("wxp_s", [128, KC * 96])
    wdt_s = di("wdt_s", [DTR, H])
    wout = di("wout", [H, H])
    wr_s = di("wr_s", [128, KC * 8])
    wg_s = di("wg_s", [NE, NIC, 128, KC * 128])
    wu_s = di("wu_s", [NE, NIC, 128, KC * 128])
    wd_s = di("wd_s", [NE, KC, 128, NIC * 128])
    vec4 = di("vec4", [128, 4 * KC])
    alog_s = di("alog_s", [128, KC * NST])
    ones_col_d = di("ones_col", [128, 1])
    ones_row_d = di("ones_row", [1, 128])
    ident_d = di("ident", [128, 128])
    msend_d = di("msend", [128, 1])
    mrecv_d = di("mrecv", [128, 1])
    out_fm = nc.dram_tensor("out_fm", [H, 2048], F32, kind="ExternalOutput").ap()

    ds = lambda n, s: nc.dram_tensor(n, s, F32).ap()
    xn_sp = ds("xn_sp", [H, 2048])
    delta_sp = ds("delta_sp", [H, 2048])
    xcv_sp = ds("xcv_sp", [H, 2048])
    bc_sp = ds("bc_sp", [32, 2048])
    cc_in = nc.dram_tensor("cc_in", [128, KC * NST], F32)
    cc_out = nc.dram_tensor("cc_out", [128, KC * NST], F32)
    BF16 = mybir.dt.bfloat16
    wg_b = nc.dram_tensor("wg_b", [NE, NIC, 128, 2 * KC * 128], BF16).ap()
    wu_b = nc.dram_tensor("wu_b", [NE, NIC, 128, 2 * KC * 128], BF16).ap()
    wd_b = nc.dram_tensor("wd_b", [NE, KC, 128, 2 * NIC * 128], BF16).ap()
    y_sp = nc.dram_tensor("y_sp", [H, 2048], F32).ap()
    ha_sp = nc.dram_tensor("ha_sp", [NE, NIC, 128, 2 * W], BF16).ap()

    with ExitStack() as stack:
        e_ = stack.enter_context
        sb = lambda n, s, dt=F32: e_(nc.sbuf_tensor(n, s, dt))
        S1 = sb("S1", [128, KC * SW])
        S2 = sb("S2", [128, KC * SW])
        S3 = sb("S3", [128, KC * SW])
        WS = [sb(f"ws{i}", [128, 2048]) for i in range(4)]
        vec_sb = sb("vec_sb", [128, 4 * KC])
        alog_sb = sb("alog_sb", [128, KC * NST])
        a_sb = sb("a_sb", [128, KC * NST])
        wxp_sb = sb("wxp_sb", [128, KC * 96])
        wr_sb = sb("wr_sb", [128, KC * 8])
        ones_col = sb("ones_col_s", [128, 1])
        ones_row = sb("ones_row_s", [1, 128])
        ident = sb("ident_s", [128, 128])
        msend = sb("msend_s", [128, 1])
        mrecv = sb("mrecv_s", [128, 1])
        state1 = sb("state1", [128, KC * NST])
        state2 = sb("state2", [128, KC * NST])
        r2all = sb("r2all", [1, 2048])
        xc_carry = sb("xc_carry", [128, KC * 3])
        xh_sb = sb("xh_sb", [128, KC * 3])
        xnh_sb = sb("xnh_sb", [128, KC * 3])
        h3_sb = sb("h3_sb", [128, KC * 3])
        tail_sb = sb("tail_sb", [128, 12])
        t3a = sb("t3a", [128, 3])
        eps_sb = sb("eps_sb", [128, 1])
        RA = sb("RA", [1, SW]); RB = sb("RB", [1, SW]); RC = sb("RC", [1, SW])
        RD = sb("RD", [1, SW]); RE = sb("RE", [1, SW]); RF = sb("RF", [1, SW])
        G = [sb(f"g{i}", [128, W]) for i in range(14)]
        tb0, tb1 = G[0], G[1]
        hb0, hb1 = G[2], G[3]
        ub = G[4]
        dA0, dA1 = G[5], G[6]
        bx0, bx1 = G[7], G[8]
        hs0, hs1 = G[9], G[10]
        e1b, spb, rbb = G[11], G[12], G[13]
        dch0, dch1 = G[0], G[1]
        xvh0, xvh1 = G[2], G[3]
        ym0, ym1 = G[11], G[12]
        zsb = G[0]
        to0, to1 = G[5], G[6]
        xg0, xg1 = G[7], G[8]
        gsb = G[0]
        cbb = G[1]
        hu_t = G[2]
        mskb = sb("mskb", [128, W], U8)
        r2b = sb("r2b", [128, W])
        dl_sb = sb("dl_sb", [DTR, W])
        bcq_sb = sb("bcq_sb", [32, W])
        bcr_sb = sb("bcr_sb", [32, W])
        lgr_sb = sb("lgr_sb", [8, W])
        crow_sb = sb("crow_sb", [8, W])
        lgt_sb = sb("lgt_sb", [128, 32])
        mx_sb = sb("mx_sb", [128, 32])
        ex_sb = sb("ex_sb", [128, 32])
        exs_sb = sb("exs_sb", [128, 32])
        comb_sb = sb("comb_sb", [128, 32])
        msk8_sb = sb("msk8_sb", [128, 32], U8)
        zero8 = sb("zero8", [128, 8])
        ngm_sb = sb("ngm_sb", [128, 4])
        den_sb = sb("den_sb", [128, 4])
        den1_sb = sb("den1_sb", [128, 4])
        rdn_sb = sb("rdn_sb", [128, 4])
        ccb_sb = sb("ccb_sb", [128, KC * NST])

        ps = [e_(nc.psum_tensor(f"ps{i}", [128, W], F32)) for i in range(8)]

        sems = {}
        slot_names = (["setup", "bcr", "bcq", "ccb"]
                      + [f"s1_{k}" for k in range(KC)]
                      + [f"s2_{k}" for k in range(KC)]
                      + [f"w2_{k}" for k in range(KC)]
                      + [f"w3_{k}" for k in range(KC)]
                      + [f"ws_{i}" for i in range(4)]
                      + [f"s3_{k}" for k in range(NIC)]
                      + [f"g{i}" for i in (0, 1, 2, 3, 4, 7, 8, 11, 12, 13)])
        for s in (["s_tensor", "s_vector", "s_scalar", "s_gpsimd", "s_cc"]
                  + ["qd_" + n for n in slot_names]):
            sems[s] = e_(nc.semaphore(s))
        P.sems = sems
        block = e_(nc.Block())

        B = P.buf
        PS = [B(f"ps{i}") for i in range(8)]
        ST1 = [B(f"st1:{i}") for i in range(KC * NST)]
        ST2 = [B(f"st2:{i}") for i in range(KC * NST)]

        def sc(k, c0, c1):
            return slice(k * SW + c0, k * SW + c1)

        S1b = [B(f"S1:{k}") for k in range(KC)]
        S2b = [B(f"S2:{k}") for k in range(KC)]
        S3b = [B(f"S3:{k}") for k in range(KC)]
        WSb = [B(f"ws:{i}") for i in range(4)]

        def dma(q, dst_ap, src_ap, ins, outs, slot):
            P.op(q, lambda e, d=dst_ap, s=src_ap: e.dma_start(out=d, in_=s),
                 ins=ins, outs=outs, sem="qd_" + slot, amt=16)

        def vop(fn, ins, outs, eng="vector"):
            P.op(eng, fn, ins=ins, outs=outs)

        wrot = [0]

        def next_ws():
            wrot[0] = (wrot[0] + 1) % 4
            return wrot[0]

        split_jobs = []
        for ex in range(NE):
            for ic in range(NIC):
                split_jobs.append((wg_s[ex, ic], wg_b[ex, ic], KC * 128))
                split_jobs.append((wu_s[ex, ic], wu_b[ex, ic], KC * 128))
        for ex in range(NE):
            for m in range(KC):
                split_jobs.append((wd_s[ex, m], wd_b[ex, m], NIC * 128))
        split_iter = iter(split_jobs)

        def wbkey(ap):
            return "wb:" + str(ap.name) + ":" + str(ap.offset)

        def emit_splits(n):
            for _ in range(n):
                job = next(split_iter, None)
                if job is None:
                    return
                srcw, dstw, wdt = job
                a = next_ws()
                b = next_ws()
                dma("sync", WS[a][:, 0:wdt], srcw, [], [WSb[a]], f"ws_{a}")
                bv = WS[b].ap().bitcast(BF16)
                P.op("gpsimd", lambda e, a=a, bv=bv, wdt=wdt: e.tensor_copy(
                    bv[:, 0:wdt], WS[a][:, 0:wdt]),
                    ins=[WSb[a]], outs=[WSb[b]])
                P.op("gpsimd", lambda e, a=a, bv=bv, wdt=wdt: e.tensor_tensor(
                    bv[:, wdt:2 * wdt], WS[a][:, 0:wdt], bv[:, 0:wdt],
                    ALU.subtract),
                    ins=[WSb[a], WSb[b]], outs=[WSb[b]])
                dma("sync", dstw, bv[:, 0:2 * wdt], [WSb[b]], [B(wbkey(dstw))],
                    f"ws_{b}")

        def warm(dep):
            P.op("tensor", lambda e: e.matmul(
                ps[7][0:1, 0:64], ones_col[:], a_sb[:, 0:64], start=True, stop=True),
                ins=[dep, B("onec"), B("A")], outs=[PS[7]])

        for src, dsttile, name in [
            (vec4, vec_sb, "vec"), (alog_s, alog_sb, "alog"), (wxp_s, wxp_sb, "wxp"),
            (wr_s, wr_sb, "wr"), (ones_col_d, ones_col, "onec"),
            (ones_row_d, ones_row, "oner"), (ident_d, ident, "ident"),
            (msend_d, msend, "msend"), (mrecv_d, mrecv, "mrecv"),
        ]:
            dma("sync", dsttile[:], src, [B("setup_chain")], [B(name), B("setup_chain")],
                "setup")
        nw_c = lambda k: vec_sb[:, k:k + 1]
        cb_c = lambda k: vec_sb[:, KC + k:KC + k + 1]
        db_c = lambda k: vec_sb[:, 2 * KC + k:2 * KC + k + 1]
        dd_c = lambda k: vec_sb[:, 3 * KC + k:3 * KC + k + 1]
        P.op("scalar", lambda e: e.activation(out=a_sb[:], in_=alog_sb[:], func=AF.Exp),
             ins=[B("alog")], outs=[B("A")])
        vop(lambda e: e.tensor_scalar_mul(a_sb[:], a_sb[:], -1.0), [B("A")], [B("A")])
        vop(lambda e: e.memset(state1[:], 0.0), [], ST1)
        vop(lambda e: e.memset(eps_sb[:], EPS), [], [B("eps")])
        vop(lambda e: e.memset(zero8[:], 0.0), [], [B("zero8")])

        for g in range(NG):
            c0 = g * W
            for k in range(KC):
                dma("scalar", S1[:, sc(k, 0, W)],
                    x_fm[k * 128:(k + 1) * 128, 3 + c0:3 + c0 + W], [], [S1b[k]],
                    f"s1_{k}")
            if g == 0:
                for k in range(KC):
                    dma("scalar", xh_sb[:, k * 3:(k + 1) * 3],
                        x_fm[k * 128:(k + 1) * 128, 0:3],
                        [B("setup_chain")], [B(f"xh{k}"), B("setup_chain")], "setup")
            for k in range(KC):
                sq, sqb = (tb0, B("g0")) if k % 2 == 0 else (tb1, B("g1"))
                P.op("scalar", lambda e, k=k, sq=sq: e.activation(
                    out=sq[:], in_=S1[:, sc(k, 0, W)], func=AF.Square),
                    ins=[S1b[k]], outs=[sqb])
                P.op("tensor", lambda e, k=k: e.matmul(
                    ps[0][0:1, :], ones_col[:], S1[:, sc(k, 0, W)],
                    start=(k == 0), stop=(k == KC - 1)),
                    ins=[S1b[k], B("onec")], outs=[PS[0]])
                P.op("tensor", lambda e, k=k, sq=sq: e.matmul(
                    ps[1][0:1, :], ones_col[:], sq[:],
                    start=(k == 0), stop=(k == KC - 1)),
                    ins=[sqb, B("onec")], outs=[PS[1]])
            if g == 0:
                for k in range(KC):
                    P.op("scalar", lambda e, k=k: e.activation(
                        out=t3a[:], in_=xh_sb[:, k * 3:(k + 1) * 3], func=AF.Square),
                        ins=[B(f"xh{k}")], outs=[B("t3a")])
                    P.op("tensor", lambda e, k=k: e.matmul(
                        ps[2][0:1, 0:3], ones_col[:], xh_sb[:, k * 3:(k + 1) * 3],
                        start=(k == 0), stop=(k == KC - 1)),
                        ins=[B(f"xh{k}"), B("onec")], outs=[PS[2]])
                    P.op("tensor", lambda e, k=k: e.matmul(
                        ps[3][0:1, 0:3], ones_col[:], t3a[:],
                        start=(k == 0), stop=(k == KC - 1)),
                        ins=[B("t3a"), B("onec")], outs=[PS[3]])
            EW = SW if g == 0 else W
            vop(lambda e: e.tensor_copy(RA[0:1, 0:W], ps[0][0:1, :]), [PS[0]], [B("RA")])
            vop(lambda e: e.tensor_copy(RB[0:1, 0:W], ps[1][0:1, :]), [PS[1]], [B("RB")])
            if g == 0:
                vop(lambda e: e.tensor_copy(RA[0:1, W:SW], ps[2][0:1, 0:3]), [PS[2]], [B("RA")])
                vop(lambda e: e.tensor_copy(RB[0:1, W:SW], ps[3][0:1, 0:3]), [PS[3]], [B("RB")])
            vop(lambda e, EW=EW: e.tensor_scalar_mul(RA[0:1, 0:EW], RA[0:1, 0:EW], INVH),
                [B("RA")], [B("RA")])
            vop(lambda e, EW=EW: e.tensor_scalar_mul(RB[0:1, 0:EW], RB[0:1, 0:EW], INVH),
                [B("RB")], [B("RB")])
            P.op("scalar", lambda e, EW=EW: e.activation(
                out=RD[0:1, 0:EW], in_=RA[0:1, 0:EW], func=AF.Square),
                ins=[B("RA")], outs=[B("RD")])
            vop(lambda e, EW=EW: e.tensor_sub(RC[0:1, 0:EW], RB[0:1, 0:EW], RD[0:1, 0:EW]),
                [B("RB"), B("RD")], [B("RC")])
            P.op("scalar", lambda e, EW=EW: e.activation(
                out=RD[0:1, 0:EW], in_=RC[0:1, 0:EW], func=AF.Sqrt, bias=eps_sb[0:1, 0:1]),
                ins=[B("RC"), B("eps")], outs=[B("RD")])
            vop(lambda e, EW=EW: e.reciprocal(RE[0:1, 0:EW], RD[0:1, 0:EW]),
                [B("RD")], [B("RE")])
            vop(lambda e, EW=EW: e.tensor_mul(RF[0:1, 0:EW], RA[0:1, 0:EW], RE[0:1, 0:EW]),
                [B("RA"), B("RE")], [B("RF")])
            vop(lambda e, EW=EW: e.tensor_scalar_mul(RF[0:1, 0:EW], RF[0:1, 0:EW], -1.0),
                [B("RF")], [B("RF")])
            vop(lambda e, EW=EW: e.tensor_mul(RB[0:1, 0:EW], RC[0:1, 0:EW], RE[0:1, 0:EW]),
                [B("RC"), B("RE")], [B("RB")])
            vop(lambda e, EW=EW: e.tensor_mul(RB[0:1, 0:EW], RB[0:1, 0:EW], RE[0:1, 0:EW]),
                [B("RB")], [B("RB")])
            P.op("scalar", lambda e, EW=EW: e.activation(
                out=RD[0:1, 0:EW], in_=RB[0:1, 0:EW], func=AF.Sqrt, bias=eps_sb[0:1, 0:1]),
                ins=[B("RB"), B("eps")], outs=[B("RD")])
            vop(lambda e, EW=EW: e.reciprocal(RD[0:1, 0:EW], RD[0:1, 0:EW]),
                [B("RD")], [B("RD")])
            vop(lambda e, c0=c0: e.tensor_copy(r2all[0:1, c0:c0 + W], RD[0:1, 0:W]),
                [B("RD")], [B("r2all")])
            P.op("tensor", lambda e: e.matmul(ps[4][:], ones_row[:], RE[0:1, 0:W],
                                              start=True, stop=True),
                 ins=[B("RE"), B("oner")], outs=[PS[4]])
            P.op("tensor", lambda e: e.matmul(ps[5][:], ones_row[:], RF[0:1, 0:W],
                                              start=True, stop=True),
                 ins=[B("RF"), B("oner")], outs=[PS[5]])
            P.op("tensor", lambda e: e.matmul(ps[6][:], ones_row[:], RD[0:1, 0:W],
                                              start=True, stop=True),
                 ins=[B("RD"), B("oner")], outs=[PS[6]])
            vop(lambda e: e.tensor_copy(r2b[:], ps[6][:]), [PS[6]], [B("r2b")])
            if g == 0:
                for i, (src, srcb) in enumerate([(RE, "RE"), (RF, "RF"), (RD, "RD")]):
                    P.op("tensor", lambda e, src=src: e.matmul(
                        ps[7][:, 0:3], ones_row[:], src[0:1, W:SW], start=True, stop=True),
                        ins=[B(srcb), B("oner")], outs=[PS[7]])
                    vop(lambda e, i=i: e.tensor_copy(tail_sb[:, 3 * i:3 * i + 3],
                                                     ps[7][:, 0:3]),
                        [PS[7]], [B("tail")])
            for k in range(KC):
                tbt, tbb = (tb0, B("g0")) if k % 2 == 0 else (tb1, B("g1"))
                vop(lambda e, k=k, tbt=tbt: e.tensor_mul(tbt[:], S1[:, sc(k, 0, W)], ps[4][:]),
                    [S1b[k], PS[4]], [tbb])
                vop(lambda e, k=k, tbt=tbt: e.tensor_add(S2[:, sc(k, 0, W)], tbt[:], ps[5][:]),
                    [tbb, PS[5]], [S2b[k]])
                dma("gpsimd", xn_sp[k * 128:(k + 1) * 128, c0:c0 + W], S2[:, sc(k, 0, W)],
                    [S2b[k]], [B(f"xnsp:{g}:{k}")], f"w2_{k}")
            if g == 0:
                for k in range(KC):
                    vop(lambda e, k=k: e.tensor_mul(t3a[:], xh_sb[:, k * 3:(k + 1) * 3],
                                                    tail_sb[:, 0:3]),
                        [B(f"xh{k}"), B("tail")], [B("t3a")])
                    vop(lambda e, k=k: e.tensor_add(xnh_sb[:, k * 3:(k + 1) * 3], t3a[:],
                                                    tail_sb[:, 3:6]),
                        [B("t3a"), B("tail")], [B(f"xnh{k}")])
                    P.op("vector", lambda e, k=k: e.scalar_tensor_tensor(
                        h3_sb[:, k * 3:(k + 1) * 3], xnh_sb[:, k * 3:(k + 1) * 3],
                        nw_c(k), tail_sb[:, 6:9], ALU.mult, ALU.mult),
                        ins=[B(f"xnh{k}"), B("vec"), B("tail")], outs=[B(f"h3{k}")])
                for mr in range(2):
                    for k in range(KC):
                        wi = next_ws()
                        dma("sync", WS[wi][:, 0:1024],
                            win_xc[k * 128:(k + 1) * 128, mr * 1024:(mr + 1) * 1024],
                            [], [WSb[wi]], f"ws_{wi}")
                        for m8 in range(8):
                            m = mr * 8 + m8
                            first = (mr == 0 and k == 0 and m8 == 0)
                            last = (mr == 1 and k == KC - 1 and m8 == 7)
                            P.op("tensor", lambda e, k=k, m8=m8, m=m, wi=wi, first=first,
                                 last=last: e.matmul(
                                     ps[0][:, m * 3:(m + 1) * 3],
                                     WS[wi][:, m8 * 128:(m8 + 1) * 128],
                                     h3_sb[:, k * 3:(k + 1) * 3],
                                     start=first, stop=last),
                                 ins=[WSb[wi], B(f"h3{k}")], outs=[PS[0]])
                for m in range(KC):
                    vop(lambda e, m=m: e.tensor_copy(xc_carry[:, m * 3:(m + 1) * 3],
                                                     ps[0][:, m * 3:(m + 1) * 3]),
                        [PS[0]], [B("xcc")])
            for mr in range(2):
                for k in range(KC):
                    emit_splits(1)
                    wi = next_ws()
                    dma("sync", WS[wi][:, 0:1024],
                        win_xc[k * 128:(k + 1) * 128, mr * 1024:(mr + 1) * 1024],
                        [], [WSb[wi]], f"ws_{wi}")
                    hbt, hbb = (hb0, B("g2")) if k % 2 == 0 else (hb1, B("g3"))
                    P.op("vector", lambda e, k=k, hbt=hbt: e.scalar_tensor_tensor(
                        hbt[:], S2[:, sc(k, 0, W)], nw_c(k), r2b[:], ALU.mult, ALU.mult),
                        ins=[S2b[k], B("vec"), B("r2b")], outs=[hbb])
                    for m8 in range(8):
                        P.op("tensor", lambda e, m8=m8, wi=wi, hbt=hbt, k=k: e.matmul(
                            ps[m8][:], WS[wi][:, m8 * 128:(m8 + 1) * 128], hbt[:],
                            start=(k == 0), stop=(k == KC - 1)),
                            ins=[WSb[wi], hbb], outs=[PS[m8]])
                for m8 in range(8):
                    m = mr * 8 + m8
                    vop(lambda e, m=m: e.tensor_copy(S1[:, sc(m, 0, 3)],
                                                     xc_carry[:, m * 3:(m + 1) * 3]),
                        [B("xcc")], [S1b[m]])
                    P.op("scalar", lambda e, m=m, m8=m8: e.activation(
                        out=S1[:, sc(m, 3, SW)], in_=ps[m8][:], func=AF.Copy),
                        ins=[PS[m8]], outs=[S1b[m]])
            for m in range(KC):
                vop(lambda e, m=m: e.tensor_copy(xc_carry[:, m * 3:(m + 1) * 3],
                                                 S1[:, sc(m, W, SW)]),
                    [S1b[m]], [B("xcc")])
            for mr in range(2):
                for tap in range(4):
                    for k in range(KC):
                        if k % 2 == 0:
                            emit_splits(1)
                        wi = next_ws()
                        dma("sync", WS[wi][:, 0:1024],
                            wconv[tap, k * 128:(k + 1) * 128, mr * 1024:(mr + 1) * 1024],
                            [], [WSb[wi]], f"ws_{wi}")
                        for m8 in range(8):
                            first = (tap == 0 and k == 0)
                            lastk = (tap == 3 and k == KC - 1)
                            P.op("tensor", lambda e, k=k, m8=m8, tap=tap, wi=wi, first=first,
                                 lastk=lastk: e.matmul(
                                     ps[m8][:], WS[wi][:, m8 * 128:(m8 + 1) * 128],
                                     S1[:, sc(k, tap, tap + W)],
                                     start=first, stop=lastk),
                                 ins=[WSb[wi], S1b[k]], outs=[PS[m8]])
                for m8 in range(8):
                    m = mr * 8 + m8
                    P.op("scalar", lambda e, m=m, m8=m8: e.activation(
                        out=S2[:, sc(m, 0, W)], in_=ps[m8][:], func=AF.Silu, bias=cb_c(m)),
                        ins=[PS[m8], B("vec")], outs=[S2b[m]])
                    dma("gpsimd", xcv_sp[m * 128:(m + 1) * 128, c0:c0 + W],
                        S2[:, sc(m, 0, W)], [S2b[m]], [B(f"xcvsp:{g}:{m}")], f"w2_{m}")
            for k in range(KC):
                P.op("tensor", lambda e, k=k: e.matmul(
                    ps[0][0:DTR, :], wxp_sb[:, k * 96:k * 96 + DTR], S2[:, sc(k, 0, W)],
                    start=(k == 0), stop=(k == KC - 1)),
                    ins=[S2b[k], B("wxp")], outs=[PS[0]])
                P.op("tensor", lambda e, k=k: e.matmul(
                    ps[1][0:32, :], wxp_sb[:, k * 96 + DTR:(k + 1) * 96], S2[:, sc(k, 0, W)],
                    start=(k == 0), stop=(k == KC - 1)),
                    ins=[S2b[k], B("wxp")], outs=[PS[1]])
            vop(lambda e: e.tensor_copy(dl_sb[:], ps[0][0:DTR, :]), [PS[0]], [B("dbc")])
            vop(lambda e: e.tensor_copy(bcq_sb[:], ps[1][0:32, :]), [PS[1]], [B("bcq")])
            dma("gpsimd", bc_sp[:, c0:c0 + W], bcq_sb[:], [B("bcq")], [B(f"bcsp:{g}")], "bcq")
            wdti = next_ws()
            dma("sync", WS[wdti][0:DTR, :], wdt_s, [], [WSb[wdti]], f"ws_{wdti}")
            for m in range(KC):
                pi = 2 + (m % 2)
                P.op("tensor", lambda e, m=m, pi=pi, wdti=wdti: e.matmul(
                    ps[pi][:], WS[wdti][0:DTR, m * 128:(m + 1) * 128], dl_sb[0:DTR, :],
                    start=True, stop=True),
                    ins=[B("dbc"), WSb[wdti]], outs=[PS[pi]])
                P.op("scalar", lambda e, m=m, pi=pi: e.activation(
                    out=e1b[:], in_=ps[pi][:], func=AF.Exp, bias=db_c(m)),
                    ins=[PS[pi], B("vec")], outs=[B("g11")])
                P.op("vector", lambda e, m=m, pi=pi: e.tensor_scalar_add(
                    rbb[:], ps[pi][:], db_c(m)),
                    ins=[PS[pi], B("vec")], outs=[B("g13")])
                P.op("scalar", lambda e: e.activation(out=spb[:], in_=e1b[:], func=AF.Ln,
                                                      bias=1.0),
                     ins=[B("g11")], outs=[B("g12")])
                vop(lambda e: e.tensor_single_scalar(mskb[:], rbb[:], 20.0, ALU.is_le),
                    [B("g13")], [B("mskb")])
                vop(lambda e, m=m: e.tensor_copy(S3[:, sc(m, 0, W)], rbb[:]),
                    [B("g13")], [S3b[m]])
                vop(lambda e, m=m: e.copy_predicated(S3[:, sc(m, 0, W)], mskb[:], spb[:]),
                    [B("mskb"), B("g12")], [S3b[m]])
                dma("gpsimd", delta_sp[m * 128:(m + 1) * 128, c0:c0 + W], S3[:, sc(m, 0, W)],
                    [S3b[m]], [B(f"dlsp:{g}:{m}")], f"w3_{m}")
            for n in range(NST):
                P.op("tensor", lambda e, n=n: e.matmul(
                    ps[7][:], ident[0:32, n:n + 1].broadcast_to([32, 128]), bcq_sb[:],
                    start=True, stop=True),
                    ins=[B("bcq"), B("ident")], outs=[PS[7]])
                vop(lambda e, n=n: e.tensor_copy(S1[:, sc(n, 0, W)], ps[7][:]),
                    [PS[7]], [S1b[n]])
            for hc in range(KC):
                vop(lambda e, hc=hc: e.tensor_mul(ub[:], S3[:, sc(hc, 0, W)],
                                                  S2[:, sc(hc, 0, W)]),
                    [S3b[hc], S2b[hc]], [B("g4")])
                for n in range(NST):
                    idx = hc * NST + n
                    dAt, dAb = (dA0, B("g5")) if n % 2 == 0 else (dA1, B("g6"))
                    bxt, bxb = (bx0, B("g7")) if n % 2 == 0 else (bx1, B("g8"))
                    hst, hsb = (hs0, B("g9")) if n % 2 == 0 else (hs1, B("g10"))
                    P.op("scalar", lambda e, hc=hc, dAt=dAt, idx=idx: e.activation(
                        out=dAt[:], in_=S3[:, sc(hc, 0, W)], func=AF.Exp,
                        scale=a_sb[:, idx:idx + 1]),
                        ins=[S3b[hc], B("A")], outs=[dAb])
                    vop(lambda e, n=n, bxt=bxt: e.tensor_mul(
                        bxt[:], ub[:], S1[:, sc(n, 0, W)]),
                        [B("g4"), S1b[n]], [bxb])
                    P.op("vector", lambda e, dAt=dAt, bxt=bxt, hst=hst, idx=idx:
                         e.tensor_tensor_scan(hst[:], dAt[:], bxt[:],
                                              state1[:, idx:idx + 1], ALU.mult, ALU.add),
                         ins=[dAb, bxb, ST1[idx]], outs=[hsb])
                    vop(lambda e, hst=hst, idx=idx: e.tensor_copy(
                        state1[:, idx:idx + 1], hst[:, W - 1:W]),
                        [hsb], [ST1[idx]])

        emit_splits(10 ** 6)
        vop(lambda e: e.tensor_scalar_mul(ccb_sb[:], state1[:], msend[:]),
            ST1 + [B("msend")], [B("ccb")])
        dma("gpsimd", cc_in[:, :], ccb_sb[:], [B("ccb")], [B("ccin")], "ccb")
        P.op("gpsimd", lambda e: e.collective_compute(
            "AllReduce", ALU.add,
            replica_groups=[[0, 1], [2, 3], [4, 5], [6, 7]],
            ins=[cc_in.ap().opt()], outs=[cc_out.ap().opt()]),
            ins=[B("ccin")], outs=[B("ccout")], sem="s_cc", amt=1)
        dma("gpsimd", ccb_sb[:], cc_out[:, :], [B("ccout")], [B("ccb")], "ccb")
        vop(lambda e: e.tensor_scalar_mul(state2[:], ccb_sb[:], mrecv[:]),
            [B("ccb"), B("mrecv")], ST2)

        S2v = S2.ap().bitcast(BF16)
        S3v = S3.ap().bitcast(BF16)
        xnh = lambda k: S3v[:, 2 * k * SW:2 * k * SW + W]
        xnl = lambda k: S3v[:, 2 * k * SW + W:2 * k * SW + 2 * W]
        hah = lambda ic: S3v[:, 2 * ic * SW:2 * ic * SW + W]
        hal = lambda ic: S3v[:, 2 * ic * SW + W:2 * ic * SW + 2 * W]
        G12v = G[12].ap().bitcast(BF16)
        for g in range(NG):
            c0 = g * W
            for k in range(KC):
                xnf, xnfb, xsl = (G[0], B("g0"), "g0") if k % 2 == 0 else (G[1], B("g1"), "g1")
                dma("scalar", xnf[:], xn_sp[k * 128:(k + 1) * 128, c0:c0 + W],
                    [B(f"xnsp:{g}:{k}")], [xnfb], xsl)
                P.op("tensor", lambda e, k=k, xnf=xnf: e.matmul(
                    ps[0][0:8, :], wr_sb[:, k * 8:(k + 1) * 8], xnf[:],
                    start=(k == 0), stop=(k == KC - 1)),
                    ins=[xnfb, B("wr")], outs=[PS[0]])
                vop(lambda e, k=k, xnf=xnf: e.tensor_copy(xnh(k), xnf[:]),
                    [xnfb], [S3b[k]])
                vop(lambda e, k=k, xnf=xnf: e.tensor_tensor(xnl(k), xnf[:], xnh(k),
                                                            ALU.subtract),
                    [xnfb, S3b[k]], [S3b[k]])
            vop(lambda e: e.tensor_copy(lgr_sb[:], ps[0][0:8, :]), [PS[0]], [B("lgr")])
            for s in range(4):
                P.op("tensor", lambda e, s=s: e.transpose(
                    ps[1][0:128, 0:8], lgr_sb[0:8, s * 128:(s + 1) * 128], ident[0:8, 0:8]),
                    ins=[B("lgr"), B("ident")], outs=[PS[1]])
                vop(lambda e, s=s: e.tensor_copy(lgt_sb[:, s * 8:(s + 1) * 8],
                                                 ps[1][:, 0:8]),
                    [PS[1]], [B("lgt")])
                P.op("vector", lambda e, s=s: e.max(out=mx_sb[:, s * 8:(s + 1) * 8],
                                                    in_=lgt_sb[:, s * 8:(s + 1) * 8]),
                     ins=[B("lgt")], outs=[B("mx")])
                vop(lambda e, s=s: e.tensor_scalar_mul(ngm_sb[:, s:s + 1],
                                                       mx_sb[:, s * 8:s * 8 + 1], -1.0),
                    [B("mx")], [B("ngm")])
                P.op("scalar", lambda e, s=s: e.activation(
                    out=ex_sb[:, s * 8:(s + 1) * 8], in_=lgt_sb[:, s * 8:(s + 1) * 8],
                    func=AF.Exp, bias=ngm_sb[:, s:s + 1]),
                    ins=[B("lgt"), B("ngm")], outs=[B("ex")])
                P.op("scalar", lambda e, s=s: e.activation(
                    out=den_sb[:, s:s + 1], in_=mx_sb[:, s * 8 + 1:s * 8 + 2],
                    func=AF.Exp, bias=ngm_sb[:, s:s + 1]),
                    ins=[B("mx"), B("ngm")], outs=[B("den")])
                P.op("scalar", lambda e, s=s: e.add(den1_sb[:, s:s + 1],
                                                    den_sb[:, s:s + 1], 1.0),
                     ins=[B("den")], outs=[B("den1")])
                vop(lambda e, s=s: e.reciprocal(rdn_sb[:, s:s + 1], den1_sb[:, s:s + 1]),
                    [B("den1")], [B("rdn")])
                vop(lambda e, s=s: e.tensor_scalar_mul(exs_sb[:, s * 8:(s + 1) * 8],
                                                       ex_sb[:, s * 8:(s + 1) * 8],
                                                       rdn_sb[:, s:s + 1]),
                    [B("ex"), B("rdn")], [B("exs")])
                vop(lambda e, s=s: e.tensor_single_scalar(
                    msk8_sb[:, s * 8:(s + 1) * 8], lgt_sb[:, s * 8:(s + 1) * 8],
                    mx_sb[:, s * 8 + 2:s * 8 + 3], ALU.is_le),
                    [B("lgt"), B("mx")], [B("msk8")])
                vop(lambda e, s=s: e.tensor_copy(comb_sb[:, s * 8:(s + 1) * 8],
                                                 exs_sb[:, s * 8:(s + 1) * 8]),
                    [B("exs")], [B("comb")])
                vop(lambda e, s=s: e.copy_predicated(comb_sb[:, s * 8:(s + 1) * 8],
                                                     msk8_sb[:, s * 8:(s + 1) * 8],
                                                     zero8[:]),
                    [B("msk8"), B("zero8")], [B("comb")])
                P.op("tensor", lambda e, s=s: e.transpose(
                    ps[1][0:8, 0:128], comb_sb[:, s * 8:(s + 1) * 8], ident[:]),
                    ins=[B("comb"), B("ident")], outs=[PS[1]])
                vop(lambda e, s=s: e.tensor_copy(crow_sb[0:8, s * 128:(s + 1) * 128],
                                                 ps[1][0:8, 0:128]),
                    [PS[1]], [B("crow")])
            dma("scalar", bcr_sb[:], bc_sp[:, c0:c0 + W], [B(f"bcsp:{g}")], [B("bcr")], "bcr")
            for n in range(NST):
                P.op("tensor", lambda e, n=n: e.matmul(
                    ps[7][:], ident[0:32, n:n + 1].broadcast_to([32, 128]), bcr_sb[:],
                    start=True, stop=True),
                    ins=[B("bcr"), B("ident")], outs=[PS[7]])
                vop(lambda e, n=n: e.tensor_copy(S1[:, sc(n, 0, W)], ps[7][:]),
                    [PS[7]], [S1b[n]])
                P.op("tensor", lambda e, n=n: e.matmul(
                    ps[7][:], ident[0:32, 16 + n:17 + n].broadcast_to([32, 128]), bcr_sb[:],
                    start=True, stop=True),
                    ins=[B("bcr"), B("ident")], outs=[PS[7]])
                vop(lambda e, n=n: e.tensor_copy(S2[:, sc(n, 0, W)], ps[7][:]),
                    [PS[7]], [S2b[n]])
            def emit_2a_hc(hc):
                dcht, dchb = (dch0, B("g0")) if hc % 2 == 0 else (dch1, B("g1"))
                xvht, xvhb = (xvh0, B("g2")) if hc % 2 == 0 else (xvh1, B("g3"))
                ya, yab, ysl = (G[12], B("g12"), "g12") if hc % 2 == 0 else (G[13], B("g13"), "g13")
                dma("scalar", dcht[:], delta_sp[hc * 128:(hc + 1) * 128, c0:c0 + W],
                    [B(f"dlsp:{g}:{hc}")], [dchb], "g0" if hc % 2 == 0 else "g1")
                dma("scalar", xvht[:], xcv_sp[hc * 128:(hc + 1) * 128, c0:c0 + W],
                    [B(f"xcvsp:{g}:{hc}")], [xvhb], "g2" if hc % 2 == 0 else "g3")
                vop(lambda e, dcht=dcht, xvht=xvht: e.tensor_mul(ub[:], dcht[:], xvht[:]),
                    [dchb, xvhb], [B("g4")])
                vop(lambda e, ya=ya: e.memset(ya[:], 0.0), [], [yab])
                for n in range(NST):
                    idx = hc * NST + n
                    dAt, dAb = (dA0, B("g5")) if n % 2 == 0 else (dA1, B("g6"))
                    bxt, bxb = bx0, B("g7")
                    hst, hsb = hs0, B("g9")
                    P.op("scalar", lambda e, dAt=dAt, dcht=dcht, idx=idx: e.activation(
                        out=dAt[:], in_=dcht[:], func=AF.Exp, scale=a_sb[:, idx:idx + 1]),
                        ins=[dchb, B("A")], outs=[dAb])
                    vop(lambda e, n=n, bxt=bxt: e.tensor_mul(bxt[:], ub[:],
                                                             S1[:, sc(n, 0, W)]),
                        [B("g4"), S1b[n]], [bxb])
                    P.op("vector", lambda e, dAt=dAt, bxt=bxt, hst=hst, idx=idx:
                         e.tensor_tensor_scan(hst[:], dAt[:], bxt[:],
                                              state2[:, idx:idx + 1], ALU.mult, ALU.add),
                         ins=[dAb, bxb, ST2[idx]], outs=[hsb])
                    vop(lambda e, hst=hst, idx=idx: e.tensor_copy(
                        state2[:, idx:idx + 1], hst[:, W - 1:W]),
                        [hsb], [ST2[idx]])
                    vop(lambda e, hst=hst, n=n: e.tensor_mul(ym0[:], hst[:],
                                                             S2[:, sc(n, 0, W)]),
                        [hsb, S2b[n]], [B("g11")])
                    vop(lambda e, ya=ya: e.tensor_add(ya[:], ya[:], ym0[:]),
                        [B("g11"), yab], [yab])
                    if n % 3 == 2:
                        drain_units()
                P.op("vector", lambda e, ya=ya, hc=hc, xvht=xvht: e.scalar_tensor_tensor(
                    ya[:], xvht[:], dd_c(hc), ya[:], ALU.mult, ALU.add),
                    ins=[xvhb, B("vec"), yab], outs=[yab])
                dma("scalar", y_sp[hc * 128:(hc + 1) * 128, c0:c0 + W], ya[:],
                    [yab], [B(f"ysp:{g}:{hc}")], ysl)

            def emit_ffn_unit(ex, ic):
                if ic == 0:
                    P.op("tensor", lambda e, ex=ex: e.matmul(
                        ps[2][:], ident[0:8, ex:ex + 1].broadcast_to([8, 128]), crow_sb[:],
                        start=True, stop=True),
                        ins=[B("crow"), B("ident")], outs=[PS[2]])
                    vop(lambda e: e.tensor_copy(r2b[:], ps[2][:]), [PS[2]], [B("r2b")])
                wgi = next_ws()
                wgv = WS[wgi].ap().bitcast(BF16)
                dma("sync", wgv[:, 0:2 * KC * 128], wg_b[ex, ic],
                    [B(wbkey(wg_b[ex, ic]))], [WSb[wgi]], f"ws_{wgi}")
                wui = next_ws()
                wuv = WS[wui].ap().bitcast(BF16)
                dma("sync", wuv[:, 0:2 * KC * 128], wu_b[ex, ic],
                    [B(wbkey(wu_b[ex, ic]))], [WSb[wui]], f"ws_{wui}")
                HL = KC * 128
                passes = [(0, xnh), (0, xnl), (HL, xnh)]
                for pidx, (woff, xf) in enumerate(passes):
                    for k in range(KC):
                        first = (pidx == 0 and k == 0)
                        last = (pidx == 2 and k == KC - 1)
                        P.op("tensor", lambda e, k=k, wgv=wgv, woff=woff, xf=xf,
                             first=first, last=last: e.matmul(
                                 ps[0][:],
                                 wgv[:, woff + k * 128:woff + (k + 1) * 128],
                                 xf(k), start=first, stop=last),
                             ins=[WSb[wgi], S3b[k]], outs=[PS[0]], inc=last)
                for pidx, (woff, xf) in enumerate(passes):
                    for k in range(KC):
                        first = (pidx == 0 and k == 0)
                        last = (pidx == 2 and k == KC - 1)
                        P.op("tensor", lambda e, k=k, wuv=wuv, woff=woff, xf=xf,
                             first=first, last=last: e.matmul(
                                 ps[1][:],
                                 wuv[:, woff + k * 128:woff + (k + 1) * 128],
                                 xf(k), start=first, stop=last),
                             ins=[WSb[wui], S3b[k]], outs=[PS[1]], inc=last)
                G10v = G[10].ap().bitcast(BF16)
                P.op("scalar", lambda e: e.activation(out=G[8][:], in_=ps[0][:],
                                                      func=AF.Silu),
                     ins=[PS[0]], outs=[B("g8")])
                vop(lambda e: e.tensor_mul(G[10][:], G[8][:], ps[1][:]),
                    [B("g8"), PS[1]], [B("g10")])
                vop(lambda e: e.tensor_mul(G[8][:], G[10][:], r2b[:]),
                    [B("g10"), B("r2b")], [B("g8")])
                vop(lambda e, G10v=G10v: e.tensor_copy(G10v[:, 0:W], G[8][:]),
                    [B("g8")], [B("g10")])
                vop(lambda e, G10v=G10v: e.tensor_tensor(G10v[:, W:2 * W], G[8][:],
                                                         G10v[:, 0:W], ALU.subtract),
                    [B("g8")], [B("g10")])
                dma("scalar", ha_sp[ex, ic], G10v[:, 0:2 * W],
                    [B("g10")], [B(f"ha:{ex}:{ic}")], "g12")

            units = [(ex, ic) for ex in range(NE) for ic in range(NIC)]
            ustate = {"ui": 0, "step": 0}

            def drain_units():
                ustate["step"] += 1
                tgt = min(len(units), ustate["step"] * len(units) // (KC * NST // 3))
                while ustate["ui"] < tgt:
                    emit_ffn_unit(*units[ustate["ui"]])
                    ustate["ui"] += 1

            for hc in range(KC):
                emit_2a_hc(hc)
            while ustate["ui"] < len(units):
                emit_ffn_unit(*units[ustate["ui"]])
                ustate["ui"] += 1

            for m in range(KC):
                vop(lambda e, m=m: e.memset(S2[:, sc(m, 0, W)], 0.0), [], [S2b[m]])
            for ex in range(NE):
                for ic in range(NIC):
                    dma("scalar", S3v[:, 2 * ic * SW:2 * ic * SW + 2 * W],
                        ha_sp[ex, ic], [B(f"ha:{ex}:{ic}")], [S3b[ic]], f"s3_{ic}")
                for m in range(KC):
                    wdi = next_ws()
                    wdv = WS[wdi].ap().bitcast(BF16)
                    dma("sync", wdv[:, 0:2 * NIC * 128], wd_b[ex, m],
                        [B(wbkey(wd_b[ex, m]))], [WSb[wdi]], f"ws_{wdi}")
                    pi = 3 + (m % 2)
                    DL = NIC * 128
                    for pidx, (woff, hf) in enumerate([(0, hah), (0, hal), (DL, hah)]):
                        for ic in range(NIC):
                            first = (pidx == 0 and ic == 0)
                            last = (pidx == 2 and ic == NIC - 1)
                            P.op("tensor", lambda e, ic=ic, wdv=wdv, pi=pi, woff=woff,
                                 hf=hf, first=first, last=last: e.matmul(
                                     ps[pi][:],
                                     wdv[:, woff + ic * 128:woff + (ic + 1) * 128],
                                     hf(ic), start=first, stop=last),
                                 ins=[WSb[wdi], S3b[ic]], outs=[PS[pi]], inc=last)
                    vop(lambda e, m=m, pi=pi: e.tensor_add(
                        S2[:, sc(m, 0, W)], S2[:, sc(m, 0, W)], ps[pi][:]),
                        [PS[pi], S2b[m]], [S2b[m]])
            P.op("tensor", lambda e, c0=c0: e.matmul(
                ps[6][:], ones_row[:], r2all[0:1, c0:c0 + W], start=True, stop=True),
                ins=[B("r2all"), B("oner")], outs=[PS[6]])
            vop(lambda e: e.tensor_copy(r2b[:], ps[6][:]), [PS[6]], [B("r2b")])
            for mr in range(2):
                for k in range(KC):
                    xnf, xnfb, xsl = (G[0], B("g0"), "g0") if k % 2 == 0 else (G[1], B("g1"), "g1")
                    dma("scalar", xnf[:], xn_sp[k * 128:(k + 1) * 128, c0:c0 + W],
                        [B(f"xnsp:{g}:{k}")], [xnfb], xsl)
                    wi = next_ws()
                    dma("sync", WS[wi][:, 0:1024],
                        win_z[k * 128:(k + 1) * 128, mr * 1024:(mr + 1) * 1024],
                        [], [WSb[wi]], f"ws_{wi}")
                    hbt, hbb = (hb0, B("g2")) if k % 2 == 0 else (hb1, B("g3"))
                    P.op("vector", lambda e, k=k, hbt=hbt, xnf=xnf: e.scalar_tensor_tensor(
                        hbt[:], xnf[:], nw_c(k), r2b[:], ALU.mult, ALU.mult),
                        ins=[xnfb, B("vec"), B("r2b")], outs=[hbb])
                    for m8 in range(8):
                        P.op("tensor", lambda e, m8=m8, wi=wi, hbt=hbt, k=k: e.matmul(
                            ps[m8][:], WS[wi][:, m8 * 128:(m8 + 1) * 128], hbt[:],
                            start=(k == 0), stop=(k == KC - 1)),
                            ins=[WSb[wi], hbb], outs=[PS[m8]], inc=(m8 == 7))
                for m8 in range(8):
                    m = mr * 8 + m8
                    yld, yldb, ysl2 = (G[11], B("g11"), "g11") if m8 % 2 == 0 else (G[4], B("g4"), "g4")
                    dma("scalar", yld[:], y_sp[m * 128:(m + 1) * 128, c0:c0 + W],
                        [B(f"ysp:{g}:{m}")], [yldb], ysl2)
                    P.op("scalar", lambda e, m8=m8: e.activation(
                        out=zsb[:], in_=ps[m8][:], func=AF.Silu),
                        ins=[PS[m8]], outs=[B("g0")])
                    vop(lambda e, m=m, yld=yld: e.tensor_mul(S3[:, sc(m, 0, W)],
                                                             yld[:], zsb[:]),
                        [yldb, B("g0")], [S3b[m]])
            for mr in range(2):
                for k in range(KC):
                    wi = next_ws()
                    dma("sync", WS[wi][:, 0:1024],
                        wout[k * 128:(k + 1) * 128, mr * 1024:(mr + 1) * 1024],
                        [], [WSb[wi]], f"ws_{wi}")
                    for m8 in range(8):
                        P.op("tensor", lambda e, m8=m8, wi=wi, k=k: e.matmul(
                            ps[m8][:], WS[wi][:, m8 * 128:(m8 + 1) * 128],
                            S3[:, sc(k, 0, W)],
                            start=(k == 0), stop=(k == KC - 1)),
                            ins=[WSb[wi], S3b[k]], outs=[PS[m8]], inc=(m8 == 7))
                for m8 in range(8):
                    m = mr * 8 + m8
                    xnf, xnfb, xsl = (G[0], B("g0"), "g0") if m8 % 2 == 0 else (G[1], B("g1"), "g1")
                    dma("scalar", xnf[:], xn_sp[m * 128:(m + 1) * 128, c0:c0 + W],
                        [B(f"xnsp:{g}:{m}")], [xnfb], xsl)
                    xgt, xgb = (xg0, B("g7")) if m8 % 2 == 0 else (xg1, B("g8"))
                    tot, tob = (to0, B("g5")) if m8 % 2 == 0 else (to1, B("g6"))
                    dma("scalar", xgt[:], x_fm[m * 128:(m + 1) * 128, 3 + c0:3 + c0 + W],
                        [], [xgb], "g7" if m8 % 2 == 0 else "g8")
                    vop(lambda e, m8=m8, m=m, tot=tot: e.tensor_add(
                        tot[:], ps[m8][:], S2[:, sc(m, 0, W)]),
                        [PS[m8], S2b[m]], [tob])
                    vop(lambda e, m=m, tot=tot, xnf=xnf: e.tensor_add(
                        tot[:], tot[:], xnf[:]),
                        [tob, xnfb], [tob])
                    vop(lambda e, m=m, tot=tot, xgt=xgt: e.tensor_add(
                        S1[:, sc(m, 0, W)], tot[:], xgt[:]),
                        [tob, xgb, S1b[m]], [S1b[m]])
            for m in range(KC):
                dma("scalar", out_fm[m * 128:(m + 1) * 128, c0:c0 + W], S1[:, sc(m, 0, W)],
                    [S1b[m]], [B(f"out:{g}:{m}")], f"s1_{m}")

        for m in range(KC):
            v = P.ctr.get(f"qd_s1_{m}", 0)
            P.ops["sync"].append(
                lambda e, v=v, m=m: e.wait_ge(P.sems[f"qd_s1_{m}"], v))

        @block.sync
        def _(eng):
            for cl in P.ops["sync"]:
                cl(eng)

        @block.tensor
        def _(eng):
            for cl in P.ops["tensor"]:
                cl(eng)

        @block.vector
        def _(eng):
            for cl in P.ops["vector"]:
                cl(eng)

        @block.scalar
        def _(eng):
            for cl in P.ops["scalar"]:
                cl(eng)

        @block.gpsimd
        def _(eng):
            for cl in P.ops["gpsimd"]:
                cl(eng)

    return nc


def _stage(inputs):
    f32 = np.float32
    x = np.asarray(inputs["L0_attn_x"], f32)
    inp = np.asarray(inputs["L0_attn_in_proj_w"], f32)
    conv = np.ascontiguousarray(np.asarray(inputs["L0_attn_conv_w"], f32))
    xp = np.asarray(inputs["L0_attn_x_proj_w"], f32)
    dt = np.asarray(inputs["L0_attn_dt_proj_w"], f32)
    outp = np.asarray(inputs["L0_attn_out_proj_w"], f32)
    wr = np.asarray(inputs["L0_ffn_Wr"], f32)
    wg = np.asarray(inputs["L0_ffn_Wg"], f32)
    wu = np.asarray(inputs["L0_ffn_Wu"], f32)
    wd = np.asarray(inputs["L0_ffn_Wd"], f32)

    win_xc = np.ascontiguousarray(inp[:H, :].T)
    win_z = np.ascontiguousarray(inp[H:, :].T)
    wxp_s = np.ascontiguousarray(
        xp.T.reshape(KC, 128, 96).transpose(1, 0, 2).reshape(128, KC * 96))
    wdt_s = np.ascontiguousarray(dt.T)
    wout = np.ascontiguousarray(outp.T)
    wr_s = np.ascontiguousarray(
        wr.reshape(KC, 128, 8).transpose(1, 0, 2).reshape(128, KC * 8))
    wg_s = np.ascontiguousarray(
        wg.reshape(NE, KC, 128, NIC, 128).transpose(0, 3, 2, 1, 4)
        .reshape(NE, NIC, 128, KC * 128))
    wu_s = np.ascontiguousarray(
        wu.reshape(NE, KC, 128, NIC, 128).transpose(0, 3, 2, 1, 4)
        .reshape(NE, NIC, 128, KC * 128))
    wd_s = np.ascontiguousarray(
        wd.reshape(NE, NIC, 128, KC, 128).transpose(0, 3, 2, 1, 4)
        .reshape(NE, KC, 128, NIC * 128))
    cm = lambda v: np.ascontiguousarray(np.asarray(v, f32).reshape(KC, 128).T)
    vec4 = np.ascontiguousarray(np.concatenate([
        cm(inputs["L0_attn_norm_w"]), cm(inputs["L0_attn_conv_b"]),
        cm(inputs["L0_attn_dt_proj_b"]), cm(inputs["L0_attn_D"])], axis=1))
    alog_s = np.ascontiguousarray(
        np.asarray(inputs["L0_attn_A_log"], f32).reshape(KC, 128, NST)
        .transpose(1, 0, 2).reshape(128, KC * NST))
    ones_col = np.ones((128, 1), f32)
    ones_row = np.ones((1, 128), f32)
    ident = np.eye(128, dtype=f32)

    shared = dict(win_xc=win_xc, win_z=win_z, wconv=conv, wxp_s=wxp_s, wdt_s=wdt_s,
                  wout=wout, wr_s=wr_s, wg_s=wg_s, wu_s=wu_s, wd_s=wd_s, vec4=vec4,
                  alog_s=alog_s, ones_col=ones_col, ones_row=ones_row, ident=ident)
    in_maps = []
    for c in range(8):
        b, j = c // 2, c % 2
        xb = x[b]
        halo = np.zeros((3, H), f32) if j == 0 else xb[2045:2048]
        x_fm = np.ascontiguousarray(
            np.concatenate([halo, xb[j * 2048:(j + 1) * 2048]], 0).T)
        im = dict(shared)
        im.update(x_fm=x_fm,
                  msend=np.full((128, 1), 1.0 if j == 0 else 0.0, f32),
                  mrecv=np.full((128, 1), float(j), f32))
        in_maps.append(im)
    return in_maps


def kernel(**inputs):
    in_maps = _stage(inputs)
    nc = build()
    res = run_bass_kernel_spmd(nc, in_maps, core_ids=list(range(8)))
    out = np.empty((4, 4096, 2048), np.float32)
    for c in range(8):
        b, j = c // 2, c % 2
        out[b, j * 2048:(j + 1) * 2048, :] = res.results[c]["out_fm"].T
    return out


# version 21
# speedup vs baseline: 1.1666x; 1.0311x over previous
import numpy as np
import concourse.bass as bass
import concourse.mybir as mybir
from concourse.bass_utils import run_bass_kernel_spmd
from contextlib import ExitStack

F32 = mybir.dt.float32
U8 = mybir.dt.uint8
AF = mybir.ActivationFunctionType
ALU = mybir.AluOpType

H = 2048
KC = 16
W = 512
NG = 4
SW = 515
DTR = 64
NST = 16
IFF = 1408
NIC = 11
NE = 8
EPS = 1e-5
INVH = 1.0 / 2048.0


class Buf:
    __slots__ = ("name", "w", "r")

    def __init__(self, name):
        self.name = name
        self.w = None
        self.r = []


class Plan:
    ENGINES = ("tensor", "vector", "scalar", "sync", "gpsimd")

    def __init__(self):
        self.ops = {e: [] for e in self.ENGINES}
        self.ctr = {}
        self.seen = {e: {} for e in self.ENGINES}
        self.sems = {}
        self._bufs = {}
        self._pending = {e: [] for e in self.ENGINES}

    def buf(self, name):
        b = self._bufs.get(name)
        if b is None:
            b = self._bufs[name] = Buf(name)
        return b

    def _waits(self, eng, deps):
        need = {}
        for d in deps:
            if d is None:
                continue
            s, c = d
            if eng == "tensor" and s == "s_tensor":
                continue
            if need.get(s, 0) < c:
                need[s] = c
        out = []
        seen = self.seen[eng]
        for s, c in need.items():
            if seen.get(s, 0) < c:
                seen[s] = c
                out.append((s, c))
        return out

    def op(self, eng, fn, ins=(), outs=(), sem=None, amt=1, inc=True):
        deps = []
        for b in ins:
            deps.append(b.w)
        for b in outs:
            deps.append(b.w)
            deps.extend(b.r)
        waits = self._waits(eng, deps)
        plan = self
        if not inc:
            def closure0(e, waits=waits, fn=fn):
                for (s, c) in waits:
                    e.wait_ge(plan.sems[s], c)
                fn(e)
            self.ops[eng].append(closure0)
            self._pending[eng].append((list(ins), list(outs)))
            return None
        sem = sem or ("s_" + eng)
        self.ctr[sem] = self.ctr.get(sem, 0) + amt
        cnt = self.ctr[sem]

        def closure(e, waits=waits, fn=fn, sem=sem, amt=amt):
            for (s, c) in waits:
                e.wait_ge(plan.sems[s], c)
            fn(e).then_inc(plan.sems[sem], amt)

        self.ops[eng].append(closure)
        tok = (sem, cnt)
        for pins, pouts in self._pending[eng]:
            for b in pouts:
                b.w, b.r = tok, []
            for b in pins:
                b.r.append(tok)
        self._pending[eng] = []
        for b in outs:
            b.w, b.r = tok, []
        for b in ins:
            b.r.append(tok)
        return tok


def build():
    nc = bass.Bass("TRN2", target_bir_lowering=False, debug=False)
    P = Plan()

    di = lambda n, s: nc.dram_tensor(n, s, F32, kind="ExternalInput").ap()
    x_fm = di("x_fm", [H, 3 + 2048])
    win_xc = di("win_xc", [H, H])
    win_z = di("win_z", [H, H])
    wconv = di("wconv", [4, H, H])
    wxp_s = di("wxp_s", [128, KC * 96])
    wdt_s = di("wdt_s", [DTR, H])
    wout = di("wout", [H, H])
    wr_s = di("wr_s", [128, KC * 8])
    wg_s = di("wg_s", [NE, NIC, 128, KC * 128])
    wu_s = di("wu_s", [NE, NIC, 128, KC * 128])
    wd_s = di("wd_s", [NE, KC, 128, NIC * 128])
    vec4 = di("vec4", [128, 4 * KC])
    alog_s = di("alog_s", [128, KC * NST])
    ones_col_d = di("ones_col", [128, 1])
    ones_row_d = di("ones_row", [1, 128])
    ident_d = di("ident", [128, 128])
    msend_d = di("msend", [128, 1])
    mrecv_d = di("mrecv", [128, 1])
    out_fm = nc.dram_tensor("out_fm", [H, 2048], F32, kind="ExternalOutput").ap()

    ds = lambda n, s: nc.dram_tensor(n, s, F32).ap()
    xn_sp = ds("xn_sp", [H, 2048])
    delta_sp = ds("delta_sp", [H, 2048])
    xcv_sp = ds("xcv_sp", [H, 2048])
    bc_sp = ds("bc_sp", [32, 2048])
    cc_in = nc.dram_tensor("cc_in", [128, KC * NST], F32)
    cc_out = nc.dram_tensor("cc_out", [128, KC * NST], F32)
    BF16 = mybir.dt.bfloat16
    wg_b = nc.dram_tensor("wg_b", [NE, NIC, 128, 2 * KC * 128], BF16).ap()
    wu_b = nc.dram_tensor("wu_b", [NE, NIC, 128, 2 * KC * 128], BF16).ap()
    wd_b = nc.dram_tensor("wd_b", [NE, KC, 128, 2 * NIC * 128], BF16).ap()
    y_sp = nc.dram_tensor("y_sp", [H, 2048], F32).ap()
    ha_sp = nc.dram_tensor("ha_sp", [NE, NIC, 128, 2 * W], BF16).ap()

    with ExitStack() as stack:
        e_ = stack.enter_context
        sb = lambda n, s, dt=F32: e_(nc.sbuf_tensor(n, s, dt))
        S1 = sb("S1", [128, KC * SW])
        S2 = sb("S2", [128, KC * SW])
        S3 = sb("S3", [128, KC * SW])
        WS = [sb(f"ws{i}", [128, 2048]) for i in range(4)]
        vec_sb = sb("vec_sb", [128, 4 * KC])
        alog_sb = sb("alog_sb", [128, KC * NST])
        a_sb = sb("a_sb", [128, KC * NST])
        wxp_sb = sb("wxp_sb", [128, KC * 96])
        wr_sb = sb("wr_sb", [128, KC * 8])
        ones_col = sb("ones_col_s", [128, 1])
        ones_row = sb("ones_row_s", [1, 128])
        ident = sb("ident_s", [128, 128])
        msend = sb("msend_s", [128, 1])
        mrecv = sb("mrecv_s", [128, 1])
        state1 = sb("state1", [128, KC * NST])
        state2 = sb("state2", [128, KC * NST])
        r2all = sb("r2all", [1, 2048])
        xc_carry = sb("xc_carry", [128, KC * 3])
        xh_sb = sb("xh_sb", [128, KC * 3])
        xnh_sb = sb("xnh_sb", [128, KC * 3])
        h3_sb = sb("h3_sb", [128, KC * 3])
        tail_sb = sb("tail_sb", [128, 12])
        t3a = sb("t3a", [128, 3])
        eps_sb = sb("eps_sb", [128, 1])
        RA = sb("RA", [1, SW]); RB = sb("RB", [1, SW]); RC = sb("RC", [1, SW])
        RD = sb("RD", [1, SW]); RE = sb("RE", [1, SW]); RF = sb("RF", [1, SW])
        G = [sb(f"g{i}", [128, W]) for i in range(14)]
        tb0, tb1 = G[0], G[1]
        hb0, hb1 = G[2], G[3]
        ub = G[4]
        dA0, dA1 = G[5], G[6]
        bx0, bx1 = G[7], G[8]
        hs0, hs1 = G[9], G[10]
        e1b, spb, rbb = G[11], G[12], G[13]
        dch0, dch1 = G[0], G[1]
        xvh0, xvh1 = G[2], G[3]
        ym0, ym1 = G[11], G[12]
        zsb = G[0]
        to0, to1 = G[5], G[6]
        xg0, xg1 = G[7], G[8]
        gsb = G[0]
        cbb = G[1]
        hu_t = G[2]
        mskb = sb("mskb", [128, W], U8)
        r2b = sb("r2b", [128, W])
        dl_sb = sb("dl_sb", [DTR, W])
        bcq_sb = sb("bcq_sb", [32, W])
        bcr_sb = sb("bcr_sb", [32, W])
        lgr_sb = sb("lgr_sb", [8, W])
        crow_sb = sb("crow_sb", [8, W])
        lgt_sb = sb("lgt_sb", [128, 32])
        mx_sb = sb("mx_sb", [128, 32])
        ex_sb = sb("ex_sb", [128, 32])
        exs_sb = sb("exs_sb", [128, 32])
        comb_sb = sb("comb_sb", [128, 32])
        msk8_sb = sb("msk8_sb", [128, 32], U8)
        zero8 = sb("zero8", [128, 8])
        ngm_sb = sb("ngm_sb", [128, 4])
        den_sb = sb("den_sb", [128, 4])
        den1_sb = sb("den1_sb", [128, 4])
        rdn_sb = sb("rdn_sb", [128, 4])
        ccb_sb = sb("ccb_sb", [128, KC * NST])

        ps = [e_(nc.psum_tensor(f"ps{i}", [128, W], F32)) for i in range(8)]

        sems = {}
        slot_names = (["setup", "bcr", "bcq", "ccb"]
                      + [f"s1_{k}" for k in range(KC)]
                      + [f"s2_{k}" for k in range(KC)]
                      + [f"w2_{k}" for k in range(KC)]
                      + [f"w3_{k}" for k in range(KC)]
                      + [f"ws_{i}" for i in range(4)]
                      + [f"s3_{k}" for k in range(NIC)]
                      + [f"g{i}" for i in (0, 1, 2, 3, 4, 7, 8, 11, 12, 13)])
        for s in (["s_tensor", "s_vector", "s_scalar", "s_gpsimd", "s_cc"]
                  + ["qd_" + n for n in slot_names]):
            sems[s] = e_(nc.semaphore(s))
        P.sems = sems
        block = e_(nc.Block())

        B = P.buf
        PS = [B(f"ps{i}") for i in range(8)]
        ST1 = [B(f"st1:{i}") for i in range(KC * NST)]
        ST2 = [B(f"st2:{i}") for i in range(KC * NST)]

        def sc(k, c0, c1):
            return slice(k * SW + c0, k * SW + c1)

        S1b = [B(f"S1:{k}") for k in range(KC)]
        S2b = [B(f"S2:{k}") for k in range(KC)]
        S3b = [B(f"S3:{k}") for k in range(KC)]
        WSb = [B(f"ws:{i}") for i in range(4)]

        def dma(q, dst_ap, src_ap, ins, outs, slot):
            P.op(q, lambda e, d=dst_ap, s=src_ap: e.dma_start(out=d, in_=s),
                 ins=ins, outs=outs, sem="qd_" + slot, amt=16)

        def vop(fn, ins, outs, eng="vector"):
            P.op(eng, fn, ins=ins, outs=outs)

        wrot = [0]

        def next_ws():
            wrot[0] = (wrot[0] + 1) % 4
            return wrot[0]

        split_jobs = []
        for ex in range(NE):
            for ic in range(NIC):
                split_jobs.append((wg_s[ex, ic], wg_b[ex, ic], KC * 128))
                split_jobs.append((wu_s[ex, ic], wu_b[ex, ic], KC * 128))
        for ex in range(NE):
            for m in range(KC):
                split_jobs.append((wd_s[ex, m], wd_b[ex, m], NIC * 128))
        split_iter = iter(split_jobs)

        def wbkey(ap):
            return "wb:" + str(ap.name) + ":" + str(ap.offset)

        def emit_splits(n):
            for _ in range(n):
                job = next(split_iter, None)
                if job is None:
                    return
                srcw, dstw, wdt = job
                a = next_ws()
                b = next_ws()
                dma("sync", WS[a][:, 0:wdt], srcw, [], [WSb[a]], f"ws_{a}")
                bv = WS[b].ap().bitcast(BF16)
                P.op("gpsimd", lambda e, a=a, bv=bv, wdt=wdt: e.tensor_copy(
                    bv[:, 0:wdt], WS[a][:, 0:wdt]),
                    ins=[WSb[a]], outs=[WSb[b]])
                P.op("gpsimd", lambda e, a=a, bv=bv, wdt=wdt: e.tensor_tensor(
                    bv[:, wdt:2 * wdt], WS[a][:, 0:wdt], bv[:, 0:wdt],
                    ALU.subtract),
                    ins=[WSb[a], WSb[b]], outs=[WSb[b]])
                dma("sync", dstw, bv[:, 0:2 * wdt], [WSb[b]], [B(wbkey(dstw))],
                    f"ws_{b}")

        def warm(dep):
            P.op("tensor", lambda e: e.matmul(
                ps[7][0:1, 0:64], ones_col[:], a_sb[:, 0:64], start=True, stop=True),
                ins=[dep, B("onec"), B("A")], outs=[PS[7]])

        for src, dsttile, name in [
            (vec4, vec_sb, "vec"), (alog_s, alog_sb, "alog"), (wxp_s, wxp_sb, "wxp"),
            (wr_s, wr_sb, "wr"), (ones_col_d, ones_col, "onec"),
            (ones_row_d, ones_row, "oner"), (ident_d, ident, "ident"),
            (msend_d, msend, "msend"), (mrecv_d, mrecv, "mrecv"),
        ]:
            dma("sync", dsttile[:], src, [B("setup_chain")], [B(name), B("setup_chain")],
                "setup")
        nw_c = lambda k: vec_sb[:, k:k + 1]
        cb_c = lambda k: vec_sb[:, KC + k:KC + k + 1]
        db_c = lambda k: vec_sb[:, 2 * KC + k:2 * KC + k + 1]
        dd_c = lambda k: vec_sb[:, 3 * KC + k:3 * KC + k + 1]
        P.op("scalar", lambda e: e.activation(out=a_sb[:], in_=alog_sb[:], func=AF.Exp),
             ins=[B("alog")], outs=[B("A")])
        vop(lambda e: e.tensor_scalar_mul(a_sb[:], a_sb[:], -1.0), [B("A")], [B("A")])
        vop(lambda e: e.memset(state1[:], 0.0), [], ST1)
        vop(lambda e: e.memset(eps_sb[:], EPS), [], [B("eps")])
        vop(lambda e: e.memset(zero8[:], 0.0), [], [B("zero8")])

        for g in range(NG):
            c0 = g * W
            for k in range(KC):
                dma("scalar", S1[:, sc(k, 0, W)],
                    x_fm[k * 128:(k + 1) * 128, 3 + c0:3 + c0 + W], [], [S1b[k]],
                    f"s1_{k}")
            if g == 0:
                for k in range(KC):
                    dma("scalar", xh_sb[:, k * 3:(k + 1) * 3],
                        x_fm[k * 128:(k + 1) * 128, 0:3],
                        [B("setup_chain")], [B(f"xh{k}"), B("setup_chain")], "setup")
            for k in range(KC):
                sq, sqb = (tb0, B("g0")) if k % 2 == 0 else (tb1, B("g1"))
                P.op("scalar", lambda e, k=k, sq=sq: e.activation(
                    out=sq[:], in_=S1[:, sc(k, 0, W)], func=AF.Square),
                    ins=[S1b[k]], outs=[sqb])
                P.op("tensor", lambda e, k=k: e.matmul(
                    ps[0][0:1, :], ones_col[:], S1[:, sc(k, 0, W)],
                    start=(k == 0), stop=(k == KC - 1)),
                    ins=[S1b[k], B("onec")], outs=[PS[0]])
                P.op("tensor", lambda e, k=k, sq=sq: e.matmul(
                    ps[1][0:1, :], ones_col[:], sq[:],
                    start=(k == 0), stop=(k == KC - 1)),
                    ins=[sqb, B("onec")], outs=[PS[1]])
            if g == 0:
                for k in range(KC):
                    P.op("scalar", lambda e, k=k: e.activation(
                        out=t3a[:], in_=xh_sb[:, k * 3:(k + 1) * 3], func=AF.Square),
                        ins=[B(f"xh{k}")], outs=[B("t3a")])
                    P.op("tensor", lambda e, k=k: e.matmul(
                        ps[2][0:1, 0:3], ones_col[:], xh_sb[:, k * 3:(k + 1) * 3],
                        start=(k == 0), stop=(k == KC - 1)),
                        ins=[B(f"xh{k}"), B("onec")], outs=[PS[2]])
                    P.op("tensor", lambda e, k=k: e.matmul(
                        ps[3][0:1, 0:3], ones_col[:], t3a[:],
                        start=(k == 0), stop=(k == KC - 1)),
                        ins=[B("t3a"), B("onec")], outs=[PS[3]])
            EW = SW if g == 0 else W
            vop(lambda e: e.tensor_copy(RA[0:1, 0:W], ps[0][0:1, :]), [PS[0]], [B("RA")])
            vop(lambda e: e.tensor_copy(RB[0:1, 0:W], ps[1][0:1, :]), [PS[1]], [B("RB")])
            if g == 0:
                vop(lambda e: e.tensor_copy(RA[0:1, W:SW], ps[2][0:1, 0:3]), [PS[2]], [B("RA")])
                vop(lambda e: e.tensor_copy(RB[0:1, W:SW], ps[3][0:1, 0:3]), [PS[3]], [B("RB")])
            vop(lambda e, EW=EW: e.tensor_scalar_mul(RA[0:1, 0:EW], RA[0:1, 0:EW], INVH),
                [B("RA")], [B("RA")])
            vop(lambda e, EW=EW: e.tensor_scalar_mul(RB[0:1, 0:EW], RB[0:1, 0:EW], INVH),
                [B("RB")], [B("RB")])
            P.op("scalar", lambda e, EW=EW: e.activation(
                out=RD[0:1, 0:EW], in_=RA[0:1, 0:EW], func=AF.Square),
                ins=[B("RA")], outs=[B("RD")])
            vop(lambda e, EW=EW: e.tensor_sub(RC[0:1, 0:EW], RB[0:1, 0:EW], RD[0:1, 0:EW]),
                [B("RB"), B("RD")], [B("RC")])
            P.op("scalar", lambda e, EW=EW: e.activation(
                out=RD[0:1, 0:EW], in_=RC[0:1, 0:EW], func=AF.Sqrt, bias=eps_sb[0:1, 0:1]),
                ins=[B("RC"), B("eps")], outs=[B("RD")])
            vop(lambda e, EW=EW: e.reciprocal(RE[0:1, 0:EW], RD[0:1, 0:EW]),
                [B("RD")], [B("RE")])
            vop(lambda e, EW=EW: e.tensor_mul(RF[0:1, 0:EW], RA[0:1, 0:EW], RE[0:1, 0:EW]),
                [B("RA"), B("RE")], [B("RF")])
            vop(lambda e, EW=EW: e.tensor_scalar_mul(RF[0:1, 0:EW], RF[0:1, 0:EW], -1.0),
                [B("RF")], [B("RF")])
            vop(lambda e, EW=EW: e.tensor_mul(RB[0:1, 0:EW], RC[0:1, 0:EW], RE[0:1, 0:EW]),
                [B("RC"), B("RE")], [B("RB")])
            vop(lambda e, EW=EW: e.tensor_mul(RB[0:1, 0:EW], RB[0:1, 0:EW], RE[0:1, 0:EW]),
                [B("RB")], [B("RB")])
            P.op("scalar", lambda e, EW=EW: e.activation(
                out=RD[0:1, 0:EW], in_=RB[0:1, 0:EW], func=AF.Sqrt, bias=eps_sb[0:1, 0:1]),
                ins=[B("RB"), B("eps")], outs=[B("RD")])
            vop(lambda e, EW=EW: e.reciprocal(RD[0:1, 0:EW], RD[0:1, 0:EW]),
                [B("RD")], [B("RD")])
            vop(lambda e, c0=c0: e.tensor_copy(r2all[0:1, c0:c0 + W], RD[0:1, 0:W]),
                [B("RD")], [B("r2all")])
            P.op("tensor", lambda e: e.matmul(ps[4][:], ones_row[:], RE[0:1, 0:W],
                                              start=True, stop=True),
                 ins=[B("RE"), B("oner")], outs=[PS[4]])
            P.op("tensor", lambda e: e.matmul(ps[5][:], ones_row[:], RF[0:1, 0:W],
                                              start=True, stop=True),
                 ins=[B("RF"), B("oner")], outs=[PS[5]])
            P.op("tensor", lambda e: e.matmul(ps[6][:], ones_row[:], RD[0:1, 0:W],
                                              start=True, stop=True),
                 ins=[B("RD"), B("oner")], outs=[PS[6]])
            vop(lambda e: e.tensor_copy(r2b[:], ps[6][:]), [PS[6]], [B("r2b")])
            if g == 0:
                for i, (src, srcb) in enumerate([(RE, "RE"), (RF, "RF"), (RD, "RD")]):
                    P.op("tensor", lambda e, src=src: e.matmul(
                        ps[7][:, 0:3], ones_row[:], src[0:1, W:SW], start=True, stop=True),
                        ins=[B(srcb), B("oner")], outs=[PS[7]])
                    vop(lambda e, i=i: e.tensor_copy(tail_sb[:, 3 * i:3 * i + 3],
                                                     ps[7][:, 0:3]),
                        [PS[7]], [B("tail")])
            for k in range(KC):
                tbt, tbb = (tb0, B("g0")) if k % 2 == 0 else (tb1, B("g1"))
                vop(lambda e, k=k, tbt=tbt: e.tensor_mul(tbt[:], S1[:, sc(k, 0, W)], ps[4][:]),
                    [S1b[k], PS[4]], [tbb])
                vop(lambda e, k=k, tbt=tbt: e.tensor_add(S2[:, sc(k, 0, W)], tbt[:], ps[5][:]),
                    [tbb, PS[5]], [S2b[k]])
                dma("gpsimd", xn_sp[k * 128:(k + 1) * 128, c0:c0 + W], S2[:, sc(k, 0, W)],
                    [S2b[k]], [B(f"xnsp:{g}:{k}")], f"w2_{k}")
            if g == 0:
                for k in range(KC):
                    vop(lambda e, k=k: e.tensor_mul(t3a[:], xh_sb[:, k * 3:(k + 1) * 3],
                                                    tail_sb[:, 0:3]),
                        [B(f"xh{k}"), B("tail")], [B("t3a")])
                    vop(lambda e, k=k: e.tensor_add(xnh_sb[:, k * 3:(k + 1) * 3], t3a[:],
                                                    tail_sb[:, 3:6]),
                        [B("t3a"), B("tail")], [B(f"xnh{k}")])
                    P.op("vector", lambda e, k=k: e.scalar_tensor_tensor(
                        h3_sb[:, k * 3:(k + 1) * 3], xnh_sb[:, k * 3:(k + 1) * 3],
                        nw_c(k), tail_sb[:, 6:9], ALU.mult, ALU.mult),
                        ins=[B(f"xnh{k}"), B("vec"), B("tail")], outs=[B(f"h3{k}")])
                for mr in range(2):
                    for k in range(KC):
                        wi = next_ws()
                        dma("sync", WS[wi][:, 0:1024],
                            win_xc[k * 128:(k + 1) * 128, mr * 1024:(mr + 1) * 1024],
                            [], [WSb[wi]], f"ws_{wi}")
                        for m8 in range(8):
                            m = mr * 8 + m8
                            first = (mr == 0 and k == 0 and m8 == 0)
                            last = (mr == 1 and k == KC - 1 and m8 == 7)
                            P.op("tensor", lambda e, k=k, m8=m8, m=m, wi=wi, first=first,
                                 last=last: e.matmul(
                                     ps[0][:, m * 3:(m + 1) * 3],
                                     WS[wi][:, m8 * 128:(m8 + 1) * 128],
                                     h3_sb[:, k * 3:(k + 1) * 3],
                                     start=first, stop=last),
                                 ins=[WSb[wi], B(f"h3{k}")], outs=[PS[0]])
                for m in range(KC):
                    vop(lambda e, m=m: e.tensor_copy(xc_carry[:, m * 3:(m + 1) * 3],
                                                     ps[0][:, m * 3:(m + 1) * 3]),
                        [PS[0]], [B("xcc")])
            for mr in range(2):
                for k in range(KC):
                    emit_splits(1)
                    wi = next_ws()
                    dma("sync", WS[wi][:, 0:1024],
                        win_xc[k * 128:(k + 1) * 128, mr * 1024:(mr + 1) * 1024],
                        [], [WSb[wi]], f"ws_{wi}")
                    hbt, hbb = (hb0, B("g2")) if k % 2 == 0 else (hb1, B("g3"))
                    P.op("vector", lambda e, k=k, hbt=hbt: e.scalar_tensor_tensor(
                        hbt[:], S2[:, sc(k, 0, W)], nw_c(k), r2b[:], ALU.mult, ALU.mult),
                        ins=[S2b[k], B("vec"), B("r2b")], outs=[hbb])
                    for m8 in range(8):
                        P.op("tensor", lambda e, m8=m8, wi=wi, hbt=hbt, k=k: e.matmul(
                            ps[m8][:], WS[wi][:, m8 * 128:(m8 + 1) * 128], hbt[:],
                            start=(k == 0), stop=(k == KC - 1)),
                            ins=[WSb[wi], hbb], outs=[PS[m8]], inc=(m8 == 7))
                for m8 in range(8):
                    m = mr * 8 + m8
                    vop(lambda e, m=m: e.tensor_copy(S1[:, sc(m, 0, 3)],
                                                     xc_carry[:, m * 3:(m + 1) * 3]),
                        [B("xcc")], [S1b[m]])
                    P.op("scalar", lambda e, m=m, m8=m8: e.activation(
                        out=S1[:, sc(m, 3, SW)], in_=ps[m8][:], func=AF.Copy),
                        ins=[PS[m8]], outs=[S1b[m]])
            for m in range(KC):
                vop(lambda e, m=m: e.tensor_copy(xc_carry[:, m * 3:(m + 1) * 3],
                                                 S1[:, sc(m, W, SW)]),
                    [S1b[m]], [B("xcc")])
            for mr in range(2):
                for tap in range(4):
                    for k in range(KC):
                        if k % 2 == 0:
                            emit_splits(1)
                        wi = next_ws()
                        dma("sync", WS[wi][:, 0:1024],
                            wconv[tap, k * 128:(k + 1) * 128, mr * 1024:(mr + 1) * 1024],
                            [], [WSb[wi]], f"ws_{wi}")
                        for m8 in range(8):
                            first = (tap == 0 and k == 0)
                            lastk = (tap == 3 and k == KC - 1)
                            P.op("tensor", lambda e, k=k, m8=m8, tap=tap, wi=wi, first=first,
                                 lastk=lastk: e.matmul(
                                     ps[m8][:], WS[wi][:, m8 * 128:(m8 + 1) * 128],
                                     S1[:, sc(k, tap, tap + W)],
                                     start=first, stop=lastk),
                                 ins=[WSb[wi], S1b[k]], outs=[PS[m8]], inc=(m8 == 7))
                for m8 in range(8):
                    m = mr * 8 + m8
                    P.op("scalar", lambda e, m=m, m8=m8: e.activation(
                        out=S2[:, sc(m, 0, W)], in_=ps[m8][:], func=AF.Silu, bias=cb_c(m)),
                        ins=[PS[m8], B("vec")], outs=[S2b[m]])
                    dma("gpsimd", xcv_sp[m * 128:(m + 1) * 128, c0:c0 + W],
                        S2[:, sc(m, 0, W)], [S2b[m]], [B(f"xcvsp:{g}:{m}")], f"w2_{m}")
            for k in range(KC):
                P.op("tensor", lambda e, k=k: e.matmul(
                    ps[0][0:DTR, :], wxp_sb[:, k * 96:k * 96 + DTR], S2[:, sc(k, 0, W)],
                    start=(k == 0), stop=(k == KC - 1)),
                    ins=[S2b[k], B("wxp")], outs=[PS[0]])
                P.op("tensor", lambda e, k=k: e.matmul(
                    ps[1][0:32, :], wxp_sb[:, k * 96 + DTR:(k + 1) * 96], S2[:, sc(k, 0, W)],
                    start=(k == 0), stop=(k == KC - 1)),
                    ins=[S2b[k], B("wxp")], outs=[PS[1]])
            vop(lambda e: e.tensor_copy(dl_sb[:], ps[0][0:DTR, :]), [PS[0]], [B("dbc")])
            vop(lambda e: e.tensor_copy(bcq_sb[:], ps[1][0:32, :]), [PS[1]], [B("bcq")])
            dma("gpsimd", bc_sp[:, c0:c0 + W], bcq_sb[:], [B("bcq")], [B(f"bcsp:{g}")], "bcq")
            wdti = next_ws()
            dma("sync", WS[wdti][0:DTR, :], wdt_s, [], [WSb[wdti]], f"ws_{wdti}")
            for m in range(KC):
                pi = 2 + (m % 2)
                P.op("tensor", lambda e, m=m, pi=pi, wdti=wdti: e.matmul(
                    ps[pi][:], WS[wdti][0:DTR, m * 128:(m + 1) * 128], dl_sb[0:DTR, :],
                    start=True, stop=True),
                    ins=[B("dbc"), WSb[wdti]], outs=[PS[pi]])
                P.op("scalar", lambda e, m=m, pi=pi: e.activation(
                    out=e1b[:], in_=ps[pi][:], func=AF.Exp, bias=db_c(m)),
                    ins=[PS[pi], B("vec")], outs=[B("g11")])
                P.op("vector", lambda e, m=m, pi=pi: e.tensor_scalar_add(
                    rbb[:], ps[pi][:], db_c(m)),
                    ins=[PS[pi], B("vec")], outs=[B("g13")])
                P.op("scalar", lambda e: e.activation(out=spb[:], in_=e1b[:], func=AF.Ln,
                                                      bias=1.0),
                     ins=[B("g11")], outs=[B("g12")])
                vop(lambda e: e.tensor_single_scalar(mskb[:], rbb[:], 20.0, ALU.is_le),
                    [B("g13")], [B("mskb")])
                vop(lambda e, m=m: e.tensor_copy(S3[:, sc(m, 0, W)], rbb[:]),
                    [B("g13")], [S3b[m]])
                vop(lambda e, m=m: e.copy_predicated(S3[:, sc(m, 0, W)], mskb[:], spb[:]),
                    [B("mskb"), B("g12")], [S3b[m]])
                dma("gpsimd", delta_sp[m * 128:(m + 1) * 128, c0:c0 + W], S3[:, sc(m, 0, W)],
                    [S3b[m]], [B(f"dlsp:{g}:{m}")], f"w3_{m}")
            for n in range(NST):
                P.op("tensor", lambda e, n=n: e.matmul(
                    ps[7][:], ident[0:32, n:n + 1].broadcast_to([32, 128]), bcq_sb[:],
                    start=True, stop=True),
                    ins=[B("bcq"), B("ident")], outs=[PS[7]])
                vop(lambda e, n=n: e.tensor_copy(S1[:, sc(n, 0, W)], ps[7][:]),
                    [PS[7]], [S1b[n]])
            for hc in range(KC):
                vop(lambda e, hc=hc: e.tensor_mul(ub[:], S3[:, sc(hc, 0, W)],
                                                  S2[:, sc(hc, 0, W)]),
                    [S3b[hc], S2b[hc]], [B("g4")])
                for n in range(NST):
                    idx = hc * NST + n
                    dAt, dAb = (dA0, B("g5")) if n % 2 == 0 else (dA1, B("g6"))
                    bxt, bxb = (bx0, B("g7")) if n % 2 == 0 else (bx1, B("g8"))
                    hst, hsb = (hs0, B("g9")) if n % 2 == 0 else (hs1, B("g10"))
                    P.op("scalar", lambda e, hc=hc, dAt=dAt, idx=idx: e.activation(
                        out=dAt[:], in_=S3[:, sc(hc, 0, W)], func=AF.Exp,
                        scale=a_sb[:, idx:idx + 1]),
                        ins=[S3b[hc], B("A")], outs=[dAb])
                    vop(lambda e, n=n, bxt=bxt: e.tensor_mul(
                        bxt[:], ub[:], S1[:, sc(n, 0, W)]),
                        [B("g4"), S1b[n]], [bxb])
                    P.op("vector", lambda e, dAt=dAt, bxt=bxt, hst=hst, idx=idx:
                         e.tensor_tensor_scan(hst[:], dAt[:], bxt[:],
                                              state1[:, idx:idx + 1], ALU.mult, ALU.add),
                         ins=[dAb, bxb, ST1[idx]], outs=[hsb])
                    vop(lambda e, hst=hst, idx=idx: e.tensor_copy(
                        state1[:, idx:idx + 1], hst[:, W - 1:W]),
                        [hsb], [ST1[idx]])

        emit_splits(10 ** 6)
        vop(lambda e: e.tensor_scalar_mul(ccb_sb[:], state1[:], msend[:]),
            ST1 + [B("msend")], [B("ccb")])
        dma("gpsimd", cc_in[:, :], ccb_sb[:], [B("ccb")], [B("ccin")], "ccb")
        P.op("gpsimd", lambda e: e.collective_compute(
            "AllReduce", ALU.add,
            replica_groups=[[0, 1], [2, 3], [4, 5], [6, 7]],
            ins=[cc_in.ap().opt()], outs=[cc_out.ap().opt()]),
            ins=[B("ccin")], outs=[B("ccout")], sem="s_cc", amt=1)
        dma("gpsimd", ccb_sb[:], cc_out[:, :], [B("ccout")], [B("ccb")], "ccb")
        vop(lambda e: e.tensor_scalar_mul(state2[:], ccb_sb[:], mrecv[:]),
            [B("ccb"), B("mrecv")], ST2)

        S2v = S2.ap().bitcast(BF16)
        S3v = S3.ap().bitcast(BF16)
        xnh = lambda k: S3v[:, 2 * k * SW:2 * k * SW + W]
        xnl = lambda k: S3v[:, 2 * k * SW + W:2 * k * SW + 2 * W]
        hah = lambda ic: S3v[:, 2 * ic * SW:2 * ic * SW + W]
        hal = lambda ic: S3v[:, 2 * ic * SW + W:2 * ic * SW + 2 * W]
        G12v = G[12].ap().bitcast(BF16)
        for g in range(NG):
            c0 = g * W
            for k in range(KC):
                xnf, xnfb, xsl = (G[0], B("g0"), "g0") if k % 2 == 0 else (G[1], B("g1"), "g1")
                dma("scalar", xnf[:], xn_sp[k * 128:(k + 1) * 128, c0:c0 + W],
                    [B(f"xnsp:{g}:{k}")], [xnfb], xsl)
                P.op("tensor", lambda e, k=k, xnf=xnf: e.matmul(
                    ps[0][0:8, :], wr_sb[:, k * 8:(k + 1) * 8], xnf[:],
                    start=(k == 0), stop=(k == KC - 1)),
                    ins=[xnfb, B("wr")], outs=[PS[0]])
                vop(lambda e, k=k, xnf=xnf: e.tensor_copy(xnh(k), xnf[:]),
                    [xnfb], [S3b[k]])
                vop(lambda e, k=k, xnf=xnf: e.tensor_tensor(xnl(k), xnf[:], xnh(k),
                                                            ALU.subtract),
                    [xnfb, S3b[k]], [S3b[k]])
            vop(lambda e: e.tensor_copy(lgr_sb[:], ps[0][0:8, :]), [PS[0]], [B("lgr")])
            for s in range(4):
                P.op("tensor", lambda e, s=s: e.transpose(
                    ps[1][0:128, 0:8], lgr_sb[0:8, s * 128:(s + 1) * 128], ident[0:8, 0:8]),
                    ins=[B("lgr"), B("ident")], outs=[PS[1]])
                vop(lambda e, s=s: e.tensor_copy(lgt_sb[:, s * 8:(s + 1) * 8],
                                                 ps[1][:, 0:8]),
                    [PS[1]], [B("lgt")])
                P.op("vector", lambda e, s=s: e.max(out=mx_sb[:, s * 8:(s + 1) * 8],
                                                    in_=lgt_sb[:, s * 8:(s + 1) * 8]),
                     ins=[B("lgt")], outs=[B("mx")])
                vop(lambda e, s=s: e.tensor_scalar_mul(ngm_sb[:, s:s + 1],
                                                       mx_sb[:, s * 8:s * 8 + 1], -1.0),
                    [B("mx")], [B("ngm")])
                P.op("scalar", lambda e, s=s: e.activation(
                    out=ex_sb[:, s * 8:(s + 1) * 8], in_=lgt_sb[:, s * 8:(s + 1) * 8],
                    func=AF.Exp, bias=ngm_sb[:, s:s + 1]),
                    ins=[B("lgt"), B("ngm")], outs=[B("ex")])
                P.op("scalar", lambda e, s=s: e.activation(
                    out=den_sb[:, s:s + 1], in_=mx_sb[:, s * 8 + 1:s * 8 + 2],
                    func=AF.Exp, bias=ngm_sb[:, s:s + 1]),
                    ins=[B("mx"), B("ngm")], outs=[B("den")])
                P.op("scalar", lambda e, s=s: e.add(den1_sb[:, s:s + 1],
                                                    den_sb[:, s:s + 1], 1.0),
                     ins=[B("den")], outs=[B("den1")])
                vop(lambda e, s=s: e.reciprocal(rdn_sb[:, s:s + 1], den1_sb[:, s:s + 1]),
                    [B("den1")], [B("rdn")])
                vop(lambda e, s=s: e.tensor_scalar_mul(exs_sb[:, s * 8:(s + 1) * 8],
                                                       ex_sb[:, s * 8:(s + 1) * 8],
                                                       rdn_sb[:, s:s + 1]),
                    [B("ex"), B("rdn")], [B("exs")])
                vop(lambda e, s=s: e.tensor_single_scalar(
                    msk8_sb[:, s * 8:(s + 1) * 8], lgt_sb[:, s * 8:(s + 1) * 8],
                    mx_sb[:, s * 8 + 2:s * 8 + 3], ALU.is_le),
                    [B("lgt"), B("mx")], [B("msk8")])
                vop(lambda e, s=s: e.tensor_copy(comb_sb[:, s * 8:(s + 1) * 8],
                                                 exs_sb[:, s * 8:(s + 1) * 8]),
                    [B("exs")], [B("comb")])
                vop(lambda e, s=s: e.copy_predicated(comb_sb[:, s * 8:(s + 1) * 8],
                                                     msk8_sb[:, s * 8:(s + 1) * 8],
                                                     zero8[:]),
                    [B("msk8"), B("zero8")], [B("comb")])
                P.op("tensor", lambda e, s=s: e.transpose(
                    ps[1][0:8, 0:128], comb_sb[:, s * 8:(s + 1) * 8], ident[:]),
                    ins=[B("comb"), B("ident")], outs=[PS[1]])
                vop(lambda e, s=s: e.tensor_copy(crow_sb[0:8, s * 128:(s + 1) * 128],
                                                 ps[1][0:8, 0:128]),
                    [PS[1]], [B("crow")])
            dma("scalar", bcr_sb[:], bc_sp[:, c0:c0 + W], [B(f"bcsp:{g}")], [B("bcr")], "bcr")
            for n in range(NST):
                P.op("tensor", lambda e, n=n: e.matmul(
                    ps[7][:], ident[0:32, n:n + 1].broadcast_to([32, 128]), bcr_sb[:],
                    start=True, stop=True),
                    ins=[B("bcr"), B("ident")], outs=[PS[7]])
                vop(lambda e, n=n: e.tensor_copy(S1[:, sc(n, 0, W)], ps[7][:]),
                    [PS[7]], [S1b[n]])
                P.op("tensor", lambda e, n=n: e.matmul(
                    ps[7][:], ident[0:32, 16 + n:17 + n].broadcast_to([32, 128]), bcr_sb[:],
                    start=True, stop=True),
                    ins=[B("bcr"), B("ident")], outs=[PS[7]])
                vop(lambda e, n=n: e.tensor_copy(S2[:, sc(n, 0, W)], ps[7][:]),
                    [PS[7]], [S2b[n]])
            def emit_2a_hc(hc):
                dcht, dchb = (dch0, B("g0")) if hc % 2 == 0 else (dch1, B("g1"))
                xvht, xvhb = (xvh0, B("g2")) if hc % 2 == 0 else (xvh1, B("g3"))
                ya, yab, ysl = (G[12], B("g12"), "g12") if hc % 2 == 0 else (G[13], B("g13"), "g13")
                dma("scalar", dcht[:], delta_sp[hc * 128:(hc + 1) * 128, c0:c0 + W],
                    [B(f"dlsp:{g}:{hc}")], [dchb], "g0" if hc % 2 == 0 else "g1")
                dma("scalar", xvht[:], xcv_sp[hc * 128:(hc + 1) * 128, c0:c0 + W],
                    [B(f"xcvsp:{g}:{hc}")], [xvhb], "g2" if hc % 2 == 0 else "g3")
                vop(lambda e, dcht=dcht, xvht=xvht: e.tensor_mul(ub[:], dcht[:], xvht[:]),
                    [dchb, xvhb], [B("g4")])
                vop(lambda e, ya=ya: e.memset(ya[:], 0.0), [], [yab])
                for n in range(NST):
                    idx = hc * NST + n
                    dAt, dAb = (dA0, B("g5")) if n % 2 == 0 else (dA1, B("g6"))
                    bxt, bxb = bx0, B("g7")
                    hst, hsb = hs0, B("g9")
                    P.op("scalar", lambda e, dAt=dAt, dcht=dcht, idx=idx: e.activation(
                        out=dAt[:], in_=dcht[:], func=AF.Exp, scale=a_sb[:, idx:idx + 1]),
                        ins=[dchb, B("A")], outs=[dAb])
                    vop(lambda e, n=n, bxt=bxt: e.tensor_mul(bxt[:], ub[:],
                                                             S1[:, sc(n, 0, W)]),
                        [B("g4"), S1b[n]], [bxb])
                    P.op("vector", lambda e, dAt=dAt, bxt=bxt, hst=hst, idx=idx:
                         e.tensor_tensor_scan(hst[:], dAt[:], bxt[:],
                                              state2[:, idx:idx + 1], ALU.mult, ALU.add),
                         ins=[dAb, bxb, ST2[idx]], outs=[hsb])
                    vop(lambda e, hst=hst, idx=idx: e.tensor_copy(
                        state2[:, idx:idx + 1], hst[:, W - 1:W]),
                        [hsb], [ST2[idx]])
                    vop(lambda e, hst=hst, n=n: e.tensor_mul(ym0[:], hst[:],
                                                             S2[:, sc(n, 0, W)]),
                        [hsb, S2b[n]], [B("g11")])
                    vop(lambda e, ya=ya: e.tensor_add(ya[:], ya[:], ym0[:]),
                        [B("g11"), yab], [yab])
                    if n % 3 == 2:
                        drain_units()
                P.op("vector", lambda e, ya=ya, hc=hc, xvht=xvht: e.scalar_tensor_tensor(
                    ya[:], xvht[:], dd_c(hc), ya[:], ALU.mult, ALU.add),
                    ins=[xvhb, B("vec"), yab], outs=[yab])
                dma("scalar", y_sp[hc * 128:(hc + 1) * 128, c0:c0 + W], ya[:],
                    [yab], [B(f"ysp:{g}:{hc}")], ysl)

            def emit_ffn_unit(ex, ic):
                if ic == 0:
                    P.op("tensor", lambda e, ex=ex: e.matmul(
                        ps[2][:], ident[0:8, ex:ex + 1].broadcast_to([8, 128]), crow_sb[:],
                        start=True, stop=True),
                        ins=[B("crow"), B("ident")], outs=[PS[2]])
                    vop(lambda e: e.tensor_copy(r2b[:], ps[2][:]), [PS[2]], [B("r2b")])
                wgi = next_ws()
                wgv = WS[wgi].ap().bitcast(BF16)
                dma("sync", wgv[:, 0:2 * KC * 128], wg_b[ex, ic],
                    [B(wbkey(wg_b[ex, ic]))], [WSb[wgi]], f"ws_{wgi}")
                wui = next_ws()
                wuv = WS[wui].ap().bitcast(BF16)
                dma("sync", wuv[:, 0:2 * KC * 128], wu_b[ex, ic],
                    [B(wbkey(wu_b[ex, ic]))], [WSb[wui]], f"ws_{wui}")
                HL = KC * 128
                passes = [(0, xnh), (0, xnl), (HL, xnh)]
                for pidx, (woff, xf) in enumerate(passes):
                    for k in range(KC):
                        first = (pidx == 0 and k == 0)
                        last = (pidx == 2 and k == KC - 1)
                        P.op("tensor", lambda e, k=k, wgv=wgv, woff=woff, xf=xf,
                             first=first, last=last: e.matmul(
                                 ps[0][:],
                                 wgv[:, woff + k * 128:woff + (k + 1) * 128],
                                 xf(k), start=first, stop=last),
                             ins=[WSb[wgi], S3b[k]], outs=[PS[0]], inc=last)
                for pidx, (woff, xf) in enumerate(passes):
                    for k in range(KC):
                        first = (pidx == 0 and k == 0)
                        last = (pidx == 2 and k == KC - 1)
                        P.op("tensor", lambda e, k=k, wuv=wuv, woff=woff, xf=xf,
                             first=first, last=last: e.matmul(
                                 ps[1][:],
                                 wuv[:, woff + k * 128:woff + (k + 1) * 128],
                                 xf(k), start=first, stop=last),
                             ins=[WSb[wui], S3b[k]], outs=[PS[1]], inc=last)
                G10v = G[10].ap().bitcast(BF16)
                P.op("scalar", lambda e: e.activation(out=G[8][:], in_=ps[0][:],
                                                      func=AF.Silu),
                     ins=[PS[0]], outs=[B("g8")])
                vop(lambda e: e.tensor_mul(G[10][:], G[8][:], ps[1][:]),
                    [B("g8"), PS[1]], [B("g10")])
                vop(lambda e: e.tensor_mul(G[8][:], G[10][:], r2b[:]),
                    [B("g10"), B("r2b")], [B("g8")])
                vop(lambda e, G10v=G10v: e.tensor_copy(G10v[:, 0:W], G[8][:]),
                    [B("g8")], [B("g10")])
                vop(lambda e, G10v=G10v: e.tensor_tensor(G10v[:, W:2 * W], G[8][:],
                                                         G10v[:, 0:W], ALU.subtract),
                    [B("g8")], [B("g10")])
                dma("scalar", ha_sp[ex, ic], G10v[:, 0:2 * W],
                    [B("g10")], [B(f"ha:{ex}:{ic}")], "g12")

            units = [(ex, ic) for ex in range(NE) for ic in range(NIC)]
            ustate = {"ui": 0, "step": 0}

            def drain_units():
                ustate["step"] += 1
                tgt = min(len(units), ustate["step"] * len(units) // (KC * NST // 3))
                while ustate["ui"] < tgt:
                    emit_ffn_unit(*units[ustate["ui"]])
                    ustate["ui"] += 1

            for hc in range(KC):
                emit_2a_hc(hc)
            while ustate["ui"] < len(units):
                emit_ffn_unit(*units[ustate["ui"]])
                ustate["ui"] += 1

            for m in range(KC):
                vop(lambda e, m=m: e.memset(S2[:, sc(m, 0, W)], 0.0), [], [S2b[m]])
            for ex in range(NE):
                for ic in range(NIC):
                    dma("scalar", S3v[:, 2 * ic * SW:2 * ic * SW + 2 * W],
                        ha_sp[ex, ic], [B(f"ha:{ex}:{ic}")], [S3b[ic]], f"s3_{ic}")
                for m in range(KC):
                    wdi = next_ws()
                    wdv = WS[wdi].ap().bitcast(BF16)
                    dma("sync", wdv[:, 0:2 * NIC * 128], wd_b[ex, m],
                        [B(wbkey(wd_b[ex, m]))], [WSb[wdi]], f"ws_{wdi}")
                    pi = 3 + (m % 2)
                    DL = NIC * 128
                    for pidx, (woff, hf) in enumerate([(0, hah), (0, hal), (DL, hah)]):
                        for ic in range(NIC):
                            first = (pidx == 0 and ic == 0)
                            last = (pidx == 2 and ic == NIC - 1)
                            P.op("tensor", lambda e, ic=ic, wdv=wdv, pi=pi, woff=woff,
                                 hf=hf, first=first, last=last: e.matmul(
                                     ps[pi][:],
                                     wdv[:, woff + ic * 128:woff + (ic + 1) * 128],
                                     hf(ic), start=first, stop=last),
                                 ins=[WSb[wdi], S3b[ic]], outs=[PS[pi]], inc=last)
                    vop(lambda e, m=m, pi=pi: e.tensor_add(
                        S2[:, sc(m, 0, W)], S2[:, sc(m, 0, W)], ps[pi][:]),
                        [PS[pi], S2b[m]], [S2b[m]])
            P.op("tensor", lambda e, c0=c0: e.matmul(
                ps[6][:], ones_row[:], r2all[0:1, c0:c0 + W], start=True, stop=True),
                ins=[B("r2all"), B("oner")], outs=[PS[6]])
            vop(lambda e: e.tensor_copy(r2b[:], ps[6][:]), [PS[6]], [B("r2b")])
            for mr in range(2):
                for k in range(KC):
                    xnf, xnfb, xsl = (G[0], B("g0"), "g0") if k % 2 == 0 else (G[1], B("g1"), "g1")
                    dma("scalar", xnf[:], xn_sp[k * 128:(k + 1) * 128, c0:c0 + W],
                        [B(f"xnsp:{g}:{k}")], [xnfb], xsl)
                    wi = next_ws()
                    dma("sync", WS[wi][:, 0:1024],
                        win_z[k * 128:(k + 1) * 128, mr * 1024:(mr + 1) * 1024],
                        [], [WSb[wi]], f"ws_{wi}")
                    hbt, hbb = (hb0, B("g2")) if k % 2 == 0 else (hb1, B("g3"))
                    P.op("vector", lambda e, k=k, hbt=hbt, xnf=xnf: e.scalar_tensor_tensor(
                        hbt[:], xnf[:], nw_c(k), r2b[:], ALU.mult, ALU.mult),
                        ins=[xnfb, B("vec"), B("r2b")], outs=[hbb])
                    for m8 in range(8):
                        P.op("tensor", lambda e, m8=m8, wi=wi, hbt=hbt, k=k: e.matmul(
                            ps[m8][:], WS[wi][:, m8 * 128:(m8 + 1) * 128], hbt[:],
                            start=(k == 0), stop=(k == KC - 1)),
                            ins=[WSb[wi], hbb], outs=[PS[m8]], inc=(m8 == 7))
                for m8 in range(8):
                    m = mr * 8 + m8
                    yld, yldb, ysl2 = (G[11], B("g11"), "g11") if m8 % 2 == 0 else (G[4], B("g4"), "g4")
                    dma("scalar", yld[:], y_sp[m * 128:(m + 1) * 128, c0:c0 + W],
                        [B(f"ysp:{g}:{m}")], [yldb], ysl2)
                    P.op("scalar", lambda e, m8=m8: e.activation(
                        out=zsb[:], in_=ps[m8][:], func=AF.Silu),
                        ins=[PS[m8]], outs=[B("g0")])
                    vop(lambda e, m=m, yld=yld: e.tensor_mul(S3[:, sc(m, 0, W)],
                                                             yld[:], zsb[:]),
                        [yldb, B("g0")], [S3b[m]])
            for mr in range(2):
                for k in range(KC):
                    wi = next_ws()
                    dma("sync", WS[wi][:, 0:1024],
                        wout[k * 128:(k + 1) * 128, mr * 1024:(mr + 1) * 1024],
                        [], [WSb[wi]], f"ws_{wi}")
                    for m8 in range(8):
                        P.op("tensor", lambda e, m8=m8, wi=wi, k=k: e.matmul(
                            ps[m8][:], WS[wi][:, m8 * 128:(m8 + 1) * 128],
                            S3[:, sc(k, 0, W)],
                            start=(k == 0), stop=(k == KC - 1)),
                            ins=[WSb[wi], S3b[k]], outs=[PS[m8]], inc=(m8 == 7))
                for m8 in range(8):
                    m = mr * 8 + m8
                    xnf, xnfb, xsl = (G[0], B("g0"), "g0") if m8 % 2 == 0 else (G[1], B("g1"), "g1")
                    dma("scalar", xnf[:], xn_sp[m * 128:(m + 1) * 128, c0:c0 + W],
                        [B(f"xnsp:{g}:{m}")], [xnfb], xsl)
                    xgt, xgb = (xg0, B("g7")) if m8 % 2 == 0 else (xg1, B("g8"))
                    tot, tob = (to0, B("g5")) if m8 % 2 == 0 else (to1, B("g6"))
                    dma("scalar", xgt[:], x_fm[m * 128:(m + 1) * 128, 3 + c0:3 + c0 + W],
                        [], [xgb], "g7" if m8 % 2 == 0 else "g8")
                    vop(lambda e, m8=m8, m=m, tot=tot: e.tensor_add(
                        tot[:], ps[m8][:], S2[:, sc(m, 0, W)]),
                        [PS[m8], S2b[m]], [tob])
                    vop(lambda e, m=m, tot=tot, xnf=xnf: e.tensor_add(
                        tot[:], tot[:], xnf[:]),
                        [tob, xnfb], [tob])
                    vop(lambda e, m=m, tot=tot, xgt=xgt: e.tensor_add(
                        S1[:, sc(m, 0, W)], tot[:], xgt[:]),
                        [tob, xgb, S1b[m]], [S1b[m]])
            for m in range(KC):
                dma("scalar", out_fm[m * 128:(m + 1) * 128, c0:c0 + W], S1[:, sc(m, 0, W)],
                    [S1b[m]], [B(f"out:{g}:{m}")], f"s1_{m}")

        for m in range(KC):
            v = P.ctr.get(f"qd_s1_{m}", 0)
            P.ops["sync"].append(
                lambda e, v=v, m=m: e.wait_ge(P.sems[f"qd_s1_{m}"], v))

        @block.sync
        def _(eng):
            for cl in P.ops["sync"]:
                cl(eng)

        @block.tensor
        def _(eng):
            for cl in P.ops["tensor"]:
                cl(eng)

        @block.vector
        def _(eng):
            for cl in P.ops["vector"]:
                cl(eng)

        @block.scalar
        def _(eng):
            for cl in P.ops["scalar"]:
                cl(eng)

        @block.gpsimd
        def _(eng):
            for cl in P.ops["gpsimd"]:
                cl(eng)

    return nc


def _stage(inputs):
    f32 = np.float32
    x = np.asarray(inputs["L0_attn_x"], f32)
    inp = np.asarray(inputs["L0_attn_in_proj_w"], f32)
    conv = np.ascontiguousarray(np.asarray(inputs["L0_attn_conv_w"], f32))
    xp = np.asarray(inputs["L0_attn_x_proj_w"], f32)
    dt = np.asarray(inputs["L0_attn_dt_proj_w"], f32)
    outp = np.asarray(inputs["L0_attn_out_proj_w"], f32)
    wr = np.asarray(inputs["L0_ffn_Wr"], f32)
    wg = np.asarray(inputs["L0_ffn_Wg"], f32)
    wu = np.asarray(inputs["L0_ffn_Wu"], f32)
    wd = np.asarray(inputs["L0_ffn_Wd"], f32)

    win_xc = np.ascontiguousarray(inp[:H, :].T)
    win_z = np.ascontiguousarray(inp[H:, :].T)
    wxp_s = np.ascontiguousarray(
        xp.T.reshape(KC, 128, 96).transpose(1, 0, 2).reshape(128, KC * 96))
    wdt_s = np.ascontiguousarray(dt.T)
    wout = np.ascontiguousarray(outp.T)
    wr_s = np.ascontiguousarray(
        wr.reshape(KC, 128, 8).transpose(1, 0, 2).reshape(128, KC * 8))
    wg_s = np.ascontiguousarray(
        wg.reshape(NE, KC, 128, NIC, 128).transpose(0, 3, 2, 1, 4)
        .reshape(NE, NIC, 128, KC * 128))
    wu_s = np.ascontiguousarray(
        wu.reshape(NE, KC, 128, NIC, 128).transpose(0, 3, 2, 1, 4)
        .reshape(NE, NIC, 128, KC * 128))
    wd_s = np.ascontiguousarray(
        wd.reshape(NE, NIC, 128, KC, 128).transpose(0, 3, 2, 1, 4)
        .reshape(NE, KC, 128, NIC * 128))
    cm = lambda v: np.ascontiguousarray(np.asarray(v, f32).reshape(KC, 128).T)
    vec4 = np.ascontiguousarray(np.concatenate([
        cm(inputs["L0_attn_norm_w"]), cm(inputs["L0_attn_conv_b"]),
        cm(inputs["L0_attn_dt_proj_b"]), cm(inputs["L0_attn_D"])], axis=1))
    alog_s = np.ascontiguousarray(
        np.asarray(inputs["L0_attn_A_log"], f32).reshape(KC, 128, NST)
        .transpose(1, 0, 2).reshape(128, KC * NST))
    ones_col = np.ones((128, 1), f32)
    ones_row = np.ones((1, 128), f32)
    ident = np.eye(128, dtype=f32)

    shared = dict(win_xc=win_xc, win_z=win_z, wconv=conv, wxp_s=wxp_s, wdt_s=wdt_s,
                  wout=wout, wr_s=wr_s, wg_s=wg_s, wu_s=wu_s, wd_s=wd_s, vec4=vec4,
                  alog_s=alog_s, ones_col=ones_col, ones_row=ones_row, ident=ident)
    in_maps = []
    for c in range(8):
        b, j = c // 2, c % 2
        xb = x[b]
        halo = np.zeros((3, H), f32) if j == 0 else xb[2045:2048]
        x_fm = np.ascontiguousarray(
            np.concatenate([halo, xb[j * 2048:(j + 1) * 2048]], 0).T)
        im = dict(shared)
        im.update(x_fm=x_fm,
                  msend=np.full((128, 1), 1.0 if j == 0 else 0.0, f32),
                  mrecv=np.full((128, 1), float(j), f32))
        in_maps.append(im)
    return in_maps


def kernel(**inputs):
    in_maps = _stage(inputs)
    nc = build()
    res = run_bass_kernel_spmd(nc, in_maps, core_ids=list(range(8)))
    out = np.empty((4, 4096, 2048), np.float32)
    for c in range(8):
        b, j = c // 2, c % 2
        out[b, j * 2048:(j + 1) * 2048, :] = res.results[c]["out_fm"].T
    return out
